# version 01
# baseline (speedup 1.0000x reference)
import contextlib
import numpy as np
import ml_dtypes

import concourse.bass as bass
import concourse.mybir as mybir
from concourse.bass_utils import run_bass_kernel_spmd
from concourse import dve_ops as dvo
from concourse.dve_op import DveOp
from concourse.dve_spec import Spec, Src0, C0, C1, Zero, maxx, minn, lower
from concourse.dve_table_gen import dve_ver_for

F32 = mybir.dt.float32
BF16 = mybir.dt.bfloat16
ALU = mybir.AluOpType
AF = mybir.ActivationFunctionType
AX = mybir.AxisListType
BF = ml_dtypes.bfloat16

MAGIC = float(np.float32(1.5 * 2 ** 23))
DEKC = float(np.float32(2 ** 22 + 1))

NCORES = 8
B = 16
SEQ = 2048
E = 1024
EC = 8
I = 2816
NS = 2
IH = I // NS
ICH = IH // 128
NT = SEQ // 128
NSEQ = B // NCORES
NHEAD = 16
HD = 64
EPS = 1e-5

DEBUG_STAGE = 3


def _refA(in0, in1, s0, s1, imm2):
    f = np.float64 if in0.dtype == np.float64 else np.float32
    r1 = ((in0.astype(f) + np.array(s1, f)) - np.array(s1, f)).astype(f)
    return np.clip(r1, -s0, s0).astype(f)


def _refB(in0, in1, s0, s1, imm2):
    an32 = in0.astype(np.float32)
    w = (an32 * np.float32(s1)).astype(np.float32)
    t = (w - an32).astype(np.float32)
    D = (w - t).astype(in0.dtype if in0.dtype == np.float64 else np.float32)
    return (D - np.clip(D, -s0, s0)).astype(D.dtype)


def register_ops():
    ver = dve_ver_for("TRN2")
    out = []
    _r1 = (Src0 + C1) - C1
    specA = Spec(body=maxx(minn(_r1, C0), Zero - C0))
    _w = Src0 * C1
    _D = _w - (_w - Src0)
    specB = Spec(body=_D - maxx(minn(_D, C0), Zero - C0))
    for name, spec, ref in [("ANT_FP4_A", specA, _refA), ("ANT_FP4_B", specB, _refB)]:
        ex = next((o for o in dvo.OPS if o.name == name), None)
        if ex is not None:
            out.append(ex)
            continue
        compiled = lower(spec, ver=ver, perf_en=False, name=name)
        op = DveOp(name, spec, subdim=False, uops_sha={ver: compiled.sha(ver)},
                   reference=ref)
        row = dvo._CUSTOM_DVE_ROW_BASE + len(dvo.OPS)
        assert row < 0x20
        dvo.OPS.append(op)
        dvo._SUB_OPCODE_FOR_NAME[name] = row
        dvo.CUSTOM_DVE_SPECS[name] = spec
        dvo.CUSTOM_DVE_REFS[name] = ref
        out.append(op)
    return out


OPA, OPB = register_ops()


class Sem:
    def __init__(self, h):
        self.h = h
        self.n = 0

    def inc(self, inst, k=1):
        inst.then_inc(self.h, k)
        self.n += k
        return self.n


class Emitter:
    ENGINES = ("sync", "tensor", "vector", "scalar", "gpsimd")

    def __init__(self, nc):
        self.nc = nc
        self.new_phase()

    def new_phase(self):
        self.ops = {e: [] for e in self.ENGINES}

    def add(self, eng, fn):
        self.ops[eng].append(fn)

    def flush(self):
        ops = self.ops
        with self.nc.Block() as block:
            @block.sync
            def _(eng, ops=ops):
                for f in ops["sync"]:
                    f(eng)

            @block.tensor
            def _(eng, ops=ops):
                for f in ops["tensor"]:
                    f(eng)

            @block.vector
            def _(eng, ops=ops):
                for f in ops["vector"]:
                    f(eng)

            @block.scalar
            def _(eng, ops=ops):
                for f in ops["scalar"]:
                    f(eng)

            @block.gpsimd
            def _(eng, ops=ops):
                for f in ops["gpsimd"]:
                    f(eng)
        self.new_phase()


class Carver:
    def __init__(self, pools):
        self.pools = pools
        self.pos = [0] * len(pools)

    def f32(self, cols):
        for i, (t, n) in enumerate(self.pools):
            if self.pos[i] + cols <= n:
                off = self.pos[i]
                self.pos[i] += cols
                return t[:, off:off + cols]
        raise RuntimeError(f"carver OOM: need {cols}, pos={self.pos}")

    def bf16(self, cols):
        assert cols % 2 == 0
        return self.f32(cols // 2).bitcast(BF16)


def bcast16(ap, nblocks):
    return bass.AP(ap.tensor, ap.offset,
                   [list(ap.ap[0]), [list(ap.ap[1])[0], nblocks], [0, 16]])


class Guards:

    def __init__(self):
        self.g = {}

    def set(self, key, sem, n):
        self.g[key] = (sem, n)

    def snap(self, *keys):
        best = {}
        for k in keys:
            if k in self.g:
                s, n = self.g[k]
                if best.get(id(s), (None, -1))[1] < n:
                    best[id(s)] = (s, n)
        return tuple(best.values())

    @staticmethod
    def apply(eng, snap):
        for s, n in snap:
            if n > 0:
                eng.wait_ge(s.h, n)


def build_program():
    import os as _os
    nc = bass.Bass("TRN2", target_bir_lowering=False,
                   detect_race_conditions=not _os.environ.get("NORACE"))

    x_in = nc.dram_tensor("x", [NSEQ * SEQ, E], F32, kind="ExternalInput").ap()
    y_out = nc.dram_tensor("y", [NSEQ * SEQ, E], F32, kind="ExternalOutput").ap()
    wg_in = nc.dram_tensor("wg", [I, E], F32, kind="ExternalInput").ap()
    wu_in = nc.dram_tensor("wu", [I, E], F32, kind="ExternalInput").ap()
    wd_in = nc.dram_tensor("wd", [E, I], F32, kind="ExternalInput").ap()
    qw_in = nc.dram_tensor("qw", [E, E], F32, kind="ExternalInput").ap()
    kw_in = nc.dram_tensor("kw", [E, E], F32, kind="ExternalInput").ap()
    vw_in = nc.dram_tensor("vw", [E, E], F32, kind="ExternalInput").ap()
    ow_in = nc.dram_tensor("ow", [E, E], F32, kind="ExternalInput").ap()
    qb_in = nc.dram_tensor("qb", [E], F32, kind="ExternalInput").ap()
    kb_in = nc.dram_tensor("kb", [E], F32, kind="ExternalInput").ap()
    vb_in = nc.dram_tensor("vb", [E], F32, kind="ExternalInput").ap()
    ob_in = nc.dram_tensor("ob", [E], F32, kind="ExternalInput").ap()
    ident_in = nc.dram_tensor("ident", [128, 128], BF16, kind="ExternalInput").ap()
    mst_in = nc.dram_tensor("mstart", [128, 1], F32, kind="ExternalInput").ap()
    men_in = nc.dram_tensor("mend", [128, 1], F32, kind="ExternalInput").ap()

    wgT_d = nc.dram_tensor("wgT", [EC * 128, I], BF16, kind="Internal").ap()
    wuT_d = nc.dram_tensor("wuT", [EC * 128, I], BF16, kind="Internal").ap()
    wdT_d = nc.dram_tensor("wdT", [I, E], BF16, kind="Internal").ap()
    qwT_d = nc.dram_tensor("qwT", [EC * 128, E], BF16, kind="Internal").ap()
    kwT_d = nc.dram_tensor("kwT", [EC * 128, E], BF16, kind="Internal").ap()
    vwT_d = nc.dram_tensor("vwT", [EC * 128, E], BF16, kind="Internal").ap()
    owT_d = nc.dram_tensor("owT", [EC * 128, E], BF16, kind="Internal").ap()
    wT_scratch = {id(wg_in): wgT_d, id(wu_in): wuT_d, id(wd_in): wdT_d,
                  id(qw_in): qwT_d, id(kw_in): kwT_d, id(vw_in): vwT_d,
                  id(ow_in): owT_d}

    es = contextlib.ExitStack()

    def sbt(name, shape, dtype):
        return es.enter_context(nc.sbuf_tensor(name, shape, dtype))

    ident = sbt("identt", [128, 128], BF16)
    mst = sbt("mstt", [128, 1], F32)
    men = sbt("ment", [128, 1], F32)
    qbc = sbt("qbc", [128, EC], F32)
    kbc = sbt("kbc", [128, EC], F32)
    vbrep = sbt("vbrep", [128, E], F32)
    obrep = sbt("obrep", [128, E], F32)
    zeroV = sbt("zeroV", [128, E], BF16)
    stats = sbt("stats", [128, 96], F32)
    negmb = sbt("negmb", [128, 48], F32)
    rstdb = sbt("rstdb", [128, 48], F32)
    tinyw = sbt("tinyw", [128, 16], F32)

    X1 = sbt("X1", [128, NT * E], F32)
    XQF = sbt("XQF", [128, NT * E // 2], F32)
    WSB = sbt("WSB", [128, 12288], F32)
    WKA = sbt("WKA", [128, 12288], F32)
    junk = WKA[:, 0:512].bitcast(BF16)
    brow = None

    PS = [es.enter_context(nc.psum_tensor(f"ps{i}", [128, 512], F32))
          for i in range(8)]

    sx = Sem(es.enter_context(nc.semaphore("sx")))
    sw = Sem(es.enter_context(nc.semaphore("sw")))
    sv = Sem(es.enter_context(nc.semaphore("sv")))
    sa = Sem(es.enter_context(nc.semaphore("sa")))
    stt = Sem(es.enter_context(nc.semaphore("stt")))
    sg = Sem(es.enter_context(nc.semaphore("sg")))
    sy = Sem(es.enter_context(nc.semaphore("sy")))

    em = Emitter(nc)
    G = Guards()

    def emit_quant(vec_ops, src, n, scl, q2l, hi, q_out, inc=None):
        nb = n // 16
        bm, rec, bs = scl[:, 0:nb], scl[:, nb:2 * nb], scl[:, 2 * nb:3 * nb]

        def f(v, src=src, nb=nb, bm=bm, rec=rec, bs=bs, q2l=q2l, hi=hi,
              q_out=q_out, inc=inc):
            s3 = src.rearrange("p (s x) -> p s x", x=16)
            v.tensor_reduce(bm, s3, AX.X, ALU.max, apply_absolute_value=True)
            v.drain()
            v.tensor_scalar_max(bm, bm, 6e-12)
            v.drain()
            v.reciprocal(rec, bm)
            v.drain()
            v.tensor_scalar_mul(rec, rec, 12.0)
            v.tensor_scalar_mul(bs, bm, float(np.float32(1.0 / 12.0)))
            v.drain()
            v.tensor_tensor(s3, s3, bcast16(rec, nb), ALU.mult)
            v._custom_dve(OPA, out=q2l, in0=src, s0=4.0, s1=MAGIC)
            v._custom_dve(OPB, out=hi, in0=src, s0=4.0, s1=DEKC)
            v.tensor_add(q2l, q2l, hi)
            last = v.tensor_tensor(q_out.rearrange("p (s x) -> p s x", x=16),
                                   q2l.rearrange("p (s x) -> p s x", x=16),
                                   bcast16(bs, nb), ALU.mult)
            if inc is not None:
                last.then_inc(inc.h, 1)
        vec_ops.append(f)

    def emit_stats_finish(slot, wait_sem, wait_n):
        sc = 32 * slot
        nsl = 16 * slot

        def fin(v):
            v.wait_ge(wait_sem.h, wait_n)
            v.tensor_scalar_mul(negmb[:, nsl:nsl + 16], stats[:, sc:sc + 16], -1.0 / E)
            v.drain()
            v.tensor_mul(tinyw[:, 0:16], negmb[:, nsl:nsl + 16], negmb[:, nsl:nsl + 16])
            v.tensor_scalar(rstdb[:, nsl:nsl + 16], stats[:, sc + 16:sc + 32],
                            1.0 / E, EPS, ALU.mult, ALU.add)
            v.drain()
            v.tensor_sub(rstdb[:, nsl:nsl + 16], rstdb[:, nsl:nsl + 16],
                         tinyw[:, 0:16]).then_inc(sv.h, 1)
        em.add("vector", fin)
        sv.n += 1
        var_done = sv.n

        def sq(a):
            a.wait_ge(sv.h, var_done)
            a.activation(tinyw[:, 0:16], rstdb[:, nsl:nsl + 16], AF.Ln)
            a.drain()
            a.activation(rstdb[:, nsl:nsl + 16], tinyw[:, 0:16], AF.Exp,
                         scale=-0.5).then_inc(sa.h, 1)
        em.add("scalar", sq)
        sa.n += 1
        return sa.n

    def emit_preproc():
        C = Carver([(WKA, 12288), (WSB, 12288)])
        browt = C.f32(E)
        brow = browt

        import os
        _dbg_skip = set(os.environ.get("DBGSKIP", "").split(","))
        _loadlist = []
        if "ident" not in _dbg_skip:
            _loadlist.append(lambda s: s.dma_start(out=ident[:, :], in_=ident_in))
        if "mst" not in _dbg_skip:
            _loadlist.append(lambda s: s.dma_start(out=mst[:, :], in_=mst_in))
            _loadlist.append(lambda s: s.dma_start(out=men[:, :], in_=men_in))
        if "qbc" not in _dbg_skip:
            _loadlist.append(lambda s: s.dma_start(out=qbc[:, :], in_=qb_in.rearrange("(c p) -> p c", p=128), allow_slow_non_contiguous=True))
            _loadlist.append(lambda s: s.dma_start(out=kbc[:, :], in_=kb_in.rearrange("(c p) -> p c", p=128), allow_slow_non_contiguous=True))
        if "rep" not in _dbg_skip:
            _loadlist.append(lambda s: s.dma_start(out=vbrep[:, :], in_=bass.AP(vb_in.tensor, 0, [[0, 128], [1, E]])))
            _loadlist.append(lambda s: s.dma_start(out=obrep[:, :], in_=bass.AP(ob_in.tensor, 0, [[0, 128], [1, E]])))

        def loads(s):
            for f in _loadlist:
                f(s).then_inc(sx.h, 16)
        em.add("sync", loads)
        sx.n += len(_loadlist) * 16

        def zv(v):
            v.memset(zeroV[:, :], 0.0).then_inc(sv.h, 1)
        em.add("vector", zv)
        sv.n += 1

        wtmp = [C.f32(I), C.f32(I)]
        wq = [C.bf16(I), C.bf16(I)]
        wscl = C.f32(3 * (I // 16) + 16)
        wq2l = C.bf16(I)
        whi = C.bf16(I)
        stg = [C.bf16(512), C.bf16(512)]

        jobs = [(wg_in, I, E, True), (wu_in, I, E, True), (wd_in, E, I, True),
                (qw_in, E, E, False), (kw_in, E, E, False),
                (vw_in, E, E, False), (ow_in, E, E, False)]
        chunks = []
        for (din, rows, cols, doq) in jobs:
            for r in range(rows // 128):
                chunks.append((din, cols, doq, r))
        import os
        _lim = int(os.environ.get("DBG_NCHUNKS", "999"))
        chunks = chunks[:_lim]

        for idx, (din, cols, doq, r) in enumerate(chunks):
            rot = idx % 2
            wt = wtmp[rot][:, 0:cols]
            wqb = wq[rot][:, 0:cols]
            dout = wT_scratch[id(din)]

            gw = G.snap(("pp_wt", rot))

            def ld(s, din=din, cols=cols, r=r, wt=wt, gw=gw):
                G.apply(s, gw)
                s.dma_start(out=wt, in_=din[128 * r:128 * (r + 1), 0:cols]).then_inc(sx.h, 16)
            em.add("sync", ld)
            sx.n += 16
            x_after = sx.n

            vec_ops = []

            gw2 = G.snap(("pp_wq", rot))

            def wv(v, x_after=x_after, gw2=gw2):
                v.wait_ge(sx.h, x_after)
                G.apply(v, gw2)
            vec_ops.append(wv)
            if doq:
                emit_quant(vec_ops, wt, cols, wscl, wq2l[:, 0:cols], whi[:, 0:cols],
                           wqb, inc=sv)
            else:
                vec_ops.append(lambda v, wt=wt, wqb=wqb:
                               v.tensor_copy(wqb, wt).then_inc(sv.h, 1))
            for f in vec_ops:
                em.add("vector", f)
            sv.n += 1
            q_done = sv.n
            G.set(("pp_wt", rot), sv, q_done)

            ncc = cols // 128
            for cc in range(ncc):
                slot = cc % 4
                bank = PS[4 + slot]
                bslot = bank[:, :].bitcast(BF16)[:, 0:128]

                gw3 = G.snap(("pp_ps", slot))

                def tr(t, wqb=wqb, cc=cc, bslot=bslot, q_done=q_done, gw3=gw3):
                    t.wait_ge(sv.h, q_done)
                    G.apply(t, gw3)
                    t.transpose(bslot, wqb[:, 128 * cc:128 * (cc + 1)],
                                ident[:, :]).then_inc(stt.h, 1)
                em.add("tensor", tr)
                stt.n += 1
                trd = stt.n

                srot = (cc // 4) % 2
                sslot = stg[srot][:, 128 * (cc % 4):128 * (cc % 4) + 128]

                gw4 = G.snap(("pp_stg", srot, cc % 4))

                def ev(a, bslot=bslot, sslot=sslot, trd=trd, gw4=gw4):
                    a.wait_ge(stt.h, trd)
                    G.apply(a, gw4)
                    a.copy(sslot, bslot).then_inc(sa.h, 1)
                em.add("scalar", ev)
                sa.n += 1
                G.set(("pp_ps", slot), sa, sa.n)
                ev_done = sa.n

                def out(g, dout=dout, cc=cc, r=r, sslot=sslot, ev_done=ev_done):
                    g.wait_ge(sa.h, ev_done)
                    g.dma_start(out=dout[128 * cc:128 * (cc + 1), 128 * r:128 * (r + 1)],
                                in_=sslot).then_inc(sg.h, 16)
                em.add("gpsimd", out)
                sg.n += 16
                G.set(("pp_stg", srot, cc % 4), sg, sg.n)
            G.set(("pp_wq", rot), stt, stt.n)

        def fin(g):
            g.wait_ge(sg.h, sg.n)
        em.add("gpsimd", fin)
        em.flush()

    def emit_pre(seq):
        x3d = x_in.rearrange("(a p) e -> a p e", p=128)

        def ld(s, seq=seq):
            for t in range(NT):
                s.dma_start(out=X1[:, E * t:E * (t + 1)],
                            in_=x3d[seq * NT + t, :, :]).then_inc(sx.h, 16)
        em.add("sync", ld)
        sx.n += NT * 16
        x_done = sx.n

        def acts(a, x_done=x_done):
            for t in range(NT):
                a.wait_ge(sx.h, x_done - (NT - 1 - t) * 16)
                a.activation(junk[:, :], X1[:, E * t:E * (t + 1)], AF.Identity,
                             accum_out=stats[:, t:t + 1])
                a.activation(junk[:, :], X1[:, E * t:E * (t + 1)], AF.Square,
                             accum_out=stats[:, 16 + t:17 + t])
            a.drain()
            a.activation(junk[:, 0:1], stats[:, 0:1], AF.Identity).then_inc(sa.h, 1)
        em.add("scalar", acts)
        sa.n += 1
        emit_stats_finish(0, sa, sa.n)
        em.flush()

    xqf_ready = {}

    def emit_ffn(seq, sl, ln_slot, write_y):
        C = Carver([(WKA, 12288)])
        lnt = C.f32(E)
        xscl = C.f32(3 * (E // 16))
        q2l = C.bf16(E)
        xqtm = C.bf16(E)
        gsil = C.f32(IH)
        hfull = C.f32(IH)
        hscl = C.f32(3 * (IH // 16))
        q2lh = C.bf16(IH)
        hih = C.bf16(IH)
        hqf = [C.bf16(ICH * 128) for _ in range(4)]
        wdb = [C.bf16(E), C.bf16(E)]

        WG = WSB[:, 0:EC * IH // 2].bitcast(BF16)
        WU = WSB[:, EC * IH // 2:EC * IH].bitcast(BF16)
        nm = negmb[:, 16 * ln_slot:16 * ln_slot + 16]
        rs = rstdb[:, 16 * ln_slot:16 * ln_slot + 16]

        def wld(s, sl=sl):
            for k in range(EC):
                s.dma_start(out=WG[:, IH * k:IH * (k + 1)],
                            in_=wgT_d[128 * k:128 * (k + 1), IH * sl:IH * (sl + 1)]
                            ).then_inc(sw.h, 16)
                s.dma_start(out=WU[:, IH * k:IH * (k + 1)],
                            in_=wuT_d[128 * k:128 * (k + 1), IH * sl:IH * (sl + 1)]
                            ).then_inc(sw.h, 16)
        em.add("sync", wld)
        sw.n += EC * 2 * 16
        w_done = sw.n

        nchunks = [(0, 512), (512, 512), (1024, 384)]
        hq_ready = {}
        x1_done = {}

        for g in range(4):
            tiles = [4 * g + j for j in range(4)]
            for tt_i, t in enumerate(tiles):
                xt = X1[:, E * t:E * (t + 1)]
                xqf_t = XQF[:, (E // 2) * t:(E // 2) * (t + 1)].bitcast(BF16)

                if sl == 0:
                    vec_ops = []

                    gwl = G.snap("lnt")

                    def lnq(v, xt=xt, t=t, gwl=gwl):
                        G.apply(v, gwl)
                        v.tensor_scalar(lnt, xt, nm[:, t:t + 1], rs[:, t:t + 1],
                                        ALU.add, ALU.mult)
                    vec_ops.append(lnq)
                    emit_quant(vec_ops, lnt, E, xscl, q2l, xqtm, xqtm, inc=sv)
                    for f in vec_ops:
                        em.add("vector", f)
                    sv.n += 1
                    xq_done = sv.n
                    G.set("lnt", sv, xq_done)

                    for cc in range(EC):
                        slot = cc % 4
                        bank = PS[4 + slot]
                        bslot = bank[:, :].bitcast(BF16)[:, 0:128]

                        gwt = G.snap(("f_ps", slot), "banks_all")

                        def tr(te, bslot=bslot, cc=cc, xq_done=xq_done, gwt=gwt):
                            te.wait_ge(sv.h, xq_done)
                            G.apply(te, gwt)
                            te.transpose(bslot, xqtm[:, 128 * cc:128 * (cc + 1)],
                                         ident[:, :]).then_inc(stt.h, 1)
                        em.add("tensor", tr)
                        stt.n += 1
                        trd = stt.n

                        def ev(a, bslot=bslot, dst=xqf_t[:, 128 * cc:128 * (cc + 1)], trd=trd):
                            a.wait_ge(stt.h, trd)
                            a.copy(dst, bslot).then_inc(sa.h, 1)
                        em.add("scalar", ev)
                        sa.n += 1
                        G.set(("f_ps", slot), sa, sa.n)
                    xqf_ready[t] = (sa, sa.n)

                xr_sem, xr_n = xqf_ready[t]

                for ci, (coff, clen) in enumerate(nchunks):
                    gb = PS[ci % 2]
                    ub = PS[2 + ci % 2]

                    gwg = G.snap(("gu_g", ci % 2), ("gu_u", ci % 2), "banks_all")

                    def gu(te, coff=coff, clen=clen, gb=gb, ub=ub, xqf_t=xqf_t,
                           xr_sem=xr_sem, xr_n=xr_n, gwg=gwg):
                        te.wait_ge(sw.h, w_done)
                        te.wait_ge(xr_sem.h, xr_n)
                        G.apply(te, gwg)
                        for k in range(EC):
                            te.matmul(gb[:, 0:clen], xqf_t[:, 128 * k:128 * (k + 1)],
                                      WG[:, IH * k + coff:IH * k + coff + clen],
                                      start=(k == 0), stop=(k == EC - 1))
                        for k in range(EC):
                            last = te.matmul(ub[:, 0:clen], xqf_t[:, 128 * k:128 * (k + 1)],
                                             WU[:, IH * k + coff:IH * k + coff + clen],
                                             start=(k == 0), stop=(k == EC - 1))
                        last.then_inc(stt.h, 1)
                    em.add("tensor", gu)
                    stt.n += 1
                    gu_done = stt.n

                    def sil(a, gb=gb, coff=coff, clen=clen, gu_done=gu_done):
                        a.wait_ge(stt.h, gu_done)
                        a.activation(gsil[:, coff:coff + clen], gb[:, 0:clen],
                                     AF.Silu).then_inc(sa.h, 1)
                    em.add("scalar", sil)
                    sa.n += 1
                    G.set(("gu_g", ci % 2), sa, sa.n)
                    sil_done = sa.n

                    def hm(v, ub=ub, coff=coff, clen=clen, sil_done=sil_done):
                        v.wait_ge(sa.h, sil_done)
                        v.tensor_mul(hfull[:, coff:coff + clen],
                                     gsil[:, coff:coff + clen],
                                     ub[:, 0:clen]).then_inc(sv.h, 1)
                    em.add("vector", hm)
                    sv.n += 1
                    G.set(("gu_u", ci % 2), sv, sv.n)

                vec_ops = []
                gwh = G.snap(("hq_b", 0), ("hq_b", 1))
                vec_ops.append(lambda v, gwh=gwh: G.apply(v, gwh))
                emit_quant(vec_ops, hfull, IH, hscl, q2lh, hih, hih, inc=sv)
                for f in vec_ops:
                    em.add("vector", f)
                sv.n += 1
                hq_done = sv.n
                G.set(("hq_b", 0), sv, hq_done)
                G.set(("hq_b", 1), sv, hq_done)

                for cc in range(ICH):
                    slot = cc % 4
                    bank = PS[4 + slot]
                    bslot = bank[:, :].bitcast(BF16)[:, 0:128]

                    gwt2 = G.snap(("f_ps", slot), "banks_all")

                    def tr(te, bslot=bslot, cc=cc, hq_done=hq_done, gwt2=gwt2):
                        te.wait_ge(sv.h, hq_done)
                        G.apply(te, gwt2)
                        te.transpose(bslot, hih[:, 128 * cc:128 * (cc + 1)],
                                     ident[:, :]).then_inc(stt.h, 1)
                    em.add("tensor", tr)
                    stt.n += 1
                    trd = stt.n

                    def ev(a, bslot=bslot, dst=hqf[tt_i][:, 128 * cc:128 * (cc + 1)], trd=trd):
                        a.wait_ge(stt.h, trd)
                        a.copy(dst, bslot).then_inc(sa.h, 1)
                    em.add("scalar", ev)
                    sa.n += 1
                    G.set(("f_ps", slot), sa, sa.n)
                hq_ready[tt_i] = (sa, sa.n)

            hq_sem, hq_n = hq_ready[3]
            for c in range(ICH):
                rot = c % 2

                gww = G.snap(("wd_b", rot))

                def wdl(s, c=c, rot=rot, sl=sl, gww=gww):
                    G.apply(s, gww)
                    s.dma_start(out=wdb[rot],
                                in_=wdT_d[128 * (ICH * sl + c):128 * (ICH * sl + c + 1), :]
                                ).then_inc(sw.h, 16)
                em.add("sync", wdl)
                sw.n += 16
                wd_ld = sw.n

                gwd = G.snap("banks_dn", ("f_ps", 0), ("f_ps", 1), ("f_ps", 2),
                             ("f_ps", 3), ("f_ps", 4), ("f_ps", 5), ("f_ps", 6),
                             ("f_ps", 7), ("gu_g", 0), ("gu_g", 1),
                             ("gu_u", 0), ("gu_u", 1)) if c == 0 else ()

                def dn(te, c=c, rot=rot, hq_n=hq_n, wd_ld=wd_ld, gwd=gwd):
                    te.wait_ge(hq_sem.h, hq_n)
                    te.wait_ge(sw.h, wd_ld)
                    G.apply(te, gwd)
                    for tt_i in range(4):
                        for half in range(2):
                            last = te.matmul(PS[2 * tt_i + half][:, :],
                                             hqf[tt_i][:, 128 * c:128 * (c + 1)],
                                             wdb[rot][:, 512 * half:512 * (half + 1)],
                                             start=(c == 0), stop=(c == ICH - 1))
                    last.then_inc(stt.h, 1)
                em.add("tensor", dn)
                stt.n += 1
                G.set(("wd_b", rot), stt, stt.n)
            dn_done = stt.n

            for tt_i, t in enumerate(tiles):
                def evd(v, tt_i=tt_i, t=t, dn_done=dn_done):
                    v.wait_ge(stt.h, dn_done)
                    for half in range(2):
                        xs = X1[:, E * t + 512 * half:E * t + 512 * (half + 1)]
                        last = v.scalar_tensor_tensor(xs, PS[2 * tt_i + half][:, :],
                                                      0.5, xs, ALU.mult, ALU.add)
                    last.then_inc(sv.h, 1)
                em.add("vector", evd)
                sv.n += 1
                x1_done[t] = sv.n
            G.set("banks_dn", sv, sv.n)
            G.set("banks_all", sv, sv.n)

            if write_y:
                y3d = y_out.rearrange("(a p) e -> a p e", p=128)
                for t in tiles:
                    def sty(gps, t=t, seq=seq, n=x1_done[t]):
                        gps.wait_ge(sv.h, n)
                        gps.dma_start(out=y3d[seq * NT + t, :, :],
                                      in_=X1[:, E * t:E * (t + 1)]).then_inc(sy.h, 16)
                    em.add("gpsimd", sty)
                    sy.n += 16

        if sl == 1 and not write_y:
            def acts(a, x1_done=dict(x1_done)):
                for t in range(NT):
                    a.wait_ge(sv.h, x1_done[t])
                    a.activation(junk[:, :], X1[:, E * t:E * (t + 1)], AF.Identity,
                                 accum_out=stats[:, 32 + t:33 + t])
                    a.activation(junk[:, :], X1[:, E * t:E * (t + 1)], AF.Square,
                                 accum_out=stats[:, 48 + t:49 + t])
                a.drain()
                a.activation(junk[:, 0:1], stats[:, 32:33], AF.Identity).then_inc(sa.h, 1)
            em.add("scalar", acts)
            sa.n += 1
            emit_stats_finish(1, sa, sa.n)

        def wv(v):
            v.wait_ge(sv.h, sv.n)
        em.add("vector", wv)
        if write_y:
            def wgf(g):
                g.wait_ge(sy.h, sy.n)
            em.add("gpsimd", wgf)
        em.flush()


    def emit_attn(seq):
        C = Carver([(WKA, 12288), (XQF, NT * E // 2)])
        ln2t = C.bf16(E)
        ln2f = C.bf16(EC * 128)
        Qr = [C.bf16(EC * 128), C.bf16(EC * 128)]
        KRS, VRS = 3, 4
        Kr = [C.bf16(EC * 192) for _ in range(KRS)]
        Vr = [C.bf16(E) for _ in range(VRS)]
        smask = C.f32(2 * 192)
        pbuf = C.bf16(8 * 384)
        ptr = C.bf16(2 * 384)
        attb = [C.bf16(EC * 128), C.bf16(EC * 128)]
        lsum = C.f32(16)
        rsum = C.f32(16)
        cjunk = C.bf16(E)
        WSq = WSB[:, 0:4096].bitcast(BF16)
        WSk = WSB[:, 4096:8192].bitcast(BF16)
        WSv = WSB[:, 8192:12288].bitcast(BF16)
        WSo = C.bf16(EC * E)

        nm2 = negmb[:, 16:32]
        rs2 = rstdb[:, 16:32]

        def wld(s):
            for wsrc, wdst in ((qwT_d, WSq), (kwT_d, WSk), (vwT_d, WSv), (owT_d, WSo)):
                for k in range(EC):
                    s.dma_start(out=wdst[:, E * k:E * (k + 1)],
                                in_=wsrc[128 * k:128 * (k + 1), :]).then_inc(sw.h, 16)
        em.add("sync", wld)
        sw.n += 4 * EC * 16
        w_done = sw.n

        def vslot(u):
            if u < 0 or u >= NT:
                return zeroV
            return Vr[u % VRS]

        def pbz(v):
            v.memset(bass.AP(pbuf.tensor, pbuf.offset,
                             [list(pbuf.ap[0]), [384, 8], [1, 96]]), 0.0)
            v.memset(bass.AP(pbuf.tensor, pbuf.offset + 288,
                             [list(pbuf.ap[0]), [384, 8], [1, 96]]), 0.0)
            v.drain().then_inc(sv.h, 1)
        em.add("vector", pbz)
        sv.n += 1

        def emit_qkv(t):
            gws = G.snap("c_ln2t")

            def lnq(v, t=t, gws=gws):
                G.apply(v, gws)
                v.tensor_scalar(ln2t, X1[:, E * t:E * (t + 1)], nm2[:, t:t + 1],
                                rs2[:, t:t + 1], ALU.add, ALU.mult).then_inc(sv.h, 1)
            em.add("vector", lnq)
            sv.n += 1
            ln_done = sv.n

            for cc in range(EC):
                bi = 2 + cc % 3 if cc % 3 < 2 else 7
                bank = PS[bi]
                bslot = bank[:, :].bitcast(BF16)[:, 0:128]
                trslot = ("c_tr", bi)
                gwt = G.snap(trslot)

                def tr(te, bslot=bslot, cc=cc, ln_done=ln_done, gwt=gwt):
                    te.wait_ge(sv.h, ln_done)
                    G.apply(te, gwt)
                    te.transpose(bslot, ln2t[:, 128 * cc:128 * (cc + 1)],
                                 ident[:, :]).then_inc(stt.h, 1)
                em.add("tensor", tr)
                stt.n += 1
                trd = stt.n

                def ev(a, bslot=bslot, dst=ln2f[:, 128 * cc:128 * (cc + 1)], trd=trd):
                    a.wait_ge(stt.h, trd)
                    a.copy(dst, bslot).then_inc(sa.h, 1)
                em.add("scalar", ev)
                sa.n += 1
                G.set(trslot, sa, sa.n)
            lnf_done = (sa, sa.n)
            G.set("c_ln2t", stt, stt.n)

            for proj, WS, bias, scale in (("q", WSq, qbc, 0.125), ("k", WSk, kbc, None)):
                for oc in range(EC):
                    bank = 5 + (oc % 2)
                    gwp = G.snap(("c_qkv", bank))

                    def pmm(te, WS=WS, oc=oc, bank=bank, lnf=lnf_done, gwp=gwp,
                            wdn=w_done):
                        te.wait_ge(sw.h, wdn)
                        te.wait_ge(lnf[0].h, lnf[1])
                        G.apply(te, gwp)
                        for k in range(EC):
                            last = te.matmul(PS[bank][:, 0:128],
                                             WS[:, E * k + 128 * oc:E * k + 128 * (oc + 1)],
                                             ln2f[:, 128 * k:128 * (k + 1)],
                                             start=(k == 0), stop=(k == EC - 1))
                        last.then_inc(stt.h, 1)
                    em.add("tensor", pmm)
                    stt.n += 1
                    pdone = stt.n

                    if proj == "q":
                        dst = Qr[t % 2][:, 128 * oc:128 * (oc + 1)]
                        gwr = G.snap(("c_qr", t % 2))

                        def pev(v, dst=dst, oc=oc, bank=bank, pdone=pdone, gwr=gwr):
                            v.wait_ge(stt.h, pdone)
                            G.apply(v, gwr)
                            v.tensor_scalar(dst, PS[bank][:, 0:128], qbc[:, oc:oc + 1],
                                            0.125, ALU.add, ALU.mult).then_inc(sv.h, 1)
                    else:
                        dst = Kr[t % KRS][:, 192 * oc + 32:192 * oc + 160]
                        gwr = G.snap(("c_kr", t % KRS))

                        def pev(v, dst=dst, oc=oc, bank=bank, pdone=pdone, gwr=gwr):
                            v.wait_ge(stt.h, pdone)
                            G.apply(v, gwr)
                            v.tensor_scalar(dst, PS[bank][:, 0:128], kbc[:, oc:oc + 1],
                                            None, ALU.add).then_inc(sv.h, 1)
                    em.add("vector", pev)
                    sv.n += 1
                    G.set(("c_qkv", bank), sv, sv.n)
            for half in range(2):
                bank = 5 + (half % 2)
                gwp = G.snap(("c_qkv", bank))

                def vmm(te, half=half, bank=bank, lnf=lnf_done, gwp=gwp, wdn=w_done):
                    te.wait_ge(sw.h, wdn)
                    te.wait_ge(lnf[0].h, lnf[1])
                    G.apply(te, gwp)
                    for k in range(EC):
                        last = te.matmul(PS[bank][:, :],
                                         ln2f[:, 128 * k:128 * (k + 1)],
                                         WSv[:, E * k + 512 * half:E * k + 512 * (half + 1)],
                                         start=(k == 0), stop=(k == EC - 1))
                    last.then_inc(stt.h, 1)
                em.add("tensor", vmm)
                stt.n += 1
                vdone = stt.n
                dst = Vr[t % VRS][:, 512 * half:512 * (half + 1)]
                gwv = G.snap(("c_vr", t % VRS))

                def vev(v, dst=dst, bank=bank, half=half, vdone=vdone, gwv=gwv):
                    v.wait_ge(stt.h, vdone)
                    G.apply(v, gwv)
                    v.tensor_tensor(dst, PS[bank][:, :],
                                    vbrep[:, 512 * half:512 * (half + 1)],
                                    ALU.add).then_inc(sv.h, 1)
                em.add("vector", vev)
                sv.n += 1
                G.set(("c_qkv", bank), sv, sv.n)
            kv_done = sv.n

            def kedges(v, t=t, kv_done=kv_done):
                v.wait_ge(sv.h, kv_done)
                st_ = Kr[t % KRS]
                if t == 0:
                    v.memset(bass.AP(st_.tensor, st_.offset,
                                     [list(st_.ap[0]), [192, EC], [1, 32]]), 0.0)
                else:
                    sp = Kr[(t - 1) % KRS]
                    v.tensor_copy(
                        bass.AP(st_.tensor, st_.offset,
                                [list(st_.ap[0]), [192, EC], [1, 32]]),
                        bass.AP(sp.tensor, sp.offset + 128,
                                [list(sp.ap[0]), [192, EC], [1, 32]]))
                    v.tensor_copy(
                        bass.AP(sp.tensor, sp.offset + 160,
                                [list(sp.ap[0]), [192, EC], [1, 32]]),
                        bass.AP(st_.tensor, st_.offset + 32,
                                [list(st_.ap[0]), [192, EC], [1, 32]]))
                v.drain().then_inc(sv.h, 1)
            em.add("vector", kedges)
            sv.n += 1
            return sv.n

        def emit_kpad_end(t):
            def kz(v, t=t, w=sv.n):
                v.wait_ge(sv.h, w)
                st_ = Kr[t % KRS]
                v.memset(bass.AP(st_.tensor, st_.offset + 160,
                                 [list(st_.ap[0]), [192, EC], [1, 32]]), 0.0)
                v.drain().then_inc(sv.h, 1)
            em.add("vector", kz)
            sv.n += 1
            return sv.n

        def emit_att(t, k_ready):
            at = attb[t % 2]
            for b in range(2):
                for hp in range(4 * b, 4 * b + 4):
                    gws = G.snap(("c_s", 0), ("c_s", 1))

                    def smm(te, t=t, hp=hp, k_ready=k_ready, gws=gws):
                        te.wait_ge(sv.h, k_ready)
                        G.apply(te, gws)
                        for h01 in range(2):
                            p0 = 64 * h01
                            te.matmul(PS[h01][:, 0:192],
                                      Qr[t % 2][p0:p0 + 64, 128 * hp:128 * (hp + 1)],
                                      Kr[t % KRS][p0:p0 + 64, 192 * hp:192 * (hp + 1)],
                                      start=True, stop=True).then_inc(stt.h, 1)
                    em.add("tensor", smm)
                    stt.n += 2
                    sdone = stt.n

                    j = hp % 4
                    gwm = G.snap("c_pb" + str(j), "c_sm")

                    def msk(v, hp=hp, sdone=sdone, gwm=gwm):
                        v.wait_ge(stt.h, sdone)
                        G.apply(v, gwm)
                        v.tensor_mask_reduce(smask[:, 0:192], PS[0][:, 0:192],
                                             mst[:, :], men[:, :], 1.0, -1e30,
                                             ALU.max)
                        v.tensor_mask_reduce(smask[:, 192:384], PS[1][:, 0:192],
                                             mst[:, :], men[:, :], 1.0, -1e30,
                                             ALU.max).then_inc(sv.h, 1)
                    em.add("vector", msk)
                    sv.n += 1
                    mdone = sv.n
                    G.set(("c_s", 0), sv, sv.n)
                    G.set(("c_s", 1), sv, sv.n)

                    def ex(a, hp=hp, j=j, mdone=mdone):
                        a.wait_ge(sv.h, mdone)
                        a.activation(pbuf[:, 768 * j + 96:768 * j + 288],
                                     smask[:, 0:192], AF.Exp,
                                     accum_out=lsum[:, 2 * hp:2 * hp + 1])
                        a.activation(pbuf[:, 768 * j + 384 + 96:768 * j + 384 + 288],
                                     smask[:, 192:384], AF.Exp,
                                     accum_out=lsum[:, 2 * hp + 1:2 * hp + 2]
                                     ).then_inc(sa.h, 1)
                    em.add("scalar", ex)
                    sa.n += 1
                    G.set("c_sm", sa, sa.n)
                exp_done = sa.n

                def rc(v, b=b, exp_done=exp_done):
                    v.wait_ge(sa.h, exp_done)
                    v.drain()
                    v.reciprocal(rsum[:, 8 * b:8 * b + 8], lsum[:, 8 * b:8 * b + 8])
                    v.drain().then_inc(sv.h, 1)
                em.add("vector", rc)
                sv.n += 1
                rc_done = sv.n

                for hp in range(4 * b, 4 * b + 4):
                    j = hp % 4

                    def pm(v, hp=hp, j=j, rc_done=rc_done):
                        v.wait_ge(sv.h, rc_done)
                        v.tensor_scalar(pbuf[:, 768 * j + 96:768 * j + 288],
                                        pbuf[:, 768 * j + 96:768 * j + 288],
                                        rsum[:, 2 * hp:2 * hp + 1], None, ALU.mult)
                        v.tensor_scalar(pbuf[:, 768 * j + 384 + 96:768 * j + 384 + 288],
                                        pbuf[:, 768 * j + 384 + 96:768 * j + 384 + 288],
                                        rsum[:, 2 * hp + 1:2 * hp + 2], None,
                                        ALU.mult).then_inc(sv.h, 1)
                    em.add("vector", pm)
                    sv.n += 1
                    pm_done = sv.n

                    for i in range(6):
                        bi = 2 if i % 3 == 0 else (3 if i % 3 == 1 else 7)
                        bank = PS[bi]
                        bkey = ("c_tr", bi)
                        bslot = bank[:, :].bitcast(BF16)[:, 0:128]
                        gwt = G.snap(bkey)

                        def trp(te, j=j, i=i, bslot=bslot, pm_done=pm_done, gwt=gwt):
                            te.wait_ge(sv.h, pm_done)
                            G.apply(te, gwt)
                            te.transpose(bslot, pbuf[:, 768 * j + 128 * i:768 * j + 128 * (i + 1)],
                                         ident[:, :]).then_inc(stt.h, 1)
                        em.add("tensor", trp)
                        stt.n += 1
                        trd = stt.n

                        def evp(a, i=i, bslot=bslot, trd=trd):
                            a.wait_ge(stt.h, trd)
                            a.copy(ptr[:, 128 * i:128 * (i + 1)], bslot).then_inc(sa.h, 1)
                        em.add("scalar", evp)
                        sa.n += 1
                        G.set(bkey, sa, sa.n)
                    ptr_done = (sa, sa.n)

                    gwa = G.snap("c_av")

                    def avm(te, t=t, hp=hp, ptr_done=ptr_done, gwa=gwa):
                        te.wait_ge(ptr_done[0].h, ptr_done[1])
                        G.apply(te, gwa)
                        for h01 in range(2):
                            hh = 2 * hp + h01
                            for i in range(3):
                                vs_ = vslot(t - 1 + i)
                                last = te.matmul(PS[4][64 * h01:64 * h01 + 64, 0:128],
                                                 vs_[:, 64 * hh:64 * hh + 64],
                                                 ptr[:, 384 * h01 + 128 * i:384 * h01 + 128 * (i + 1)],
                                                 start=(i == 0), stop=(i == 2))
                        last.then_inc(stt.h, 1)
                    em.add("tensor", avm)
                    stt.n += 1
                    av_done = stt.n
                    G.set(("c_pb" + str(j)), stt, stt.n)

                    def ave(a, hp=hp, t=t, av_done=av_done):
                        a.wait_ge(stt.h, av_done)
                        a.copy(attb[t % 2][:, 128 * hp:128 * (hp + 1)],
                               PS[4][:, 0:128]).then_inc(sa.h, 1)
                    em.add("scalar", ave)
                    sa.n += 1
                    G.set("c_av", sa, sa.n)
            att_done = sa.n
            G.set(("c_qr", t % 2), stt, stt.n)
            G.set(("c_kr", t % KRS), stt, stt.n)
            for u in (t - 1, t, t + 1):
                if 0 <= u < NT:
                    G.set(("c_vr", u % VRS), stt, stt.n)

            for half in range(2):
                bank = 5 + half
                gwo = G.snap(("c_qkv", bank))

                def omm(te, t=t, half=half, bank=bank, att_done=att_done, gwo=gwo):
                    te.wait_ge(sa.h, att_done)
                    G.apply(te, gwo)
                    for k in range(EC):
                        last = te.matmul(PS[bank][:, :],
                                         attb[t % 2][:, 128 * k:128 * (k + 1)],
                                         WSo[:, E * k + 512 * half:E * k + 512 * (half + 1)],
                                         start=(k == 0), stop=(k == EC - 1))
                    last.then_inc(stt.h, 1)
                em.add("tensor", omm)
                stt.n += 1
                odone = stt.n

                def oev(v, t=t, half=half, bank=bank, odone=odone):
                    v.wait_ge(stt.h, odone)
                    xs = X1[:, E * t + 512 * half:E * t + 512 * (half + 1)]
                    v.scalar_tensor_tensor(xs, PS[bank][:, :], 1.0, xs,
                                           ALU.mult, ALU.add)
                    v.scalar_tensor_tensor(xs, obrep[:, 512 * half:512 * (half + 1)],
                                           1.0, xs, ALU.mult, ALU.add).then_inc(sv.h, 1)
                em.add("vector", oev)
                sv.n += 1
                G.set(("c_qkv", bank), sv, sv.n)
            x2_done = sv.n

            def st3(a, t=t, x2_done=x2_done):
                a.wait_ge(sv.h, x2_done)
                a.activation(cjunk, X1[:, E * t:E * (t + 1)], AF.Identity,
                             accum_out=stats[:, 64 + t:65 + t])
                a.activation(cjunk, X1[:, E * t:E * (t + 1)], AF.Square,
                             accum_out=stats[:, 80 + t:81 + t]).then_inc(sa.h, 1)
            em.add("scalar", st3)
            sa.n += 1

        import os as _os3
        _natt = int(_os3.environ.get("DBG_ATT_TILES", str(NT)))
        k_ready = {}
        for t in range(_natt):
            k_ready[t - 1] = emit_qkv(t)
            if t >= 1:
                emit_att(t - 1, k_ready[t - 1])
        if _natt < NT:
            return
        k_ready[NT - 1] = emit_kpad_end(NT - 1)
        emit_att(NT - 1, k_ready[NT - 1])

        def sfin(a):
            a.drain()
            a.activation(cjunk[:, 0:1], stats[:, 64:65], AF.Identity).then_inc(sa.h, 1)
        em.add("scalar", sfin)
        sa.n += 1
        emit_stats_finish(2, sa, sa.n)

        def wv(v):
            v.wait_ge(sv.h, sv.n)
        em.add("vector", wv)
        em.flush()

    if DEBUG_STAGE == -2:
        x3d = x_in.rearrange("(a p) e -> a p e", p=128)
        y3d = y_out.rearrange("(a p) e -> a p e", p=128)

        def mv(s):
            s.dma_start(out=X1[:, 0:E], in_=x3d[0, :, :]).then_inc(sx.h, 16)
        em.add("sync", mv)
        sx.n += 16
        n0 = sx.n

        def cp(v):
            v.wait_ge(sx.h, n0)
            v.tensor_scalar_mul(X1[:, E:2 * E], X1[:, 0:E], 2.0).then_inc(sv.h, 1)
        em.add("vector", cp)
        sv.n += 1
        em.flush()
        n1 = sv.n

        def out(g):
            g.wait_ge(sv.h, n1)
            g.dma_start(out=y3d[0, :, :], in_=X1[:, E:2 * E]).then_inc(sy.h, 16)
            g.wait_ge(sy.h, 16)
        em.add("gpsimd", out)
        sy.n += 16
        em.flush()
        es.close()
        mybir.codegen_inst_isa_subclasses(nc)
        return nc
    import os as _os2
    _tap = _os2.environ.get("DBG_TAP", "")
    emit_preproc()
    if _tap == "pre":
        emit_pre(0)
        y3d = y_out.rearrange("(a p) e -> a p e", p=128)
        n0 = sv.n + sa.n

        def dmp(g, wv=sv.n, wa=sa.n):
            g.wait_ge(sv.h, wv)
            g.wait_ge(sa.h, wa)
            g.dma_start(out=y3d[0, :, 0:96], in_=stats[:, :]).then_inc(sy.h, 16)
            g.dma_start(out=y3d[0, :, 96:144], in_=negmb[:, :]).then_inc(sy.h, 16)
            g.dma_start(out=y3d[0, :, 144:192], in_=rstdb[:, :]).then_inc(sy.h, 16)
            g.dma_start(out=y3d[1, :, :], in_=X1[:, 0:E]).then_inc(sy.h, 16)
            g.wait_ge(sy.h, 64)
        em.add("gpsimd", dmp)
        sy.n += 64
        em.flush()
        es.close()
        mybir.codegen_inst_isa_subclasses(nc)
        return nc
    if DEBUG_STAGE == -1:
        em.flush()
        es.close()
        mybir.codegen_inst_isa_subclasses(nc)
        return nc
    if DEBUG_STAGE == 0:
        dbg = WKA[:, 0:1408].bitcast(BF16)
        dbgf = WSB[:, 0:2816]
        y3d = y_out.rearrange("(a p) e -> a p e", p=128)
        base_sy = sy.n

        def dget(s, w=sg.n):
            s.wait_ge(sg.h, w)
            s.dma_start(out=dbg[:, 0:2816], in_=wgT_d[0:128, :]).then_inc(sx.h, 16)
        em.add("sync", dget)
        sx.n += 16
        n0 = sx.n

        def dcast(v, n0=n0):
            v.wait_ge(sx.h, n0)
            v.tensor_copy(dbgf[:, 0:2816], dbg[:, 0:2816]).then_inc(sv.h, 1)
        em.add("vector", dcast)
        sv.n += 1
        n1 = sv.n

        def dout(g, n1=n1):
            g.wait_ge(sv.h, n1)
            for j in range(2):
                g.dma_start(out=y3d[j, :, :], in_=dbgf[:, 1024 * j:1024 * (j + 1)]).then_inc(sy.h, 16)
        em.add("gpsimd", dout)
        sy.n += 32
        n2 = sy.n

        def dget2(s, n2=n2):
            s.wait_ge(sy.h, n2)
            s.dma_start(out=dbg[:, 0:1024], in_=wdT_d[0:128, :]).then_inc(sx.h, 16)
        em.add("sync", dget2)
        sx.n += 16
        n3 = sx.n

        def dcast2(v, n3=n3):
            v.wait_ge(sx.h, n3)
            v.tensor_copy(dbgf[:, 0:1024], dbg[:, 0:1024]).then_inc(sv.h, 1)
        em.add("vector", dcast2)
        sv.n += 1
        n4 = sv.n
        fin_sy = sy.n + 16

        def dout2(g, n4=n4, fin_sy=fin_sy):
            g.wait_ge(sv.h, n4)
            g.dma_start(out=y3d[4, :, :], in_=dbgf[:, 0:1024]).then_inc(sy.h, 16)
            g.wait_ge(sy.h, fin_sy)
        em.add("gpsimd", dout2)
        sy.n += 16
        em.flush()
        es.close()
        mybir.codegen_inst_isa_subclasses(nc)
        return nc
    for seq in range(NSEQ):
        emit_pre(seq)
        xqf_ready.clear()
        emit_ffn(seq, 0, 0, False)
        emit_ffn(seq, 1, 0, DEBUG_STAGE == 1)
        if DEBUG_STAGE >= 2:
            emit_attn(seq)
            if DEBUG_STAGE >= 3:
                emit_ffn(seq, 0, 2, False)
                emit_ffn(seq, 1, 2, True)
            if DEBUG_STAGE == 2:
                y3d = y_out.rearrange("(a p) e -> a p e", p=128)
                wsv = sv.n

                def dx2(g, seq=seq, wsv=wsv):
                    g.wait_ge(sv.h, wsv)
                    for t in range(NT):
                        g.dma_start(out=y3d[seq * NT + t, :, :],
                                    in_=X1[:, E * t:E * (t + 1)]).then_inc(sy.h, 16)
                em.add("gpsimd", dx2)
                sy.n += NT * 16

                def wgf(g):
                    g.wait_ge(sy.h, sy.n)
                em.add("gpsimd", wgf)
                em.flush()

    es.close()
    mybir.codegen_inst_isa_subclasses(nc)
    return nc


_PROG = None


def kernel(**inputs):
    global _PROG
    x = np.ascontiguousarray(np.asarray(inputs["L0_attn_hidden_states"], dtype=np.float32))
    ident = np.eye(128, dtype=BF)
    mstart = np.arange(128, dtype=np.float32).reshape(128, 1)
    mend = mstart + 65.0

    if _PROG is None:
        _PROG = build_program()
    nc = _PROG

    base = {
        "wg": np.ascontiguousarray(np.asarray(inputs["L0_ffn_w_gate"], dtype=np.float32)),
        "wu": np.ascontiguousarray(np.asarray(inputs["L0_ffn_w_up"], dtype=np.float32)),
        "wd": np.ascontiguousarray(np.asarray(inputs["L0_ffn_w_down"], dtype=np.float32)),
        "qw": np.ascontiguousarray(np.asarray(inputs["L0_attn_q_w"], dtype=np.float32)),
        "kw": np.ascontiguousarray(np.asarray(inputs["L0_attn_k_w"], dtype=np.float32)),
        "vw": np.ascontiguousarray(np.asarray(inputs["L0_attn_v_w"], dtype=np.float32)),
        "ow": np.ascontiguousarray(np.asarray(inputs["L0_attn_o_w"], dtype=np.float32)),
        "qb": np.ascontiguousarray(np.asarray(inputs["L0_attn_q_b"], dtype=np.float32)),
        "kb": np.ascontiguousarray(np.asarray(inputs["L0_attn_k_b"], dtype=np.float32)),
        "vb": np.ascontiguousarray(np.asarray(inputs["L0_attn_v_b"], dtype=np.float32)),
        "ob": np.ascontiguousarray(np.asarray(inputs["L0_attn_o_b"], dtype=np.float32)),
        "ident": ident, "mstart": mstart, "mend": mend,
    }
    in_maps = []
    for c in range(NCORES):
        m = dict(base)
        m["x"] = np.ascontiguousarray(x[c * NSEQ:(c + 1) * NSEQ].reshape(NSEQ * SEQ, E))
        in_maps.append(m)
    res = run_bass_kernel_spmd(nc, in_maps, core_ids=list(range(NCORES)))
    out = np.concatenate([r["y"].reshape(NSEQ, SEQ, E) for r in res.results], axis=0)
    return out


# version 02
# speedup vs baseline: 1.2004x; 1.2004x over previous
import contextlib
import numpy as np
import ml_dtypes

import concourse.bass as bass
import concourse.mybir as mybir
from concourse.bass_utils import run_bass_kernel_spmd
from concourse import dve_ops as dvo
from concourse.dve_op import DveOp
from concourse.dve_spec import Spec, Src0, C0, C1, Zero, maxx, minn, lower
from concourse.dve_table_gen import dve_ver_for

F32 = mybir.dt.float32
BF16 = mybir.dt.bfloat16
ALU = mybir.AluOpType
AF = mybir.ActivationFunctionType
AX = mybir.AxisListType
BF = ml_dtypes.bfloat16

MAGIC = float(np.float32(1.5 * 2 ** 23))
DEKC = float(np.float32(2 ** 22 + 1))

NCORES = 8
B = 16
SEQ = 2048
E = 1024
EC = 8
I = 2816
NS = 2
IH = I // NS
ICH = IH // 128
NT = SEQ // 128
NSEQ = B // NCORES
NHEAD = 16
HD = 64
EPS = 1e-5

DEBUG_STAGE = 3


def _refA(in0, in1, s0, s1, imm2):
    f = np.float64 if in0.dtype == np.float64 else np.float32
    r1 = ((in0.astype(f) + np.array(s1, f)) - np.array(s1, f)).astype(f)
    return np.clip(r1, -s0, s0).astype(f)


def _refB(in0, in1, s0, s1, imm2):
    an32 = in0.astype(np.float32)
    w = (an32 * np.float32(s1)).astype(np.float32)
    t = (w - an32).astype(np.float32)
    D = (w - t).astype(in0.dtype if in0.dtype == np.float64 else np.float32)
    return (D - np.clip(D, -s0, s0)).astype(D.dtype)


def register_ops():
    ver = dve_ver_for("TRN2")
    out = []
    _r1 = (Src0 + C1) - C1
    specA = Spec(body=maxx(minn(_r1, C0), Zero - C0))
    _w = Src0 * C1
    _D = _w - (_w - Src0)
    specB = Spec(body=_D - maxx(minn(_D, C0), Zero - C0))
    for name, spec, ref in [("ANT_FP4_A", specA, _refA), ("ANT_FP4_B", specB, _refB)]:
        ex = next((o for o in dvo.OPS if o.name == name), None)
        if ex is not None:
            out.append(ex)
            continue
        compiled = lower(spec, ver=ver, perf_en=False, name=name)
        op = DveOp(name, spec, subdim=False, uops_sha={ver: compiled.sha(ver)},
                   reference=ref)
        row = dvo._CUSTOM_DVE_ROW_BASE + len(dvo.OPS)
        assert row < 0x20
        dvo.OPS.append(op)
        dvo._SUB_OPCODE_FOR_NAME[name] = row
        dvo.CUSTOM_DVE_SPECS[name] = spec
        dvo.CUSTOM_DVE_REFS[name] = ref
        out.append(op)
    return out


OPA, OPB = register_ops()


class Sem:
    def __init__(self, h):
        self.h = h
        self.n = 0

    def inc(self, inst, k=1):
        inst.then_inc(self.h, k)
        self.n += k
        return self.n


class Emitter:
    ENGINES = ("sync", "tensor", "vector", "scalar", "gpsimd")

    def __init__(self, nc):
        self.nc = nc
        self.new_phase()

    def new_phase(self):
        self.ops = {e: [] for e in self.ENGINES}

    def add(self, eng, fn):
        self.ops[eng].append(fn)

    def flush(self):
        ops = self.ops
        with self.nc.Block() as block:
            @block.sync
            def _(eng, ops=ops):
                for f in ops["sync"]:
                    f(eng)

            @block.tensor
            def _(eng, ops=ops):
                for f in ops["tensor"]:
                    f(eng)

            @block.vector
            def _(eng, ops=ops):
                for f in ops["vector"]:
                    f(eng)

            @block.scalar
            def _(eng, ops=ops):
                for f in ops["scalar"]:
                    f(eng)

            @block.gpsimd
            def _(eng, ops=ops):
                for f in ops["gpsimd"]:
                    f(eng)
        self.new_phase()


class Carver:
    def __init__(self, pools):
        self.pools = pools
        self.pos = [0] * len(pools)

    def f32(self, cols):
        for i, (t, n) in enumerate(self.pools):
            if self.pos[i] + cols <= n:
                off = self.pos[i]
                self.pos[i] += cols
                return t[:, off:off + cols]
        raise RuntimeError(f"carver OOM: need {cols}, pos={self.pos}")

    def bf16(self, cols):
        assert cols % 2 == 0
        return self.f32(cols // 2).bitcast(BF16)


def bcast16(ap, nblocks):
    return bass.AP(ap.tensor, ap.offset,
                   [list(ap.ap[0]), [list(ap.ap[1])[0], nblocks], [0, 16]])


class Guards:

    def __init__(self):
        self.g = {}

    def set(self, key, sem, n):
        self.g[key] = (sem, n)

    def snap(self, *keys):
        best = {}
        for k in keys:
            if k in self.g:
                s, n = self.g[k]
                if best.get(id(s), (None, -1))[1] < n:
                    best[id(s)] = (s, n)
        return tuple(best.values())

    @staticmethod
    def apply(eng, snap):
        for s, n in snap:
            if n > 0:
                eng.wait_ge(s.h, n)


def build_program():
    import os as _os
    nc = bass.Bass("TRN2", target_bir_lowering=False,
                   detect_race_conditions=not _os.environ.get("NORACE"))

    x_in = nc.dram_tensor("x", [NSEQ * SEQ, E], F32, kind="ExternalInput").ap()
    y_out = nc.dram_tensor("y", [NSEQ * SEQ, E], F32, kind="ExternalOutput").ap()
    wg_in = nc.dram_tensor("wg", [I, E], F32, kind="ExternalInput").ap()
    wu_in = nc.dram_tensor("wu", [I, E], F32, kind="ExternalInput").ap()
    wd_in = nc.dram_tensor("wd", [E, I], F32, kind="ExternalInput").ap()
    qw_in = nc.dram_tensor("qw", [E, E], F32, kind="ExternalInput").ap()
    kw_in = nc.dram_tensor("kw", [E, E], F32, kind="ExternalInput").ap()
    vw_in = nc.dram_tensor("vw", [E, E], F32, kind="ExternalInput").ap()
    ow_in = nc.dram_tensor("ow", [E, E], F32, kind="ExternalInput").ap()
    qb_in = nc.dram_tensor("qb", [E], F32, kind="ExternalInput").ap()
    kb_in = nc.dram_tensor("kb", [E], F32, kind="ExternalInput").ap()
    vb_in = nc.dram_tensor("vb", [E], F32, kind="ExternalInput").ap()
    ob_in = nc.dram_tensor("ob", [E], F32, kind="ExternalInput").ap()
    ident_in = nc.dram_tensor("ident", [128, 128], BF16, kind="ExternalInput").ap()
    mst_in = nc.dram_tensor("mstart", [128, 1], F32, kind="ExternalInput").ap()
    men_in = nc.dram_tensor("mend", [128, 1], F32, kind="ExternalInput").ap()

    wgT_d = nc.dram_tensor("wgT", [EC * 128, I], BF16, kind="Internal").ap()
    wuT_d = nc.dram_tensor("wuT", [EC * 128, I], BF16, kind="Internal").ap()
    wdT_d = nc.dram_tensor("wdT", [I, E], BF16, kind="Internal").ap()
    qwT_d = nc.dram_tensor("qwT", [EC * 128, E], BF16, kind="Internal").ap()
    kwT_d = nc.dram_tensor("kwT", [EC * 128, E], BF16, kind="Internal").ap()
    vwT_d = nc.dram_tensor("vwT", [EC * 128, E], BF16, kind="Internal").ap()
    owT_d = nc.dram_tensor("owT", [EC * 128, E], BF16, kind="Internal").ap()
    wT_scratch = {id(wg_in): wgT_d, id(wu_in): wuT_d, id(wd_in): wdT_d,
                  id(qw_in): qwT_d, id(kw_in): kwT_d, id(vw_in): vwT_d,
                  id(ow_in): owT_d}

    es = contextlib.ExitStack()

    def sbt(name, shape, dtype):
        return es.enter_context(nc.sbuf_tensor(name, shape, dtype))

    ident = sbt("identt", [128, 128], BF16)
    mst = sbt("mstt", [128, 1], F32)
    men = sbt("ment", [128, 1], F32)
    qbc = sbt("qbc", [128, EC], F32)
    kbc = sbt("kbc", [128, EC], F32)
    vbrep = sbt("vbrep", [128, E], F32)
    obrep = sbt("obrep", [128, E], F32)
    zeroV = sbt("zeroV", [128, E], BF16)
    stats = sbt("stats", [128, 96], F32)
    negmb = sbt("negmb", [128, 48], F32)
    rstdb = sbt("rstdb", [128, 48], F32)
    tinyw = sbt("tinyw", [128, 16], F32)

    X1 = sbt("X1", [128, NT * E], F32)
    XQF = sbt("XQF", [128, NT * E // 2], F32)
    WSB = sbt("WSB", [128, 12288], F32)
    WKA = sbt("WKA", [128, 12288], F32)
    junk = WKA[:, 0:512].bitcast(BF16)
    brow = None

    PS = [es.enter_context(nc.psum_tensor(f"ps{i}", [128, 512], F32))
          for i in range(8)]

    sx = Sem(es.enter_context(nc.semaphore("sx")))
    sw = Sem(es.enter_context(nc.semaphore("sw")))
    sv = Sem(es.enter_context(nc.semaphore("sv")))
    sa = Sem(es.enter_context(nc.semaphore("sa")))
    stt = Sem(es.enter_context(nc.semaphore("stt")))
    sg = Sem(es.enter_context(nc.semaphore("sg")))
    sy = Sem(es.enter_context(nc.semaphore("sy")))

    em = Emitter(nc)
    G = Guards()

    def emit_quant(vec_ops, src, n, scl, q2l, hi, q_out, inc=None):
        nb = n // 16
        bm, rec, bs = scl[:, 0:nb], scl[:, nb:2 * nb], scl[:, 2 * nb:3 * nb]

        def f(v, src=src, nb=nb, bm=bm, rec=rec, bs=bs, q2l=q2l, hi=hi,
              q_out=q_out, inc=inc):
            s3 = src.rearrange("p (s x) -> p s x", x=16)
            v.tensor_reduce(bm, s3, AX.X, ALU.max, apply_absolute_value=True)
            v.tensor_scalar_max(bm, bm, 6e-12)
            v.reciprocal(rec, bm)
            v.drain()
            v.tensor_scalar_mul(rec, rec, 12.0)
            v.tensor_scalar_mul(bs, bm, float(np.float32(1.0 / 12.0)))
            v.tensor_tensor(s3, s3, bcast16(rec, nb), ALU.mult)
            v._custom_dve(OPA, out=q2l, in0=src, s0=4.0, s1=MAGIC)
            v._custom_dve(OPB, out=hi, in0=src, s0=4.0, s1=DEKC)
            v.tensor_add(q2l, q2l, hi)
            last = v.tensor_tensor(q_out.rearrange("p (s x) -> p s x", x=16),
                                   q2l.rearrange("p (s x) -> p s x", x=16),
                                   bcast16(bs, nb), ALU.mult)
            if inc is not None:
                last.then_inc(inc.h, 1)
        vec_ops.append(f)

    def emit_stats_finish(slot, wait_sem, wait_n):
        sc = 32 * slot
        nsl = 16 * slot

        def fin(v):
            v.wait_ge(wait_sem.h, wait_n)
            v.tensor_scalar_mul(negmb[:, nsl:nsl + 16], stats[:, sc:sc + 16], -1.0 / E)
            v.drain()
            v.tensor_mul(tinyw[:, 0:16], negmb[:, nsl:nsl + 16], negmb[:, nsl:nsl + 16])
            v.tensor_scalar(rstdb[:, nsl:nsl + 16], stats[:, sc + 16:sc + 32],
                            1.0 / E, EPS, ALU.mult, ALU.add)
            v.drain()
            v.tensor_sub(rstdb[:, nsl:nsl + 16], rstdb[:, nsl:nsl + 16],
                         tinyw[:, 0:16]).then_inc(sv.h, 1)
        em.add("vector", fin)
        sv.n += 1
        var_done = sv.n

        def sq(a):
            a.wait_ge(sv.h, var_done)
            a.activation(tinyw[:, 0:16], rstdb[:, nsl:nsl + 16], AF.Ln)
            a.drain()
            a.activation(rstdb[:, nsl:nsl + 16], tinyw[:, 0:16], AF.Exp,
                         scale=-0.5).then_inc(sa.h, 1)
        em.add("scalar", sq)
        sa.n += 1
        return sa.n

    def emit_preproc():
        C = Carver([(WKA, 12288), (WSB, 12288)])
        browt = C.f32(E)
        brow = browt

        import os
        _dbg_skip = set(os.environ.get("DBGSKIP", "").split(","))
        _loadlist = []
        if "ident" not in _dbg_skip:
            _loadlist.append(lambda s: s.dma_start(out=ident[:, :], in_=ident_in))
        if "mst" not in _dbg_skip:
            _loadlist.append(lambda s: s.dma_start(out=mst[:, :], in_=mst_in))
            _loadlist.append(lambda s: s.dma_start(out=men[:, :], in_=men_in))
        if "qbc" not in _dbg_skip:
            _loadlist.append(lambda s: s.dma_start(out=qbc[:, :], in_=qb_in.rearrange("(c p) -> p c", p=128), allow_slow_non_contiguous=True))
            _loadlist.append(lambda s: s.dma_start(out=kbc[:, :], in_=kb_in.rearrange("(c p) -> p c", p=128), allow_slow_non_contiguous=True))
        if "rep" not in _dbg_skip:
            _loadlist.append(lambda s: s.dma_start(out=vbrep[:, :], in_=bass.AP(vb_in.tensor, 0, [[0, 128], [1, E]])))
            _loadlist.append(lambda s: s.dma_start(out=obrep[:, :], in_=bass.AP(ob_in.tensor, 0, [[0, 128], [1, E]])))

        def loads(s):
            for f in _loadlist:
                f(s).then_inc(sx.h, 16)
        em.add("sync", loads)
        sx.n += len(_loadlist) * 16

        def zv(v):
            v.memset(zeroV[:, :], 0.0).then_inc(sv.h, 1)
        em.add("vector", zv)
        sv.n += 1

        wtmp = [C.f32(I), C.f32(I)]
        wq = [C.bf16(I), C.bf16(I)]
        wscl = C.f32(3 * (I // 16) + 16)
        wq2l = C.bf16(I)
        whi = C.bf16(I)
        stg = [C.bf16(512), C.bf16(512)]

        jobs = [(wg_in, I, E, True), (wu_in, I, E, True), (wd_in, E, I, True),
                (qw_in, E, E, False), (kw_in, E, E, False),
                (vw_in, E, E, False), (ow_in, E, E, False)]
        chunks = []
        for (din, rows, cols, doq) in jobs:
            for r in range(rows // 128):
                chunks.append((din, cols, doq, r))
        import os
        _lim = int(os.environ.get("DBG_NCHUNKS", "999"))
        chunks = chunks[:_lim]

        for idx, (din, cols, doq, r) in enumerate(chunks):
            rot = idx % 2
            wt = wtmp[rot][:, 0:cols]
            wqb = wq[rot][:, 0:cols]
            dout = wT_scratch[id(din)]

            gw = G.snap(("pp_wt", rot))

            def ld(s, din=din, cols=cols, r=r, wt=wt, gw=gw):
                G.apply(s, gw)
                s.dma_start(out=wt, in_=din[128 * r:128 * (r + 1), 0:cols]).then_inc(sx.h, 16)
            em.add("sync", ld)
            sx.n += 16
            x_after = sx.n

            vec_ops = []

            gw2 = G.snap(("pp_wq", rot))

            def wv(v, x_after=x_after, gw2=gw2):
                v.wait_ge(sx.h, x_after)
                G.apply(v, gw2)
            vec_ops.append(wv)
            if doq:
                emit_quant(vec_ops, wt, cols, wscl, wq2l[:, 0:cols], whi[:, 0:cols],
                           wqb, inc=sv)
            else:
                vec_ops.append(lambda v, wt=wt, wqb=wqb:
                               v.tensor_copy(wqb, wt).then_inc(sv.h, 1))
            for f in vec_ops:
                em.add("vector", f)
            sv.n += 1
            q_done = sv.n
            G.set(("pp_wt", rot), sv, q_done)

            ncc = cols // 128
            for cc in range(ncc):
                slot = cc % 4
                bank = PS[4 + slot]
                bslot = bank[:, :].bitcast(BF16)[:, 0:128]

                gw3 = G.snap(("pp_ps", slot))

                def tr(t, wqb=wqb, cc=cc, bslot=bslot, q_done=q_done, gw3=gw3):
                    t.wait_ge(sv.h, q_done)
                    G.apply(t, gw3)
                    t.transpose(bslot, wqb[:, 128 * cc:128 * (cc + 1)],
                                ident[:, :]).then_inc(stt.h, 1)
                em.add("tensor", tr)
                stt.n += 1
                trd = stt.n

                srot = (cc // 4) % 2
                sslot = stg[srot][:, 128 * (cc % 4):128 * (cc % 4) + 128]

                gw4 = G.snap(("pp_stg", srot, cc % 4))

                def ev(a, bslot=bslot, sslot=sslot, trd=trd, gw4=gw4):
                    a.wait_ge(stt.h, trd)
                    G.apply(a, gw4)
                    a.copy(sslot, bslot).then_inc(sa.h, 1)
                em.add("scalar", ev)
                sa.n += 1
                G.set(("pp_ps", slot), sa, sa.n)
                ev_done = sa.n

                def out(g, dout=dout, cc=cc, r=r, sslot=sslot, ev_done=ev_done):
                    g.wait_ge(sa.h, ev_done)
                    g.dma_start(out=dout[128 * cc:128 * (cc + 1), 128 * r:128 * (r + 1)],
                                in_=sslot).then_inc(sg.h, 16)
                em.add("gpsimd", out)
                sg.n += 16
                G.set(("pp_stg", srot, cc % 4), sg, sg.n)
            G.set(("pp_wq", rot), stt, stt.n)

        def fin(g):
            g.wait_ge(sg.h, sg.n)
        em.add("gpsimd", fin)
        em.flush()

    def emit_pre(seq):
        x3d = x_in.rearrange("(a p) e -> a p e", p=128)

        def ld(s, seq=seq):
            for t in range(NT):
                s.dma_start(out=X1[:, E * t:E * (t + 1)],
                            in_=x3d[seq * NT + t, :, :]).then_inc(sx.h, 16)
        em.add("sync", ld)
        sx.n += NT * 16
        x_done = sx.n

        def acts(a, x_done=x_done):
            for t in range(NT):
                a.wait_ge(sx.h, x_done - (NT - 1 - t) * 16)
                a.activation(junk[:, :], X1[:, E * t:E * (t + 1)], AF.Identity,
                             accum_out=stats[:, t:t + 1])
                a.activation(junk[:, :], X1[:, E * t:E * (t + 1)], AF.Square,
                             accum_out=stats[:, 16 + t:17 + t])
            a.drain()
            a.activation(junk[:, 0:1], stats[:, 0:1], AF.Identity).then_inc(sa.h, 1)
        em.add("scalar", acts)
        sa.n += 1
        emit_stats_finish(0, sa, sa.n)
        em.flush()

    xqf_ready = {}

    def emit_ffn(seq, sl, ln_slot, write_y):
        C = Carver([(WKA, 12288)])
        lnt = C.f32(E)
        xscl = C.f32(3 * (E // 16))
        q2l = C.bf16(E)
        xqtm = C.bf16(E)
        gsil = C.f32(IH)
        hfull = C.f32(IH)
        hscl = C.f32(3 * (IH // 16))
        q2lh = C.bf16(IH)
        hih = C.bf16(IH)
        hqf = [C.bf16(ICH * 128) for _ in range(4)]
        wdb = [C.bf16(E), C.bf16(E)]

        WG = WSB[:, 0:EC * IH // 2].bitcast(BF16)
        WU = WSB[:, EC * IH // 2:EC * IH].bitcast(BF16)
        nm = negmb[:, 16 * ln_slot:16 * ln_slot + 16]
        rs = rstdb[:, 16 * ln_slot:16 * ln_slot + 16]

        def wld(s, sl=sl):
            for k in range(EC):
                s.dma_start(out=WG[:, IH * k:IH * (k + 1)],
                            in_=wgT_d[128 * k:128 * (k + 1), IH * sl:IH * (sl + 1)]
                            ).then_inc(sw.h, 16)
                s.dma_start(out=WU[:, IH * k:IH * (k + 1)],
                            in_=wuT_d[128 * k:128 * (k + 1), IH * sl:IH * (sl + 1)]
                            ).then_inc(sw.h, 16)
        em.add("sync", wld)
        sw.n += EC * 2 * 16
        w_done = sw.n

        nchunks = [(0, 512), (512, 512), (1024, 384)]
        hq_ready = {}
        x1_done = {}

        for g in range(4):
            tiles = [4 * g + j for j in range(4)]
            for tt_i, t in enumerate(tiles):
                xt = X1[:, E * t:E * (t + 1)]
                xqf_t = XQF[:, (E // 2) * t:(E // 2) * (t + 1)].bitcast(BF16)

                if sl == 0:
                    vec_ops = []

                    gwl = G.snap("lnt")

                    def lnq(v, xt=xt, t=t, gwl=gwl):
                        G.apply(v, gwl)
                        v.tensor_scalar(lnt, xt, nm[:, t:t + 1], rs[:, t:t + 1],
                                        ALU.add, ALU.mult)
                    vec_ops.append(lnq)
                    emit_quant(vec_ops, lnt, E, xscl, q2l, xqtm, xqtm, inc=sv)
                    for f in vec_ops:
                        em.add("vector", f)
                    sv.n += 1
                    xq_done = sv.n
                    G.set("lnt", sv, xq_done)

                    for cc in range(EC):
                        slot = cc % 4
                        bank = PS[4 + slot]
                        bslot = bank[:, :].bitcast(BF16)[:, 0:128]

                        gwt = G.snap(("f_ps", slot), "banks_all")

                        def tr(te, bslot=bslot, cc=cc, xq_done=xq_done, gwt=gwt):
                            te.wait_ge(sv.h, xq_done)
                            G.apply(te, gwt)
                            te.transpose(bslot, xqtm[:, 128 * cc:128 * (cc + 1)],
                                         ident[:, :]).then_inc(stt.h, 1)
                        em.add("tensor", tr)
                        stt.n += 1
                        trd = stt.n

                        def ev(a, bslot=bslot, dst=xqf_t[:, 128 * cc:128 * (cc + 1)], trd=trd):
                            a.wait_ge(stt.h, trd)
                            a.copy(dst, bslot).then_inc(sa.h, 1)
                        em.add("scalar", ev)
                        sa.n += 1
                        G.set(("f_ps", slot), sa, sa.n)
                    xqf_ready[t] = (sa, sa.n)

                xr_sem, xr_n = xqf_ready[t]

                for ci, (coff, clen) in enumerate(nchunks):
                    gb = PS[ci % 2]
                    ub = PS[2 + ci % 2]

                    gwg = G.snap(("gu_g", ci % 2), ("gu_u", ci % 2), "banks_all")

                    def gu(te, coff=coff, clen=clen, gb=gb, ub=ub, xqf_t=xqf_t,
                           xr_sem=xr_sem, xr_n=xr_n, gwg=gwg):
                        te.wait_ge(sw.h, w_done)
                        te.wait_ge(xr_sem.h, xr_n)
                        G.apply(te, gwg)
                        for k in range(EC):
                            te.matmul(gb[:, 0:clen], xqf_t[:, 128 * k:128 * (k + 1)],
                                      WG[:, IH * k + coff:IH * k + coff + clen],
                                      start=(k == 0), stop=(k == EC - 1))
                        for k in range(EC):
                            last = te.matmul(ub[:, 0:clen], xqf_t[:, 128 * k:128 * (k + 1)],
                                             WU[:, IH * k + coff:IH * k + coff + clen],
                                             start=(k == 0), stop=(k == EC - 1))
                        last.then_inc(stt.h, 1)
                    em.add("tensor", gu)
                    stt.n += 1
                    gu_done = stt.n

                    def sil(a, gb=gb, coff=coff, clen=clen, gu_done=gu_done):
                        a.wait_ge(stt.h, gu_done)
                        a.activation(gsil[:, coff:coff + clen], gb[:, 0:clen],
                                     AF.Silu).then_inc(sa.h, 1)
                    em.add("scalar", sil)
                    sa.n += 1
                    G.set(("gu_g", ci % 2), sa, sa.n)
                    sil_done = sa.n

                    def hm(v, ub=ub, coff=coff, clen=clen, sil_done=sil_done):
                        v.wait_ge(sa.h, sil_done)
                        v.tensor_mul(hfull[:, coff:coff + clen],
                                     gsil[:, coff:coff + clen],
                                     ub[:, 0:clen]).then_inc(sv.h, 1)
                    em.add("vector", hm)
                    sv.n += 1
                    G.set(("gu_u", ci % 2), sv, sv.n)

                vec_ops = []
                gwh = G.snap(("hq_b", 0), ("hq_b", 1))
                vec_ops.append(lambda v, gwh=gwh: G.apply(v, gwh))
                emit_quant(vec_ops, hfull, IH, hscl, q2lh, hih, hih, inc=sv)
                for f in vec_ops:
                    em.add("vector", f)
                sv.n += 1
                hq_done = sv.n
                G.set(("hq_b", 0), sv, hq_done)
                G.set(("hq_b", 1), sv, hq_done)

                for cc in range(ICH):
                    slot = cc % 4
                    bank = PS[4 + slot]
                    bslot = bank[:, :].bitcast(BF16)[:, 0:128]

                    gwt2 = G.snap(("f_ps", slot), "banks_all")

                    def tr(te, bslot=bslot, cc=cc, hq_done=hq_done, gwt2=gwt2):
                        te.wait_ge(sv.h, hq_done)
                        G.apply(te, gwt2)
                        te.transpose(bslot, hih[:, 128 * cc:128 * (cc + 1)],
                                     ident[:, :]).then_inc(stt.h, 1)
                    em.add("tensor", tr)
                    stt.n += 1
                    trd = stt.n

                    def ev(a, bslot=bslot, dst=hqf[tt_i][:, 128 * cc:128 * (cc + 1)], trd=trd):
                        a.wait_ge(stt.h, trd)
                        a.copy(dst, bslot).then_inc(sa.h, 1)
                    em.add("scalar", ev)
                    sa.n += 1
                    G.set(("f_ps", slot), sa, sa.n)
                hq_ready[tt_i] = (sa, sa.n)

            hq_sem, hq_n = hq_ready[3]
            for c in range(ICH):
                rot = c % 2

                gww = G.snap(("wd_b", rot))

                def wdl(s, c=c, rot=rot, sl=sl, gww=gww):
                    G.apply(s, gww)
                    s.dma_start(out=wdb[rot],
                                in_=wdT_d[128 * (ICH * sl + c):128 * (ICH * sl + c + 1), :]
                                ).then_inc(sw.h, 16)
                em.add("sync", wdl)
                sw.n += 16
                wd_ld = sw.n

                gwd = G.snap("banks_dn", ("f_ps", 0), ("f_ps", 1), ("f_ps", 2),
                             ("f_ps", 3), ("f_ps", 4), ("f_ps", 5), ("f_ps", 6),
                             ("f_ps", 7), ("gu_g", 0), ("gu_g", 1),
                             ("gu_u", 0), ("gu_u", 1)) if c == 0 else ()

                def dn(te, c=c, rot=rot, hq_n=hq_n, wd_ld=wd_ld, gwd=gwd):
                    te.wait_ge(hq_sem.h, hq_n)
                    te.wait_ge(sw.h, wd_ld)
                    G.apply(te, gwd)
                    for tt_i in range(4):
                        for half in range(2):
                            last = te.matmul(PS[2 * tt_i + half][:, :],
                                             hqf[tt_i][:, 128 * c:128 * (c + 1)],
                                             wdb[rot][:, 512 * half:512 * (half + 1)],
                                             start=(c == 0), stop=(c == ICH - 1))
                    last.then_inc(stt.h, 1)
                em.add("tensor", dn)
                stt.n += 1
                G.set(("wd_b", rot), stt, stt.n)
            dn_done = stt.n

            for tt_i, t in enumerate(tiles):
                def evd(v, tt_i=tt_i, t=t, dn_done=dn_done):
                    v.wait_ge(stt.h, dn_done)
                    for half in range(2):
                        xs = X1[:, E * t + 512 * half:E * t + 512 * (half + 1)]
                        last = v.scalar_tensor_tensor(xs, PS[2 * tt_i + half][:, :],
                                                      0.5, xs, ALU.mult, ALU.add)
                    last.then_inc(sv.h, 1)
                em.add("vector", evd)
                sv.n += 1
                x1_done[t] = sv.n
            G.set("banks_dn", sv, sv.n)
            G.set("banks_all", sv, sv.n)

            if write_y:
                y3d = y_out.rearrange("(a p) e -> a p e", p=128)
                for t in tiles:
                    def sty(gps, t=t, seq=seq, n=x1_done[t]):
                        gps.wait_ge(sv.h, n)
                        gps.dma_start(out=y3d[seq * NT + t, :, :],
                                      in_=X1[:, E * t:E * (t + 1)]).then_inc(sy.h, 16)
                    em.add("gpsimd", sty)
                    sy.n += 16

        if sl == 1 and not write_y:
            def acts(a, x1_done=dict(x1_done)):
                for t in range(NT):
                    a.wait_ge(sv.h, x1_done[t])
                    a.activation(junk[:, :], X1[:, E * t:E * (t + 1)], AF.Identity,
                                 accum_out=stats[:, 32 + t:33 + t])
                    a.activation(junk[:, :], X1[:, E * t:E * (t + 1)], AF.Square,
                                 accum_out=stats[:, 48 + t:49 + t])
                a.drain()
                a.activation(junk[:, 0:1], stats[:, 32:33], AF.Identity).then_inc(sa.h, 1)
            em.add("scalar", acts)
            sa.n += 1
            emit_stats_finish(1, sa, sa.n)

        def wv(v):
            v.wait_ge(sv.h, sv.n)
        em.add("vector", wv)
        if write_y:
            def wgf(g):
                g.wait_ge(sy.h, sy.n)
            em.add("gpsimd", wgf)
        em.flush()


    def emit_attn(seq):
        C = Carver([(WKA, 12288), (XQF, NT * E // 2)])
        ln2t = C.bf16(E)
        ln2f = C.bf16(EC * 128)
        Qr = [C.bf16(EC * 128), C.bf16(EC * 128)]
        KRS, VRS = 3, 4
        Kr = [C.bf16(EC * 192) for _ in range(KRS)]
        Vr = [C.bf16(E) for _ in range(VRS)]
        smask = C.f32(2 * 192)
        pbuf = C.bf16(8 * 384)
        ptr = C.bf16(2 * 384)
        attb = [C.bf16(EC * 128), C.bf16(EC * 128)]
        lsum = C.f32(16)
        rsum = C.f32(16)
        cjunk = C.bf16(E)
        WSq = WSB[:, 0:4096].bitcast(BF16)
        WSk = WSB[:, 4096:8192].bitcast(BF16)
        WSv = WSB[:, 8192:12288].bitcast(BF16)
        WSo = C.bf16(EC * E)

        nm2 = negmb[:, 16:32]
        rs2 = rstdb[:, 16:32]

        def wld(s):
            for wsrc, wdst in ((qwT_d, WSq), (kwT_d, WSk), (vwT_d, WSv), (owT_d, WSo)):
                for k in range(EC):
                    s.dma_start(out=wdst[:, E * k:E * (k + 1)],
                                in_=wsrc[128 * k:128 * (k + 1), :]).then_inc(sw.h, 16)
        em.add("sync", wld)
        sw.n += 4 * EC * 16
        w_done = sw.n

        def vslot(u):
            if u < 0 or u >= NT:
                return zeroV
            return Vr[u % VRS]

        def pbz(v):
            v.memset(bass.AP(pbuf.tensor, pbuf.offset,
                             [list(pbuf.ap[0]), [384, 8], [1, 96]]), 0.0)
            v.memset(bass.AP(pbuf.tensor, pbuf.offset + 288,
                             [list(pbuf.ap[0]), [384, 8], [1, 96]]), 0.0)
            v.drain().then_inc(sv.h, 1)
        em.add("vector", pbz)
        sv.n += 1

        def emit_qkv(t):
            gws = G.snap("c_ln2t")

            def lnq(v, t=t, gws=gws):
                G.apply(v, gws)
                v.tensor_scalar(ln2t, X1[:, E * t:E * (t + 1)], nm2[:, t:t + 1],
                                rs2[:, t:t + 1], ALU.add, ALU.mult).then_inc(sv.h, 1)
            em.add("vector", lnq)
            sv.n += 1
            ln_done = sv.n

            for cc in range(EC):
                bi = 2 + cc % 3 if cc % 3 < 2 else 7
                bank = PS[bi]
                bslot = bank[:, :].bitcast(BF16)[:, 0:128]
                trslot = ("c_tr", bi)
                gwt = G.snap(trslot)

                def tr(te, bslot=bslot, cc=cc, ln_done=ln_done, gwt=gwt):
                    te.wait_ge(sv.h, ln_done)
                    G.apply(te, gwt)
                    te.transpose(bslot, ln2t[:, 128 * cc:128 * (cc + 1)],
                                 ident[:, :]).then_inc(stt.h, 1)
                em.add("tensor", tr)
                stt.n += 1
                trd = stt.n

                def ev(a, bslot=bslot, dst=ln2f[:, 128 * cc:128 * (cc + 1)], trd=trd):
                    a.wait_ge(stt.h, trd)
                    a.copy(dst, bslot).then_inc(sa.h, 1)
                em.add("scalar", ev)
                sa.n += 1
                G.set(trslot, sa, sa.n)
            lnf_done = (sa, sa.n)
            G.set("c_ln2t", stt, stt.n)

            for proj, WS, bias, scale in (("q", WSq, qbc, 0.125), ("k", WSk, kbc, None)):
                for oc in range(EC):
                    bank = 5 + (oc % 2)
                    gwp = G.snap(("c_qkv", bank))

                    def pmm(te, WS=WS, oc=oc, bank=bank, lnf=lnf_done, gwp=gwp,
                            wdn=w_done):
                        te.wait_ge(sw.h, wdn)
                        te.wait_ge(lnf[0].h, lnf[1])
                        G.apply(te, gwp)
                        for k in range(EC):
                            last = te.matmul(PS[bank][:, 0:128],
                                             WS[:, E * k + 128 * oc:E * k + 128 * (oc + 1)],
                                             ln2f[:, 128 * k:128 * (k + 1)],
                                             start=(k == 0), stop=(k == EC - 1))
                        last.then_inc(stt.h, 1)
                    em.add("tensor", pmm)
                    stt.n += 1
                    pdone = stt.n

                    if proj == "q":
                        dst = Qr[t % 2][:, 128 * oc:128 * (oc + 1)]
                        gwr = G.snap(("c_qr", t % 2))

                        def pev(v, dst=dst, oc=oc, bank=bank, pdone=pdone, gwr=gwr):
                            v.wait_ge(stt.h, pdone)
                            G.apply(v, gwr)
                            v.tensor_scalar(dst, PS[bank][:, 0:128], qbc[:, oc:oc + 1],
                                            0.125, ALU.add, ALU.mult).then_inc(sv.h, 1)
                    else:
                        dst = Kr[t % KRS][:, 192 * oc + 32:192 * oc + 160]
                        gwr = G.snap(("c_kr", t % KRS))

                        def pev(v, dst=dst, oc=oc, bank=bank, pdone=pdone, gwr=gwr):
                            v.wait_ge(stt.h, pdone)
                            G.apply(v, gwr)
                            v.tensor_scalar(dst, PS[bank][:, 0:128], kbc[:, oc:oc + 1],
                                            None, ALU.add).then_inc(sv.h, 1)
                    em.add("vector", pev)
                    sv.n += 1
                    G.set(("c_qkv", bank), sv, sv.n)
            for half in range(2):
                bank = 5 + (half % 2)
                gwp = G.snap(("c_qkv", bank))

                def vmm(te, half=half, bank=bank, lnf=lnf_done, gwp=gwp, wdn=w_done):
                    te.wait_ge(sw.h, wdn)
                    te.wait_ge(lnf[0].h, lnf[1])
                    G.apply(te, gwp)
                    for k in range(EC):
                        last = te.matmul(PS[bank][:, :],
                                         ln2f[:, 128 * k:128 * (k + 1)],
                                         WSv[:, E * k + 512 * half:E * k + 512 * (half + 1)],
                                         start=(k == 0), stop=(k == EC - 1))
                    last.then_inc(stt.h, 1)
                em.add("tensor", vmm)
                stt.n += 1
                vdone = stt.n
                dst = Vr[t % VRS][:, 512 * half:512 * (half + 1)]
                gwv = G.snap(("c_vr", t % VRS))

                def vev(v, dst=dst, bank=bank, half=half, vdone=vdone, gwv=gwv):
                    v.wait_ge(stt.h, vdone)
                    G.apply(v, gwv)
                    v.tensor_tensor(dst, PS[bank][:, :],
                                    vbrep[:, 512 * half:512 * (half + 1)],
                                    ALU.add).then_inc(sv.h, 1)
                em.add("vector", vev)
                sv.n += 1
                G.set(("c_qkv", bank), sv, sv.n)
            kv_done = sv.n

            def kedges(v, t=t, kv_done=kv_done):
                v.wait_ge(sv.h, kv_done)
                st_ = Kr[t % KRS]
                if t == 0:
                    v.memset(bass.AP(st_.tensor, st_.offset,
                                     [list(st_.ap[0]), [192, EC], [1, 32]]), 0.0)
                else:
                    sp = Kr[(t - 1) % KRS]
                    v.tensor_copy(
                        bass.AP(st_.tensor, st_.offset,
                                [list(st_.ap[0]), [192, EC], [1, 32]]),
                        bass.AP(sp.tensor, sp.offset + 128,
                                [list(sp.ap[0]), [192, EC], [1, 32]]))
                    v.tensor_copy(
                        bass.AP(sp.tensor, sp.offset + 160,
                                [list(sp.ap[0]), [192, EC], [1, 32]]),
                        bass.AP(st_.tensor, st_.offset + 32,
                                [list(st_.ap[0]), [192, EC], [1, 32]]))
                v.drain().then_inc(sv.h, 1)
            em.add("vector", kedges)
            sv.n += 1
            return sv.n

        def emit_kpad_end(t):
            def kz(v, t=t, w=sv.n):
                v.wait_ge(sv.h, w)
                st_ = Kr[t % KRS]
                v.memset(bass.AP(st_.tensor, st_.offset + 160,
                                 [list(st_.ap[0]), [192, EC], [1, 32]]), 0.0)
                v.drain().then_inc(sv.h, 1)
            em.add("vector", kz)
            sv.n += 1
            return sv.n

        def emit_att(t, k_ready):
            at = attb[t % 2]
            for b in range(2):
                for hp in range(4 * b, 4 * b + 4):
                    gws = G.snap(("c_s", 0), ("c_s", 1))

                    def smm(te, t=t, hp=hp, k_ready=k_ready, gws=gws):
                        te.wait_ge(sv.h, k_ready)
                        G.apply(te, gws)
                        for h01 in range(2):
                            p0 = 64 * h01
                            te.matmul(PS[h01][:, 0:192],
                                      Qr[t % 2][p0:p0 + 64, 128 * hp:128 * (hp + 1)],
                                      Kr[t % KRS][p0:p0 + 64, 192 * hp:192 * (hp + 1)],
                                      start=True, stop=True).then_inc(stt.h, 1)
                    em.add("tensor", smm)
                    stt.n += 2
                    sdone = stt.n

                    j = hp % 4
                    gwm = G.snap("c_pb" + str(j), "c_sm")

                    def msk(v, hp=hp, sdone=sdone, gwm=gwm):
                        v.wait_ge(stt.h, sdone)
                        G.apply(v, gwm)
                        v.tensor_mask_reduce(smask[:, 0:192], PS[0][:, 0:192],
                                             mst[:, :], men[:, :], 1.0, -1e30,
                                             ALU.max)
                        v.tensor_mask_reduce(smask[:, 192:384], PS[1][:, 0:192],
                                             mst[:, :], men[:, :], 1.0, -1e30,
                                             ALU.max).then_inc(sv.h, 1)
                    em.add("vector", msk)
                    sv.n += 1
                    mdone = sv.n
                    G.set(("c_s", 0), sv, sv.n)
                    G.set(("c_s", 1), sv, sv.n)

                    def ex(a, hp=hp, j=j, mdone=mdone):
                        a.wait_ge(sv.h, mdone)
                        a.activation(pbuf[:, 768 * j + 96:768 * j + 288],
                                     smask[:, 0:192], AF.Exp,
                                     accum_out=lsum[:, 2 * hp:2 * hp + 1])
                        a.activation(pbuf[:, 768 * j + 384 + 96:768 * j + 384 + 288],
                                     smask[:, 192:384], AF.Exp,
                                     accum_out=lsum[:, 2 * hp + 1:2 * hp + 2]
                                     ).then_inc(sa.h, 1)
                    em.add("scalar", ex)
                    sa.n += 1
                    G.set("c_sm", sa, sa.n)
                exp_done = sa.n

                def rc(v, b=b, exp_done=exp_done):
                    v.wait_ge(sa.h, exp_done)
                    v.drain()
                    v.reciprocal(rsum[:, 8 * b:8 * b + 8], lsum[:, 8 * b:8 * b + 8])
                    v.drain().then_inc(sv.h, 1)
                em.add("vector", rc)
                sv.n += 1
                rc_done = sv.n

                for hp in range(4 * b, 4 * b + 4):
                    j = hp % 4

                    def pm(v, hp=hp, j=j, rc_done=rc_done):
                        v.wait_ge(sv.h, rc_done)
                        v.tensor_scalar(pbuf[:, 768 * j + 96:768 * j + 288],
                                        pbuf[:, 768 * j + 96:768 * j + 288],
                                        rsum[:, 2 * hp:2 * hp + 1], None, ALU.mult)
                        v.tensor_scalar(pbuf[:, 768 * j + 384 + 96:768 * j + 384 + 288],
                                        pbuf[:, 768 * j + 384 + 96:768 * j + 384 + 288],
                                        rsum[:, 2 * hp + 1:2 * hp + 2], None,
                                        ALU.mult).then_inc(sv.h, 1)
                    em.add("vector", pm)
                    sv.n += 1
                    pm_done = sv.n

                    for i in range(6):
                        bi = 2 if i % 3 == 0 else (3 if i % 3 == 1 else 7)
                        bank = PS[bi]
                        bkey = ("c_tr", bi)
                        bslot = bank[:, :].bitcast(BF16)[:, 0:128]
                        gwt = G.snap(bkey)

                        def trp(te, j=j, i=i, bslot=bslot, pm_done=pm_done, gwt=gwt):
                            te.wait_ge(sv.h, pm_done)
                            G.apply(te, gwt)
                            te.transpose(bslot, pbuf[:, 768 * j + 128 * i:768 * j + 128 * (i + 1)],
                                         ident[:, :]).then_inc(stt.h, 1)
                        em.add("tensor", trp)
                        stt.n += 1
                        trd = stt.n

                        def evp(a, i=i, bslot=bslot, trd=trd):
                            a.wait_ge(stt.h, trd)
                            a.copy(ptr[:, 128 * i:128 * (i + 1)], bslot).then_inc(sa.h, 1)
                        em.add("scalar", evp)
                        sa.n += 1
                        G.set(bkey, sa, sa.n)
                    ptr_done = (sa, sa.n)

                    gwa = G.snap("c_av")

                    def avm(te, t=t, hp=hp, ptr_done=ptr_done, gwa=gwa):
                        te.wait_ge(ptr_done[0].h, ptr_done[1])
                        G.apply(te, gwa)
                        for h01 in range(2):
                            hh = 2 * hp + h01
                            for i in range(3):
                                vs_ = vslot(t - 1 + i)
                                last = te.matmul(PS[4][64 * h01:64 * h01 + 64, 0:128],
                                                 vs_[:, 64 * hh:64 * hh + 64],
                                                 ptr[:, 384 * h01 + 128 * i:384 * h01 + 128 * (i + 1)],
                                                 start=(i == 0), stop=(i == 2))
                        last.then_inc(stt.h, 1)
                    em.add("tensor", avm)
                    stt.n += 1
                    av_done = stt.n
                    G.set(("c_pb" + str(j)), stt, stt.n)

                    def ave(a, hp=hp, t=t, av_done=av_done):
                        a.wait_ge(stt.h, av_done)
                        a.copy(attb[t % 2][:, 128 * hp:128 * (hp + 1)],
                               PS[4][:, 0:128]).then_inc(sa.h, 1)
                    em.add("scalar", ave)
                    sa.n += 1
                    G.set("c_av", sa, sa.n)
            att_done = sa.n
            G.set(("c_qr", t % 2), stt, stt.n)
            G.set(("c_kr", t % KRS), stt, stt.n)
            for u in (t - 1, t, t + 1):
                if 0 <= u < NT:
                    G.set(("c_vr", u % VRS), stt, stt.n)

            for half in range(2):
                bank = 5 + half
                gwo = G.snap(("c_qkv", bank))

                def omm(te, t=t, half=half, bank=bank, att_done=att_done, gwo=gwo):
                    te.wait_ge(sa.h, att_done)
                    G.apply(te, gwo)
                    for k in range(EC):
                        last = te.matmul(PS[bank][:, :],
                                         attb[t % 2][:, 128 * k:128 * (k + 1)],
                                         WSo[:, E * k + 512 * half:E * k + 512 * (half + 1)],
                                         start=(k == 0), stop=(k == EC - 1))
                    last.then_inc(stt.h, 1)
                em.add("tensor", omm)
                stt.n += 1
                odone = stt.n

                def oev(v, t=t, half=half, bank=bank, odone=odone):
                    v.wait_ge(stt.h, odone)
                    xs = X1[:, E * t + 512 * half:E * t + 512 * (half + 1)]
                    v.scalar_tensor_tensor(xs, PS[bank][:, :], 1.0, xs,
                                           ALU.mult, ALU.add)
                    v.scalar_tensor_tensor(xs, obrep[:, 512 * half:512 * (half + 1)],
                                           1.0, xs, ALU.mult, ALU.add).then_inc(sv.h, 1)
                em.add("vector", oev)
                sv.n += 1
                G.set(("c_qkv", bank), sv, sv.n)
            x2_done = sv.n

            def st3(a, t=t, x2_done=x2_done):
                a.wait_ge(sv.h, x2_done)
                a.activation(cjunk, X1[:, E * t:E * (t + 1)], AF.Identity,
                             accum_out=stats[:, 64 + t:65 + t])
                a.activation(cjunk, X1[:, E * t:E * (t + 1)], AF.Square,
                             accum_out=stats[:, 80 + t:81 + t]).then_inc(sa.h, 1)
            em.add("scalar", st3)
            sa.n += 1

        import os as _os3
        _natt = int(_os3.environ.get("DBG_ATT_TILES", str(NT)))
        k_ready = {}
        for t in range(_natt):
            k_ready[t - 1] = emit_qkv(t)
            if t >= 1:
                emit_att(t - 1, k_ready[t - 1])
        if _natt < NT:
            return
        k_ready[NT - 1] = emit_kpad_end(NT - 1)
        emit_att(NT - 1, k_ready[NT - 1])

        def sfin(a):
            a.drain()
            a.activation(cjunk[:, 0:1], stats[:, 64:65], AF.Identity).then_inc(sa.h, 1)
        em.add("scalar", sfin)
        sa.n += 1
        emit_stats_finish(2, sa, sa.n)

        def wv(v):
            v.wait_ge(sv.h, sv.n)
        em.add("vector", wv)
        em.flush()

    if DEBUG_STAGE == -2:
        x3d = x_in.rearrange("(a p) e -> a p e", p=128)
        y3d = y_out.rearrange("(a p) e -> a p e", p=128)

        def mv(s):
            s.dma_start(out=X1[:, 0:E], in_=x3d[0, :, :]).then_inc(sx.h, 16)
        em.add("sync", mv)
        sx.n += 16
        n0 = sx.n

        def cp(v):
            v.wait_ge(sx.h, n0)
            v.tensor_scalar_mul(X1[:, E:2 * E], X1[:, 0:E], 2.0).then_inc(sv.h, 1)
        em.add("vector", cp)
        sv.n += 1
        em.flush()
        n1 = sv.n

        def out(g):
            g.wait_ge(sv.h, n1)
            g.dma_start(out=y3d[0, :, :], in_=X1[:, E:2 * E]).then_inc(sy.h, 16)
            g.wait_ge(sy.h, 16)
        em.add("gpsimd", out)
        sy.n += 16
        em.flush()
        es.close()
        mybir.codegen_inst_isa_subclasses(nc)
        return nc
    import os as _os2
    _tap = _os2.environ.get("DBG_TAP", "")
    emit_preproc()
    if _tap == "pre":
        emit_pre(0)
        y3d = y_out.rearrange("(a p) e -> a p e", p=128)
        n0 = sv.n + sa.n

        def dmp(g, wv=sv.n, wa=sa.n):
            g.wait_ge(sv.h, wv)
            g.wait_ge(sa.h, wa)
            g.dma_start(out=y3d[0, :, 0:96], in_=stats[:, :]).then_inc(sy.h, 16)
            g.dma_start(out=y3d[0, :, 96:144], in_=negmb[:, :]).then_inc(sy.h, 16)
            g.dma_start(out=y3d[0, :, 144:192], in_=rstdb[:, :]).then_inc(sy.h, 16)
            g.dma_start(out=y3d[1, :, :], in_=X1[:, 0:E]).then_inc(sy.h, 16)
            g.wait_ge(sy.h, 64)
        em.add("gpsimd", dmp)
        sy.n += 64
        em.flush()
        es.close()
        mybir.codegen_inst_isa_subclasses(nc)
        return nc
    if DEBUG_STAGE == -1:
        em.flush()
        es.close()
        mybir.codegen_inst_isa_subclasses(nc)
        return nc
    if DEBUG_STAGE == 0:
        dbg = WKA[:, 0:1408].bitcast(BF16)
        dbgf = WSB[:, 0:2816]
        y3d = y_out.rearrange("(a p) e -> a p e", p=128)
        base_sy = sy.n

        def dget(s, w=sg.n):
            s.wait_ge(sg.h, w)
            s.dma_start(out=dbg[:, 0:2816], in_=wgT_d[0:128, :]).then_inc(sx.h, 16)
        em.add("sync", dget)
        sx.n += 16
        n0 = sx.n

        def dcast(v, n0=n0):
            v.wait_ge(sx.h, n0)
            v.tensor_copy(dbgf[:, 0:2816], dbg[:, 0:2816]).then_inc(sv.h, 1)
        em.add("vector", dcast)
        sv.n += 1
        n1 = sv.n

        def dout(g, n1=n1):
            g.wait_ge(sv.h, n1)
            for j in range(2):
                g.dma_start(out=y3d[j, :, :], in_=dbgf[:, 1024 * j:1024 * (j + 1)]).then_inc(sy.h, 16)
        em.add("gpsimd", dout)
        sy.n += 32
        n2 = sy.n

        def dget2(s, n2=n2):
            s.wait_ge(sy.h, n2)
            s.dma_start(out=dbg[:, 0:1024], in_=wdT_d[0:128, :]).then_inc(sx.h, 16)
        em.add("sync", dget2)
        sx.n += 16
        n3 = sx.n

        def dcast2(v, n3=n3):
            v.wait_ge(sx.h, n3)
            v.tensor_copy(dbgf[:, 0:1024], dbg[:, 0:1024]).then_inc(sv.h, 1)
        em.add("vector", dcast2)
        sv.n += 1
        n4 = sv.n
        fin_sy = sy.n + 16

        def dout2(g, n4=n4, fin_sy=fin_sy):
            g.wait_ge(sv.h, n4)
            g.dma_start(out=y3d[4, :, :], in_=dbgf[:, 0:1024]).then_inc(sy.h, 16)
            g.wait_ge(sy.h, fin_sy)
        em.add("gpsimd", dout2)
        sy.n += 16
        em.flush()
        es.close()
        mybir.codegen_inst_isa_subclasses(nc)
        return nc
    for seq in range(NSEQ):
        emit_pre(seq)
        xqf_ready.clear()
        emit_ffn(seq, 0, 0, False)
        emit_ffn(seq, 1, 0, DEBUG_STAGE == 1)
        if DEBUG_STAGE >= 2:
            emit_attn(seq)
            if DEBUG_STAGE >= 3:
                emit_ffn(seq, 0, 2, False)
                emit_ffn(seq, 1, 2, True)
            if DEBUG_STAGE == 2:
                y3d = y_out.rearrange("(a p) e -> a p e", p=128)
                wsv = sv.n

                def dx2(g, seq=seq, wsv=wsv):
                    g.wait_ge(sv.h, wsv)
                    for t in range(NT):
                        g.dma_start(out=y3d[seq * NT + t, :, :],
                                    in_=X1[:, E * t:E * (t + 1)]).then_inc(sy.h, 16)
                em.add("gpsimd", dx2)
                sy.n += NT * 16

                def wgf(g):
                    g.wait_ge(sy.h, sy.n)
                em.add("gpsimd", wgf)
                em.flush()

    es.close()
    mybir.codegen_inst_isa_subclasses(nc)
    return nc


_PROG = None


def kernel(**inputs):
    global _PROG
    x = np.ascontiguousarray(np.asarray(inputs["L0_attn_hidden_states"], dtype=np.float32))
    ident = np.eye(128, dtype=BF)
    mstart = np.arange(128, dtype=np.float32).reshape(128, 1)
    mend = mstart + 65.0

    if _PROG is None:
        _PROG = build_program()
    nc = _PROG

    base = {
        "wg": np.ascontiguousarray(np.asarray(inputs["L0_ffn_w_gate"], dtype=np.float32)),
        "wu": np.ascontiguousarray(np.asarray(inputs["L0_ffn_w_up"], dtype=np.float32)),
        "wd": np.ascontiguousarray(np.asarray(inputs["L0_ffn_w_down"], dtype=np.float32)),
        "qw": np.ascontiguousarray(np.asarray(inputs["L0_attn_q_w"], dtype=np.float32)),
        "kw": np.ascontiguousarray(np.asarray(inputs["L0_attn_k_w"], dtype=np.float32)),
        "vw": np.ascontiguousarray(np.asarray(inputs["L0_attn_v_w"], dtype=np.float32)),
        "ow": np.ascontiguousarray(np.asarray(inputs["L0_attn_o_w"], dtype=np.float32)),
        "qb": np.ascontiguousarray(np.asarray(inputs["L0_attn_q_b"], dtype=np.float32)),
        "kb": np.ascontiguousarray(np.asarray(inputs["L0_attn_k_b"], dtype=np.float32)),
        "vb": np.ascontiguousarray(np.asarray(inputs["L0_attn_v_b"], dtype=np.float32)),
        "ob": np.ascontiguousarray(np.asarray(inputs["L0_attn_o_b"], dtype=np.float32)),
        "ident": ident, "mstart": mstart, "mend": mend,
    }
    in_maps = []
    for c in range(NCORES):
        m = dict(base)
        m["x"] = np.ascontiguousarray(x[c * NSEQ:(c + 1) * NSEQ].reshape(NSEQ * SEQ, E))
        in_maps.append(m)
    res = run_bass_kernel_spmd(nc, in_maps, core_ids=list(range(NCORES)))
    out = np.concatenate([r["y"].reshape(NSEQ, SEQ, E) for r in res.results], axis=0)
    return out


# version 03
# speedup vs baseline: 1.2007x; 1.0003x over previous
import contextlib
import numpy as np
import ml_dtypes

import concourse.bass as bass
import concourse.mybir as mybir
from concourse.bass_utils import run_bass_kernel_spmd
from concourse import dve_ops as dvo
from concourse.dve_op import DveOp
from concourse.dve_spec import Spec, Src0, C0, C1, Zero, maxx, minn, lower
from concourse.dve_table_gen import dve_ver_for

F32 = mybir.dt.float32
BF16 = mybir.dt.bfloat16
ALU = mybir.AluOpType
AF = mybir.ActivationFunctionType
AX = mybir.AxisListType
BF = ml_dtypes.bfloat16

MAGIC = float(np.float32(1.5 * 2 ** 23))
DEKC = float(np.float32(2 ** 22 + 1))

NCORES = 8
B = 16
SEQ = 2048
E = 1024
EC = 8
I = 2816
NS = 2
IH = I // NS
ICH = IH // 128
NT = SEQ // 128
NSEQ = B // NCORES
NHEAD = 16
HD = 64
EPS = 1e-5

DEBUG_STAGE = 3


def _refA(in0, in1, s0, s1, imm2):
    f = np.float64 if in0.dtype == np.float64 else np.float32
    r1 = ((in0.astype(f) + np.array(s1, f)) - np.array(s1, f)).astype(f)
    return np.clip(r1, -s0, s0).astype(f)


def _refB(in0, in1, s0, s1, imm2):
    an32 = in0.astype(np.float32)
    w = (an32 * np.float32(s1)).astype(np.float32)
    t = (w - an32).astype(np.float32)
    D = (w - t).astype(in0.dtype if in0.dtype == np.float64 else np.float32)
    return (D - np.clip(D, -s0, s0)).astype(D.dtype)


def register_ops():
    ver = dve_ver_for("TRN2")
    out = []
    _r1 = (Src0 + C1) - C1
    specA = Spec(body=maxx(minn(_r1, C0), Zero - C0))
    _w = Src0 * C1
    _D = _w - (_w - Src0)
    specB = Spec(body=_D - maxx(minn(_D, C0), Zero - C0))
    for name, spec, ref in [("ANT_FP4_A", specA, _refA), ("ANT_FP4_B", specB, _refB)]:
        ex = next((o for o in dvo.OPS if o.name == name), None)
        if ex is not None:
            out.append(ex)
            continue
        compiled = lower(spec, ver=ver, perf_en=False, name=name)
        op = DveOp(name, spec, subdim=False, uops_sha={ver: compiled.sha(ver)},
                   reference=ref)
        row = dvo._CUSTOM_DVE_ROW_BASE + len(dvo.OPS)
        assert row < 0x20
        dvo.OPS.append(op)
        dvo._SUB_OPCODE_FOR_NAME[name] = row
        dvo.CUSTOM_DVE_SPECS[name] = spec
        dvo.CUSTOM_DVE_REFS[name] = ref
        out.append(op)
    return out


OPA, OPB = register_ops()


class Sem:
    def __init__(self, h):
        self.h = h
        self.n = 0

    def inc(self, inst, k=1):
        inst.then_inc(self.h, k)
        self.n += k
        return self.n


class Emitter:
    ENGINES = ("sync", "tensor", "vector", "scalar", "gpsimd")

    def __init__(self, nc):
        self.nc = nc
        self.new_phase()

    def new_phase(self):
        self.ops = {e: [] for e in self.ENGINES}

    def add(self, eng, fn):
        self.ops[eng].append(fn)

    def flush(self):
        ops = self.ops
        with self.nc.Block() as block:
            @block.sync
            def _(eng, ops=ops):
                for f in ops["sync"]:
                    f(eng)

            @block.tensor
            def _(eng, ops=ops):
                for f in ops["tensor"]:
                    f(eng)

            @block.vector
            def _(eng, ops=ops):
                for f in ops["vector"]:
                    f(eng)

            @block.scalar
            def _(eng, ops=ops):
                for f in ops["scalar"]:
                    f(eng)

            @block.gpsimd
            def _(eng, ops=ops):
                for f in ops["gpsimd"]:
                    f(eng)
        self.new_phase()


class Carver:
    def __init__(self, pools):
        self.pools = pools
        self.pos = [0] * len(pools)

    def f32(self, cols):
        for i, (t, n) in enumerate(self.pools):
            if self.pos[i] + cols <= n:
                off = self.pos[i]
                self.pos[i] += cols
                return t[:, off:off + cols]
        raise RuntimeError(f"carver OOM: need {cols}, pos={self.pos}")

    def bf16(self, cols):
        assert cols % 2 == 0
        return self.f32(cols // 2).bitcast(BF16)


def bcast16(ap, nblocks):
    return bass.AP(ap.tensor, ap.offset,
                   [list(ap.ap[0]), [list(ap.ap[1])[0], nblocks], [0, 16]])


class Guards:

    def __init__(self):
        self.g = {}

    def set(self, key, sem, n):
        self.g[key] = (sem, n)

    def snap(self, *keys):
        best = {}
        for k in keys:
            if k in self.g:
                s, n = self.g[k]
                if best.get(id(s), (None, -1))[1] < n:
                    best[id(s)] = (s, n)
        return tuple(best.values())

    @staticmethod
    def apply(eng, snap):
        for s, n in snap:
            if n > 0:
                eng.wait_ge(s.h, n)


def build_program():
    import os as _os
    nc = bass.Bass("TRN2", target_bir_lowering=False,
                   detect_race_conditions=not _os.environ.get("NORACE"))

    x_in = nc.dram_tensor("x", [NSEQ * SEQ, E], F32, kind="ExternalInput").ap()
    y_out = nc.dram_tensor("y", [NSEQ * SEQ, E], F32, kind="ExternalOutput").ap()
    wg_in = nc.dram_tensor("wg", [I, E], F32, kind="ExternalInput").ap()
    wu_in = nc.dram_tensor("wu", [I, E], F32, kind="ExternalInput").ap()
    wd_in = nc.dram_tensor("wd", [E, I], F32, kind="ExternalInput").ap()
    qw_in = nc.dram_tensor("qw", [E, E], F32, kind="ExternalInput").ap()
    kw_in = nc.dram_tensor("kw", [E, E], F32, kind="ExternalInput").ap()
    vw_in = nc.dram_tensor("vw", [E, E], F32, kind="ExternalInput").ap()
    ow_in = nc.dram_tensor("ow", [E, E], F32, kind="ExternalInput").ap()
    qb_in = nc.dram_tensor("qb", [E], F32, kind="ExternalInput").ap()
    kb_in = nc.dram_tensor("kb", [E], F32, kind="ExternalInput").ap()
    vb_in = nc.dram_tensor("vb", [E], F32, kind="ExternalInput").ap()
    ob_in = nc.dram_tensor("ob", [E], F32, kind="ExternalInput").ap()
    ident_in = nc.dram_tensor("ident", [128, 128], BF16, kind="ExternalInput").ap()
    mst_in = nc.dram_tensor("mstart", [128, 1], F32, kind="ExternalInput").ap()
    men_in = nc.dram_tensor("mend", [128, 1], F32, kind="ExternalInput").ap()

    wgT_d = nc.dram_tensor("wgT", [EC * 128, I], BF16, kind="Internal").ap()
    wuT_d = nc.dram_tensor("wuT", [EC * 128, I], BF16, kind="Internal").ap()
    wdT_d = nc.dram_tensor("wdT", [I, E], BF16, kind="Internal").ap()
    qwT_d = nc.dram_tensor("qwT", [EC * 128, E], BF16, kind="Internal").ap()
    kwT_d = nc.dram_tensor("kwT", [EC * 128, E], BF16, kind="Internal").ap()
    vwT_d = nc.dram_tensor("vwT", [EC * 128, E], BF16, kind="Internal").ap()
    owT_d = nc.dram_tensor("owT", [EC * 128, E], BF16, kind="Internal").ap()
    wT_scratch = {id(wg_in): wgT_d, id(wu_in): wuT_d, id(wd_in): wdT_d,
                  id(qw_in): qwT_d, id(kw_in): kwT_d, id(vw_in): vwT_d,
                  id(ow_in): owT_d}

    es = contextlib.ExitStack()

    def sbt(name, shape, dtype):
        return es.enter_context(nc.sbuf_tensor(name, shape, dtype))

    ident = sbt("identt", [128, 128], BF16)
    mst = sbt("mstt", [128, 1], F32)
    men = sbt("ment", [128, 1], F32)
    qbc = sbt("qbc", [128, EC], F32)
    kbc = sbt("kbc", [128, EC], F32)
    vbrep = sbt("vbrep", [128, E], F32)
    obrep = sbt("obrep", [128, E], F32)
    zeroV = sbt("zeroV", [128, E], BF16)
    stats = sbt("stats", [128, 96], F32)
    negmb = sbt("negmb", [128, 48], F32)
    rstdb = sbt("rstdb", [128, 48], F32)
    tinyw = sbt("tinyw", [128, 16], F32)

    X1 = sbt("X1", [128, NT * E], F32)
    XQF = sbt("XQF", [128, NT * E // 2], F32)
    WSB = sbt("WSB", [128, 12288], F32)
    WKA = sbt("WKA", [128, 12288], F32)
    junk = WKA[:, 0:512].bitcast(BF16)
    brow = None

    PS = [es.enter_context(nc.psum_tensor(f"ps{i}", [128, 512], F32))
          for i in range(8)]

    sx = Sem(es.enter_context(nc.semaphore("sx")))
    sw = Sem(es.enter_context(nc.semaphore("sw")))
    sv = Sem(es.enter_context(nc.semaphore("sv")))
    sa = Sem(es.enter_context(nc.semaphore("sa")))
    stt = Sem(es.enter_context(nc.semaphore("stt")))
    sg = Sem(es.enter_context(nc.semaphore("sg")))
    sy = Sem(es.enter_context(nc.semaphore("sy")))

    em = Emitter(nc)
    G = Guards()

    def emit_quant(vec_ops, src, n, scl, q2l, hi, q_out, inc=None):
        nb = n // 16
        bm, rec, bs = scl[:, 0:nb], scl[:, nb:2 * nb], scl[:, 2 * nb:3 * nb]

        def f(v, src=src, nb=nb, bm=bm, rec=rec, bs=bs, q2l=q2l, hi=hi,
              q_out=q_out, inc=inc):
            s3 = src.rearrange("p (s x) -> p s x", x=16)
            v.tensor_reduce(bm, s3, AX.X, ALU.max, apply_absolute_value=True)
            v.tensor_scalar_max(bm, bm, 6e-12)
            v.reciprocal(rec, bm)
            v.drain()
            v.tensor_scalar_mul(rec, rec, 12.0)
            v.tensor_scalar_mul(bs, bm, float(np.float32(1.0 / 12.0)))
            v.tensor_tensor(s3, s3, bcast16(rec, nb), ALU.mult)
            v._custom_dve(OPA, out=q2l, in0=src, s0=4.0, s1=MAGIC)
            v._custom_dve(OPB, out=hi, in0=src, s0=4.0, s1=DEKC)
            v.tensor_add(q2l, q2l, hi)
            last = v.tensor_tensor(q_out.rearrange("p (s x) -> p s x", x=16),
                                   q2l.rearrange("p (s x) -> p s x", x=16),
                                   bcast16(bs, nb), ALU.mult)
            if inc is not None:
                last.then_inc(inc.h, 1)
        vec_ops.append(f)

    def emit_stats_finish(slot, wait_sem, wait_n):
        sc = 32 * slot
        nsl = 16 * slot

        def fin(v):
            v.wait_ge(wait_sem.h, wait_n)
            v.tensor_scalar_mul(negmb[:, nsl:nsl + 16], stats[:, sc:sc + 16], -1.0 / E)
            v.drain()
            v.tensor_mul(tinyw[:, 0:16], negmb[:, nsl:nsl + 16], negmb[:, nsl:nsl + 16])
            v.tensor_scalar(rstdb[:, nsl:nsl + 16], stats[:, sc + 16:sc + 32],
                            1.0 / E, EPS, ALU.mult, ALU.add)
            v.drain()
            v.tensor_sub(rstdb[:, nsl:nsl + 16], rstdb[:, nsl:nsl + 16],
                         tinyw[:, 0:16]).then_inc(sv.h, 1)
        em.add("vector", fin)
        sv.n += 1
        var_done = sv.n

        def sq(a):
            a.wait_ge(sv.h, var_done)
            a.activation(tinyw[:, 0:16], rstdb[:, nsl:nsl + 16], AF.Ln)
            a.drain()
            a.activation(rstdb[:, nsl:nsl + 16], tinyw[:, 0:16], AF.Exp,
                         scale=-0.5).then_inc(sa.h, 1)
        em.add("scalar", sq)
        sa.n += 1
        return sa.n

    def emit_preproc():
        C = Carver([(WKA, 12288), (WSB, 12288)])
        browt = C.f32(E)
        brow = browt

        import os
        _dbg_skip = set(os.environ.get("DBGSKIP", "").split(","))
        _loadlist = []
        if "ident" not in _dbg_skip:
            _loadlist.append(lambda s: s.dma_start(out=ident[:, :], in_=ident_in))
        if "mst" not in _dbg_skip:
            _loadlist.append(lambda s: s.dma_start(out=mst[:, :], in_=mst_in))
            _loadlist.append(lambda s: s.dma_start(out=men[:, :], in_=men_in))
        if "qbc" not in _dbg_skip:
            _loadlist.append(lambda s: s.dma_start(out=qbc[:, :], in_=qb_in.rearrange("(c p) -> p c", p=128), allow_slow_non_contiguous=True))
            _loadlist.append(lambda s: s.dma_start(out=kbc[:, :], in_=kb_in.rearrange("(c p) -> p c", p=128), allow_slow_non_contiguous=True))
        if "rep" not in _dbg_skip:
            _loadlist.append(lambda s: s.dma_start(out=vbrep[:, :], in_=bass.AP(vb_in.tensor, 0, [[0, 128], [1, E]])))
            _loadlist.append(lambda s: s.dma_start(out=obrep[:, :], in_=bass.AP(ob_in.tensor, 0, [[0, 128], [1, E]])))

        def loads(s):
            for f in _loadlist:
                f(s).then_inc(sx.h, 16)
        em.add("sync", loads)
        sx.n += len(_loadlist) * 16

        def zv(v):
            v.memset(zeroV[:, :], 0.0).then_inc(sv.h, 1)
        em.add("vector", zv)
        sv.n += 1

        wtmp = [C.f32(I), C.f32(I)]
        wq = [C.bf16(I), C.bf16(I)]
        wscl = C.f32(3 * (I // 16) + 16)
        wq2l = C.bf16(I)
        whi = C.bf16(I)
        stg = [C.bf16(512), C.bf16(512)]

        jobs = [(wg_in, I, E, True), (wu_in, I, E, True), (wd_in, E, I, True),
                (qw_in, E, E, False), (kw_in, E, E, False),
                (vw_in, E, E, False), (ow_in, E, E, False)]
        chunks = []
        for (din, rows, cols, doq) in jobs:
            for r in range(rows // 128):
                chunks.append((din, cols, doq, r))
        import os
        _lim = int(os.environ.get("DBG_NCHUNKS", "999"))
        chunks = chunks[:_lim]

        for idx, (din, cols, doq, r) in enumerate(chunks):
            rot = idx % 2
            wt = wtmp[rot][:, 0:cols]
            wqb = wq[rot][:, 0:cols]
            dout = wT_scratch[id(din)]

            gw = G.snap(("pp_wt", rot))

            def ld(s, din=din, cols=cols, r=r, wt=wt, gw=gw):
                G.apply(s, gw)
                s.dma_start(out=wt, in_=din[128 * r:128 * (r + 1), 0:cols]).then_inc(sx.h, 16)
            em.add("sync", ld)
            sx.n += 16
            x_after = sx.n

            vec_ops = []

            gw2 = G.snap(("pp_wq", rot))

            def wv(v, x_after=x_after, gw2=gw2):
                v.wait_ge(sx.h, x_after)
                G.apply(v, gw2)
            vec_ops.append(wv)
            if doq:
                emit_quant(vec_ops, wt, cols, wscl, wq2l[:, 0:cols], whi[:, 0:cols],
                           wqb, inc=sv)
            else:
                vec_ops.append(lambda v, wt=wt, wqb=wqb:
                               v.tensor_copy(wqb, wt).then_inc(sv.h, 1))
            for f in vec_ops:
                em.add("vector", f)
            sv.n += 1
            q_done = sv.n
            G.set(("pp_wt", rot), sv, q_done)

            ncc = cols // 128
            for cc in range(ncc):
                slot = cc % 4
                bank = PS[4 + slot]
                bslot = bank[:, :].bitcast(BF16)[:, 0:128]

                gw3 = G.snap(("pp_ps", slot))

                def tr(t, wqb=wqb, cc=cc, bslot=bslot, q_done=q_done, gw3=gw3):
                    t.wait_ge(sv.h, q_done)
                    G.apply(t, gw3)
                    t.transpose(bslot, wqb[:, 128 * cc:128 * (cc + 1)],
                                ident[:, :]).then_inc(stt.h, 1)
                em.add("tensor", tr)
                stt.n += 1
                trd = stt.n

                srot = (cc // 4) % 2
                sslot = stg[srot][:, 128 * (cc % 4):128 * (cc % 4) + 128]

                gw4 = G.snap(("pp_stg", srot, cc % 4))

                def ev(a, bslot=bslot, sslot=sslot, trd=trd, gw4=gw4):
                    a.wait_ge(stt.h, trd)
                    G.apply(a, gw4)
                    a.copy(sslot, bslot).then_inc(sa.h, 1)
                em.add("scalar", ev)
                sa.n += 1
                G.set(("pp_ps", slot), sa, sa.n)
                ev_done = sa.n

                def out(g, dout=dout, cc=cc, r=r, sslot=sslot, ev_done=ev_done):
                    g.wait_ge(sa.h, ev_done)
                    g.dma_start(out=dout[128 * cc:128 * (cc + 1), 128 * r:128 * (r + 1)],
                                in_=sslot).then_inc(sg.h, 16)
                em.add("gpsimd", out)
                sg.n += 16
                G.set(("pp_stg", srot, cc % 4), sg, sg.n)
            G.set(("pp_wq", rot), stt, stt.n)

        def fin(g):
            g.wait_ge(sg.h, sg.n)
        em.add("gpsimd", fin)
        em.flush()

    def emit_pre(seq):
        x3d = x_in.rearrange("(a p) e -> a p e", p=128)

        def ld(s, seq=seq):
            for t in range(NT):
                s.dma_start(out=X1[:, E * t:E * (t + 1)],
                            in_=x3d[seq * NT + t, :, :]).then_inc(sx.h, 16)
        em.add("sync", ld)
        sx.n += NT * 16
        x_done = sx.n

        def acts(a, x_done=x_done):
            for t in range(NT):
                a.wait_ge(sx.h, x_done - (NT - 1 - t) * 16)
                a.activation(junk[:, :], X1[:, E * t:E * (t + 1)], AF.Identity,
                             accum_out=stats[:, t:t + 1])
                a.activation(junk[:, :], X1[:, E * t:E * (t + 1)], AF.Square,
                             accum_out=stats[:, 16 + t:17 + t])
            a.drain()
            a.activation(junk[:, 0:1], stats[:, 0:1], AF.Identity).then_inc(sa.h, 1)
        em.add("scalar", acts)
        sa.n += 1
        emit_stats_finish(0, sa, sa.n)
        em.flush()

    xqf_ready = {}

    def emit_ffn(seq, sl, ln_slot, write_y):
        C = Carver([(WKA, 12288)])
        lnt = C.f32(E)
        xscl = C.f32(3 * (E // 16))
        q2l = C.bf16(E)
        xqtm = C.bf16(E)
        gsil = C.f32(IH)
        hfull = C.f32(IH)
        hscl = C.f32(3 * (IH // 16))
        q2lh = C.bf16(IH)
        hih = C.bf16(IH)
        hqf = [C.bf16(ICH * 128) for _ in range(4)]
        wdb = [C.bf16(E), C.bf16(E)]

        WG = WSB[:, 0:EC * IH // 2].bitcast(BF16)
        WU = WSB[:, EC * IH // 2:EC * IH].bitcast(BF16)
        nm = negmb[:, 16 * ln_slot:16 * ln_slot + 16]
        rs = rstdb[:, 16 * ln_slot:16 * ln_slot + 16]

        nchunks = [(0, 512), (512, 512), (1024, 384)]

        w_base = sw.n

        def wld(s, sl=sl):
            for coff, clen in nchunks:
                for W, wT in ((WG, wgT_d), (WU, wuT_d)):
                    for k in range(EC):
                        s.dma_start(out=W[:, IH * k + coff:IH * k + coff + clen],
                                    in_=wT[128 * k:128 * (k + 1),
                                           IH * sl + coff:IH * sl + coff + clen]
                                    ).then_inc(sw.h, 16)
        em.add("sync", wld)
        sw.n += len(nchunks) * 2 * EC * 16
        wg_need = [w_base + 16 * (16 * ci + 8) for ci in range(len(nchunks))]
        wu_need = [w_base + 16 * (16 * ci + 16) for ci in range(len(nchunks))]
        hq_ready = {}
        x1_done = {}

        for g in range(4):
            tiles = [4 * g + j for j in range(4)]
            for tt_i, t in enumerate(tiles):
                xt = X1[:, E * t:E * (t + 1)]
                xqf_t = XQF[:, (E // 2) * t:(E // 2) * (t + 1)].bitcast(BF16)

                if sl == 0:
                    vec_ops = []

                    gwl = G.snap("lnt")

                    def lnq(v, xt=xt, t=t, gwl=gwl):
                        G.apply(v, gwl)
                        v.tensor_scalar(lnt, xt, nm[:, t:t + 1], rs[:, t:t + 1],
                                        ALU.add, ALU.mult)
                    vec_ops.append(lnq)
                    emit_quant(vec_ops, lnt, E, xscl, q2l, xqtm, xqtm, inc=sv)
                    for f in vec_ops:
                        em.add("vector", f)
                    sv.n += 1
                    xq_done = sv.n
                    G.set("lnt", sv, xq_done)

                    for cc in range(EC):
                        slot = cc % 4
                        bank = PS[4 + slot]
                        bslot = bank[:, :].bitcast(BF16)[:, 0:128]

                        gwt = G.snap(("f_ps", slot), "banks_all")

                        def tr(te, bslot=bslot, cc=cc, xq_done=xq_done, gwt=gwt):
                            te.wait_ge(sv.h, xq_done)
                            G.apply(te, gwt)
                            te.transpose(bslot, xqtm[:, 128 * cc:128 * (cc + 1)],
                                         ident[:, :]).then_inc(stt.h, 1)
                        em.add("tensor", tr)
                        stt.n += 1
                        trd = stt.n

                        def ev(a, bslot=bslot, dst=xqf_t[:, 128 * cc:128 * (cc + 1)], trd=trd):
                            a.wait_ge(stt.h, trd)
                            a.copy(dst, bslot).then_inc(sa.h, 1)
                        em.add("scalar", ev)
                        sa.n += 1
                        G.set(("f_ps", slot), sa, sa.n)
                    xqf_ready[t] = (sa, sa.n)

                xr_sem, xr_n = xqf_ready[t]

                for ci, (coff, clen) in enumerate(nchunks):
                    gb = PS[ci % 2]
                    ub = PS[2 + ci % 2]

                    gwg = G.snap(("gu_g", ci % 2), ("gu_u", ci % 2), "banks_all")

                    def gu(te, coff=coff, clen=clen, gb=gb, ub=ub, xqf_t=xqf_t,
                           xr_sem=xr_sem, xr_n=xr_n, gwg=gwg,
                           wgn=wg_need[ci], wun=wu_need[ci]):
                        te.wait_ge(sw.h, wgn)
                        te.wait_ge(xr_sem.h, xr_n)
                        G.apply(te, gwg)
                        for k in range(EC):
                            te.matmul(gb[:, 0:clen], xqf_t[:, 128 * k:128 * (k + 1)],
                                      WG[:, IH * k + coff:IH * k + coff + clen],
                                      start=(k == 0), stop=(k == EC - 1))
                        te.wait_ge(sw.h, wun)
                        for k in range(EC):
                            last = te.matmul(ub[:, 0:clen], xqf_t[:, 128 * k:128 * (k + 1)],
                                             WU[:, IH * k + coff:IH * k + coff + clen],
                                             start=(k == 0), stop=(k == EC - 1))
                        last.then_inc(stt.h, 1)
                    em.add("tensor", gu)
                    stt.n += 1
                    gu_done = stt.n

                    def sil(a, gb=gb, coff=coff, clen=clen, gu_done=gu_done):
                        a.wait_ge(stt.h, gu_done)
                        a.activation(gsil[:, coff:coff + clen], gb[:, 0:clen],
                                     AF.Silu).then_inc(sa.h, 1)
                    em.add("scalar", sil)
                    sa.n += 1
                    G.set(("gu_g", ci % 2), sa, sa.n)
                    sil_done = sa.n

                    def hm(v, ub=ub, coff=coff, clen=clen, sil_done=sil_done):
                        v.wait_ge(sa.h, sil_done)
                        v.tensor_mul(hfull[:, coff:coff + clen],
                                     gsil[:, coff:coff + clen],
                                     ub[:, 0:clen]).then_inc(sv.h, 1)
                    em.add("vector", hm)
                    sv.n += 1
                    G.set(("gu_u", ci % 2), sv, sv.n)

                vec_ops = []
                gwh = G.snap(("hq_b", 0), ("hq_b", 1))
                vec_ops.append(lambda v, gwh=gwh: G.apply(v, gwh))
                emit_quant(vec_ops, hfull, IH, hscl, q2lh, hih, hih, inc=sv)
                for f in vec_ops:
                    em.add("vector", f)
                sv.n += 1
                hq_done = sv.n
                G.set(("hq_b", 0), sv, hq_done)
                G.set(("hq_b", 1), sv, hq_done)

                for cc in range(ICH):
                    slot = cc % 4
                    bank = PS[4 + slot]
                    bslot = bank[:, :].bitcast(BF16)[:, 0:128]

                    gwt2 = G.snap(("f_ps", slot), "banks_all")

                    def tr(te, bslot=bslot, cc=cc, hq_done=hq_done, gwt2=gwt2):
                        te.wait_ge(sv.h, hq_done)
                        G.apply(te, gwt2)
                        te.transpose(bslot, hih[:, 128 * cc:128 * (cc + 1)],
                                     ident[:, :]).then_inc(stt.h, 1)
                    em.add("tensor", tr)
                    stt.n += 1
                    trd = stt.n

                    def ev(a, bslot=bslot, dst=hqf[tt_i][:, 128 * cc:128 * (cc + 1)], trd=trd):
                        a.wait_ge(stt.h, trd)
                        a.copy(dst, bslot).then_inc(sa.h, 1)
                    em.add("scalar", ev)
                    sa.n += 1
                    G.set(("f_ps", slot), sa, sa.n)
                hq_ready[tt_i] = (sa, sa.n)

            hq_sem, hq_n = hq_ready[3]
            for c in range(ICH):
                rot = c % 2

                gww = G.snap(("wd_b", rot))

                def wdl(s, c=c, rot=rot, sl=sl, gww=gww):
                    G.apply(s, gww)
                    s.dma_start(out=wdb[rot],
                                in_=wdT_d[128 * (ICH * sl + c):128 * (ICH * sl + c + 1), :]
                                ).then_inc(sw.h, 16)
                em.add("sync", wdl)
                sw.n += 16
                wd_ld = sw.n

                gwd = G.snap("banks_dn", ("f_ps", 0), ("f_ps", 1), ("f_ps", 2),
                             ("f_ps", 3), ("f_ps", 4), ("f_ps", 5), ("f_ps", 6),
                             ("f_ps", 7), ("gu_g", 0), ("gu_g", 1),
                             ("gu_u", 0), ("gu_u", 1)) if c == 0 else ()

                def dn(te, c=c, rot=rot, hq_n=hq_n, wd_ld=wd_ld, gwd=gwd):
                    te.wait_ge(hq_sem.h, hq_n)
                    te.wait_ge(sw.h, wd_ld)
                    G.apply(te, gwd)
                    for tt_i in range(4):
                        for half in range(2):
                            last = te.matmul(PS[2 * tt_i + half][:, :],
                                             hqf[tt_i][:, 128 * c:128 * (c + 1)],
                                             wdb[rot][:, 512 * half:512 * (half + 1)],
                                             start=(c == 0), stop=(c == ICH - 1))
                    last.then_inc(stt.h, 1)
                em.add("tensor", dn)
                stt.n += 1
                G.set(("wd_b", rot), stt, stt.n)
            dn_done = stt.n

            for tt_i, t in enumerate(tiles):
                def evd(v, tt_i=tt_i, t=t, dn_done=dn_done):
                    v.wait_ge(stt.h, dn_done)
                    for half in range(2):
                        xs = X1[:, E * t + 512 * half:E * t + 512 * (half + 1)]
                        last = v.scalar_tensor_tensor(xs, PS[2 * tt_i + half][:, :],
                                                      0.5, xs, ALU.mult, ALU.add)
                    last.then_inc(sv.h, 1)
                em.add("vector", evd)
                sv.n += 1
                x1_done[t] = sv.n
            G.set("banks_dn", sv, sv.n)
            G.set("banks_all", sv, sv.n)

            if write_y:
                y3d = y_out.rearrange("(a p) e -> a p e", p=128)
                for t in tiles:
                    def sty(gps, t=t, seq=seq, n=x1_done[t]):
                        gps.wait_ge(sv.h, n)
                        gps.dma_start(out=y3d[seq * NT + t, :, :],
                                      in_=X1[:, E * t:E * (t + 1)]).then_inc(sy.h, 16)
                    em.add("gpsimd", sty)
                    sy.n += 16

        if sl == 1 and not write_y:
            def acts(a, x1_done=dict(x1_done)):
                for t in range(NT):
                    a.wait_ge(sv.h, x1_done[t])
                    a.activation(junk[:, :], X1[:, E * t:E * (t + 1)], AF.Identity,
                                 accum_out=stats[:, 32 + t:33 + t])
                    a.activation(junk[:, :], X1[:, E * t:E * (t + 1)], AF.Square,
                                 accum_out=stats[:, 48 + t:49 + t])
                a.drain()
                a.activation(junk[:, 0:1], stats[:, 32:33], AF.Identity).then_inc(sa.h, 1)
            em.add("scalar", acts)
            sa.n += 1
            emit_stats_finish(1, sa, sa.n)

        def wv(v):
            v.wait_ge(sv.h, sv.n)
        em.add("vector", wv)
        if write_y:
            def wgf(g):
                g.wait_ge(sy.h, sy.n)
            em.add("gpsimd", wgf)
        em.flush()


    def emit_attn(seq):
        C = Carver([(WKA, 12288), (XQF, NT * E // 2)])
        ln2t = C.bf16(E)
        ln2f = C.bf16(EC * 128)
        Qr = [C.bf16(EC * 128), C.bf16(EC * 128)]
        KRS, VRS = 3, 4
        Kr = [C.bf16(EC * 192) for _ in range(KRS)]
        Vr = [C.bf16(E) for _ in range(VRS)]
        smask = C.f32(2 * 192)
        pbuf = C.bf16(8 * 384)
        ptr = C.bf16(2 * 384)
        attb = [C.bf16(EC * 128), C.bf16(EC * 128)]
        lsum = C.f32(16)
        rsum = C.f32(16)
        cjunk = C.bf16(E)
        WSq = WSB[:, 0:4096].bitcast(BF16)
        WSk = WSB[:, 4096:8192].bitcast(BF16)
        WSv = WSB[:, 8192:12288].bitcast(BF16)
        WSo = C.bf16(EC * E)

        nm2 = negmb[:, 16:32]
        rs2 = rstdb[:, 16:32]

        w_base = sw.n

        def wld(s):
            for wsrc, wdst in ((qwT_d, WSq), (kwT_d, WSk), (vwT_d, WSv), (owT_d, WSo)):
                for k in range(EC):
                    s.dma_start(out=wdst[:, E * k:E * (k + 1)],
                                in_=wsrc[128 * k:128 * (k + 1), :]).then_inc(sw.h, 16)
        em.add("sync", wld)
        sw.n += 4 * EC * 16
        wq_done = w_base + EC * 16
        wk_done = w_base + 2 * EC * 16
        wv_done = w_base + 3 * EC * 16
        w_done = sw.n

        def vslot(u):
            if u < 0 or u >= NT:
                return zeroV
            return Vr[u % VRS]

        def pbz(v):
            v.memset(bass.AP(pbuf.tensor, pbuf.offset,
                             [list(pbuf.ap[0]), [384, 8], [1, 96]]), 0.0)
            v.memset(bass.AP(pbuf.tensor, pbuf.offset + 288,
                             [list(pbuf.ap[0]), [384, 8], [1, 96]]), 0.0)
            v.drain().then_inc(sv.h, 1)
        em.add("vector", pbz)
        sv.n += 1

        def emit_qkv(t):
            gws = G.snap("c_ln2t")

            def lnq(v, t=t, gws=gws):
                G.apply(v, gws)
                v.tensor_scalar(ln2t, X1[:, E * t:E * (t + 1)], nm2[:, t:t + 1],
                                rs2[:, t:t + 1], ALU.add, ALU.mult).then_inc(sv.h, 1)
            em.add("vector", lnq)
            sv.n += 1
            ln_done = sv.n

            for cc in range(EC):
                bi = 2 + cc % 3 if cc % 3 < 2 else 7
                bank = PS[bi]
                bslot = bank[:, :].bitcast(BF16)[:, 0:128]
                trslot = ("c_tr", bi)
                gwt = G.snap(trslot)

                def tr(te, bslot=bslot, cc=cc, ln_done=ln_done, gwt=gwt):
                    te.wait_ge(sv.h, ln_done)
                    G.apply(te, gwt)
                    te.transpose(bslot, ln2t[:, 128 * cc:128 * (cc + 1)],
                                 ident[:, :]).then_inc(stt.h, 1)
                em.add("tensor", tr)
                stt.n += 1
                trd = stt.n

                def ev(a, bslot=bslot, dst=ln2f[:, 128 * cc:128 * (cc + 1)], trd=trd):
                    a.wait_ge(stt.h, trd)
                    a.copy(dst, bslot).then_inc(sa.h, 1)
                em.add("scalar", ev)
                sa.n += 1
                G.set(trslot, sa, sa.n)
            lnf_done = (sa, sa.n)
            G.set("c_ln2t", stt, stt.n)

            for proj, WS, bias, scale in (("q", WSq, qbc, 0.125), ("k", WSk, kbc, None)):
                for oc in range(EC):
                    bank = 5 + (oc % 2)
                    gwp = G.snap(("c_qkv", bank))

                    def pmm(te, WS=WS, oc=oc, bank=bank, lnf=lnf_done, gwp=gwp,
                            wdn=(wq_done if proj == "q" else wk_done)):
                        te.wait_ge(sw.h, wdn)
                        te.wait_ge(lnf[0].h, lnf[1])
                        G.apply(te, gwp)
                        for k in range(EC):
                            last = te.matmul(PS[bank][:, 0:128],
                                             WS[:, E * k + 128 * oc:E * k + 128 * (oc + 1)],
                                             ln2f[:, 128 * k:128 * (k + 1)],
                                             start=(k == 0), stop=(k == EC - 1))
                        last.then_inc(stt.h, 1)
                    em.add("tensor", pmm)
                    stt.n += 1
                    pdone = stt.n

                    if proj == "q":
                        dst = Qr[t % 2][:, 128 * oc:128 * (oc + 1)]
                        gwr = G.snap(("c_qr", t % 2))

                        def pev(v, dst=dst, oc=oc, bank=bank, pdone=pdone, gwr=gwr):
                            v.wait_ge(stt.h, pdone)
                            G.apply(v, gwr)
                            v.tensor_scalar(dst, PS[bank][:, 0:128], qbc[:, oc:oc + 1],
                                            0.125, ALU.add, ALU.mult).then_inc(sv.h, 1)
                    else:
                        dst = Kr[t % KRS][:, 192 * oc + 32:192 * oc + 160]
                        gwr = G.snap(("c_kr", t % KRS))

                        def pev(v, dst=dst, oc=oc, bank=bank, pdone=pdone, gwr=gwr):
                            v.wait_ge(stt.h, pdone)
                            G.apply(v, gwr)
                            v.tensor_scalar(dst, PS[bank][:, 0:128], kbc[:, oc:oc + 1],
                                            None, ALU.add).then_inc(sv.h, 1)
                    em.add("vector", pev)
                    sv.n += 1
                    G.set(("c_qkv", bank), sv, sv.n)
            for half in range(2):
                bank = 5 + (half % 2)
                gwp = G.snap(("c_qkv", bank))

                def vmm(te, half=half, bank=bank, lnf=lnf_done, gwp=gwp, wdn=wv_done):
                    te.wait_ge(sw.h, wdn)
                    te.wait_ge(lnf[0].h, lnf[1])
                    G.apply(te, gwp)
                    for k in range(EC):
                        last = te.matmul(PS[bank][:, :],
                                         ln2f[:, 128 * k:128 * (k + 1)],
                                         WSv[:, E * k + 512 * half:E * k + 512 * (half + 1)],
                                         start=(k == 0), stop=(k == EC - 1))
                    last.then_inc(stt.h, 1)
                em.add("tensor", vmm)
                stt.n += 1
                vdone = stt.n
                dst = Vr[t % VRS][:, 512 * half:512 * (half + 1)]
                gwv = G.snap(("c_vr", t % VRS))

                def vev(v, dst=dst, bank=bank, half=half, vdone=vdone, gwv=gwv):
                    v.wait_ge(stt.h, vdone)
                    G.apply(v, gwv)
                    v.tensor_tensor(dst, PS[bank][:, :],
                                    vbrep[:, 512 * half:512 * (half + 1)],
                                    ALU.add).then_inc(sv.h, 1)
                em.add("vector", vev)
                sv.n += 1
                G.set(("c_qkv", bank), sv, sv.n)
            kv_done = sv.n

            def kedges(v, t=t, kv_done=kv_done):
                v.wait_ge(sv.h, kv_done)
                st_ = Kr[t % KRS]
                if t == 0:
                    v.memset(bass.AP(st_.tensor, st_.offset,
                                     [list(st_.ap[0]), [192, EC], [1, 32]]), 0.0)
                else:
                    sp = Kr[(t - 1) % KRS]
                    v.tensor_copy(
                        bass.AP(st_.tensor, st_.offset,
                                [list(st_.ap[0]), [192, EC], [1, 32]]),
                        bass.AP(sp.tensor, sp.offset + 128,
                                [list(sp.ap[0]), [192, EC], [1, 32]]))
                    v.tensor_copy(
                        bass.AP(sp.tensor, sp.offset + 160,
                                [list(sp.ap[0]), [192, EC], [1, 32]]),
                        bass.AP(st_.tensor, st_.offset + 32,
                                [list(st_.ap[0]), [192, EC], [1, 32]]))
                v.drain().then_inc(sv.h, 1)
            em.add("vector", kedges)
            sv.n += 1
            return sv.n

        def emit_kpad_end(t):
            def kz(v, t=t, w=sv.n):
                v.wait_ge(sv.h, w)
                st_ = Kr[t % KRS]
                v.memset(bass.AP(st_.tensor, st_.offset + 160,
                                 [list(st_.ap[0]), [192, EC], [1, 32]]), 0.0)
                v.drain().then_inc(sv.h, 1)
            em.add("vector", kz)
            sv.n += 1
            return sv.n

        def emit_att(t, k_ready):
            at = attb[t % 2]
            for b in range(2):
                for hp in range(4 * b, 4 * b + 4):
                    gws = G.snap(("c_s", 0), ("c_s", 1))

                    def smm(te, t=t, hp=hp, k_ready=k_ready, gws=gws):
                        te.wait_ge(sv.h, k_ready)
                        G.apply(te, gws)
                        for h01 in range(2):
                            p0 = 64 * h01
                            te.matmul(PS[h01][:, 0:192],
                                      Qr[t % 2][p0:p0 + 64, 128 * hp:128 * (hp + 1)],
                                      Kr[t % KRS][p0:p0 + 64, 192 * hp:192 * (hp + 1)],
                                      start=True, stop=True).then_inc(stt.h, 1)
                    em.add("tensor", smm)
                    stt.n += 2
                    sdone = stt.n

                    j = hp % 4
                    gwm = G.snap("c_pb" + str(j), "c_sm")

                    def msk(v, hp=hp, sdone=sdone, gwm=gwm):
                        v.wait_ge(stt.h, sdone)
                        G.apply(v, gwm)
                        v.tensor_mask_reduce(smask[:, 0:192], PS[0][:, 0:192],
                                             mst[:, :], men[:, :], 1.0, -1e30,
                                             ALU.max)
                        v.tensor_mask_reduce(smask[:, 192:384], PS[1][:, 0:192],
                                             mst[:, :], men[:, :], 1.0, -1e30,
                                             ALU.max).then_inc(sv.h, 1)
                    em.add("vector", msk)
                    sv.n += 1
                    mdone = sv.n
                    G.set(("c_s", 0), sv, sv.n)
                    G.set(("c_s", 1), sv, sv.n)

                    def ex(a, hp=hp, j=j, mdone=mdone):
                        a.wait_ge(sv.h, mdone)
                        a.activation(pbuf[:, 768 * j + 96:768 * j + 288],
                                     smask[:, 0:192], AF.Exp,
                                     accum_out=lsum[:, 2 * hp:2 * hp + 1])
                        a.activation(pbuf[:, 768 * j + 384 + 96:768 * j + 384 + 288],
                                     smask[:, 192:384], AF.Exp,
                                     accum_out=lsum[:, 2 * hp + 1:2 * hp + 2]
                                     ).then_inc(sa.h, 1)
                    em.add("scalar", ex)
                    sa.n += 1
                    G.set("c_sm", sa, sa.n)
                exp_done = sa.n

                def rc(v, b=b, exp_done=exp_done):
                    v.wait_ge(sa.h, exp_done)
                    v.drain()
                    v.reciprocal(rsum[:, 8 * b:8 * b + 8], lsum[:, 8 * b:8 * b + 8])
                    v.drain().then_inc(sv.h, 1)
                em.add("vector", rc)
                sv.n += 1
                rc_done = sv.n

                for hp in range(4 * b, 4 * b + 4):
                    j = hp % 4

                    def pm(v, hp=hp, j=j, rc_done=rc_done):
                        v.wait_ge(sv.h, rc_done)
                        v.tensor_scalar(pbuf[:, 768 * j + 96:768 * j + 288],
                                        pbuf[:, 768 * j + 96:768 * j + 288],
                                        rsum[:, 2 * hp:2 * hp + 1], None, ALU.mult)
                        v.tensor_scalar(pbuf[:, 768 * j + 384 + 96:768 * j + 384 + 288],
                                        pbuf[:, 768 * j + 384 + 96:768 * j + 384 + 288],
                                        rsum[:, 2 * hp + 1:2 * hp + 2], None,
                                        ALU.mult).then_inc(sv.h, 1)
                    em.add("vector", pm)
                    sv.n += 1
                    pm_done = sv.n

                    for i in range(6):
                        bi = 2 if i % 3 == 0 else (3 if i % 3 == 1 else 7)
                        bank = PS[bi]
                        bkey = ("c_tr", bi)
                        bslot = bank[:, :].bitcast(BF16)[:, 0:128]
                        gwt = G.snap(bkey)

                        def trp(te, j=j, i=i, bslot=bslot, pm_done=pm_done, gwt=gwt):
                            te.wait_ge(sv.h, pm_done)
                            G.apply(te, gwt)
                            te.transpose(bslot, pbuf[:, 768 * j + 128 * i:768 * j + 128 * (i + 1)],
                                         ident[:, :]).then_inc(stt.h, 1)
                        em.add("tensor", trp)
                        stt.n += 1
                        trd = stt.n

                        def evp(a, i=i, bslot=bslot, trd=trd):
                            a.wait_ge(stt.h, trd)
                            a.copy(ptr[:, 128 * i:128 * (i + 1)], bslot).then_inc(sa.h, 1)
                        em.add("scalar", evp)
                        sa.n += 1
                        G.set(bkey, sa, sa.n)
                    ptr_done = (sa, sa.n)

                    gwa = G.snap("c_av")

                    def avm(te, t=t, hp=hp, ptr_done=ptr_done, gwa=gwa):
                        te.wait_ge(ptr_done[0].h, ptr_done[1])
                        G.apply(te, gwa)
                        for h01 in range(2):
                            hh = 2 * hp + h01
                            for i in range(3):
                                vs_ = vslot(t - 1 + i)
                                last = te.matmul(PS[4][64 * h01:64 * h01 + 64, 0:128],
                                                 vs_[:, 64 * hh:64 * hh + 64],
                                                 ptr[:, 384 * h01 + 128 * i:384 * h01 + 128 * (i + 1)],
                                                 start=(i == 0), stop=(i == 2))
                        last.then_inc(stt.h, 1)
                    em.add("tensor", avm)
                    stt.n += 1
                    av_done = stt.n
                    G.set(("c_pb" + str(j)), stt, stt.n)

                    def ave(a, hp=hp, t=t, av_done=av_done):
                        a.wait_ge(stt.h, av_done)
                        a.copy(attb[t % 2][:, 128 * hp:128 * (hp + 1)],
                               PS[4][:, 0:128]).then_inc(sa.h, 1)
                    em.add("scalar", ave)
                    sa.n += 1
                    G.set("c_av", sa, sa.n)
            att_done = sa.n
            G.set(("c_qr", t % 2), stt, stt.n)
            G.set(("c_kr", t % KRS), stt, stt.n)
            for u in (t - 1, t, t + 1):
                if 0 <= u < NT:
                    G.set(("c_vr", u % VRS), stt, stt.n)

            for half in range(2):
                bank = 5 + half
                gwo = G.snap(("c_qkv", bank))

                def omm(te, t=t, half=half, bank=bank, att_done=att_done, gwo=gwo,
                        wdn=w_done):
                    te.wait_ge(sw.h, wdn)
                    te.wait_ge(sa.h, att_done)
                    G.apply(te, gwo)
                    for k in range(EC):
                        last = te.matmul(PS[bank][:, :],
                                         attb[t % 2][:, 128 * k:128 * (k + 1)],
                                         WSo[:, E * k + 512 * half:E * k + 512 * (half + 1)],
                                         start=(k == 0), stop=(k == EC - 1))
                    last.then_inc(stt.h, 1)
                em.add("tensor", omm)
                stt.n += 1
                odone = stt.n

                def oev(v, t=t, half=half, bank=bank, odone=odone):
                    v.wait_ge(stt.h, odone)
                    xs = X1[:, E * t + 512 * half:E * t + 512 * (half + 1)]
                    v.scalar_tensor_tensor(xs, PS[bank][:, :], 1.0, xs,
                                           ALU.mult, ALU.add)
                    v.scalar_tensor_tensor(xs, obrep[:, 512 * half:512 * (half + 1)],
                                           1.0, xs, ALU.mult, ALU.add).then_inc(sv.h, 1)
                em.add("vector", oev)
                sv.n += 1
                G.set(("c_qkv", bank), sv, sv.n)
            x2_done = sv.n

            def st3(a, t=t, x2_done=x2_done):
                a.wait_ge(sv.h, x2_done)
                a.activation(cjunk, X1[:, E * t:E * (t + 1)], AF.Identity,
                             accum_out=stats[:, 64 + t:65 + t])
                a.activation(cjunk, X1[:, E * t:E * (t + 1)], AF.Square,
                             accum_out=stats[:, 80 + t:81 + t]).then_inc(sa.h, 1)
            em.add("scalar", st3)
            sa.n += 1

        import os as _os3
        _natt = int(_os3.environ.get("DBG_ATT_TILES", str(NT)))
        k_ready = {}
        for t in range(_natt):
            k_ready[t - 1] = emit_qkv(t)
            if t >= 1:
                emit_att(t - 1, k_ready[t - 1])
        if _natt < NT:
            return
        k_ready[NT - 1] = emit_kpad_end(NT - 1)
        emit_att(NT - 1, k_ready[NT - 1])

        def sfin(a):
            a.drain()
            a.activation(cjunk[:, 0:1], stats[:, 64:65], AF.Identity).then_inc(sa.h, 1)
        em.add("scalar", sfin)
        sa.n += 1
        emit_stats_finish(2, sa, sa.n)

        def wv(v):
            v.wait_ge(sv.h, sv.n)
        em.add("vector", wv)
        em.flush()

    if DEBUG_STAGE == -2:
        x3d = x_in.rearrange("(a p) e -> a p e", p=128)
        y3d = y_out.rearrange("(a p) e -> a p e", p=128)

        def mv(s):
            s.dma_start(out=X1[:, 0:E], in_=x3d[0, :, :]).then_inc(sx.h, 16)
        em.add("sync", mv)
        sx.n += 16
        n0 = sx.n

        def cp(v):
            v.wait_ge(sx.h, n0)
            v.tensor_scalar_mul(X1[:, E:2 * E], X1[:, 0:E], 2.0).then_inc(sv.h, 1)
        em.add("vector", cp)
        sv.n += 1
        em.flush()
        n1 = sv.n

        def out(g):
            g.wait_ge(sv.h, n1)
            g.dma_start(out=y3d[0, :, :], in_=X1[:, E:2 * E]).then_inc(sy.h, 16)
            g.wait_ge(sy.h, 16)
        em.add("gpsimd", out)
        sy.n += 16
        em.flush()
        es.close()
        mybir.codegen_inst_isa_subclasses(nc)
        return nc
    import os as _os2
    _tap = _os2.environ.get("DBG_TAP", "")
    emit_preproc()
    if _tap == "pre":
        emit_pre(0)
        y3d = y_out.rearrange("(a p) e -> a p e", p=128)
        n0 = sv.n + sa.n

        def dmp(g, wv=sv.n, wa=sa.n):
            g.wait_ge(sv.h, wv)
            g.wait_ge(sa.h, wa)
            g.dma_start(out=y3d[0, :, 0:96], in_=stats[:, :]).then_inc(sy.h, 16)
            g.dma_start(out=y3d[0, :, 96:144], in_=negmb[:, :]).then_inc(sy.h, 16)
            g.dma_start(out=y3d[0, :, 144:192], in_=rstdb[:, :]).then_inc(sy.h, 16)
            g.dma_start(out=y3d[1, :, :], in_=X1[:, 0:E]).then_inc(sy.h, 16)
            g.wait_ge(sy.h, 64)
        em.add("gpsimd", dmp)
        sy.n += 64
        em.flush()
        es.close()
        mybir.codegen_inst_isa_subclasses(nc)
        return nc
    if DEBUG_STAGE == -1:
        em.flush()
        es.close()
        mybir.codegen_inst_isa_subclasses(nc)
        return nc
    if DEBUG_STAGE == 0:
        dbg = WKA[:, 0:1408].bitcast(BF16)
        dbgf = WSB[:, 0:2816]
        y3d = y_out.rearrange("(a p) e -> a p e", p=128)
        base_sy = sy.n

        def dget(s, w=sg.n):
            s.wait_ge(sg.h, w)
            s.dma_start(out=dbg[:, 0:2816], in_=wgT_d[0:128, :]).then_inc(sx.h, 16)
        em.add("sync", dget)
        sx.n += 16
        n0 = sx.n

        def dcast(v, n0=n0):
            v.wait_ge(sx.h, n0)
            v.tensor_copy(dbgf[:, 0:2816], dbg[:, 0:2816]).then_inc(sv.h, 1)
        em.add("vector", dcast)
        sv.n += 1
        n1 = sv.n

        def dout(g, n1=n1):
            g.wait_ge(sv.h, n1)
            for j in range(2):
                g.dma_start(out=y3d[j, :, :], in_=dbgf[:, 1024 * j:1024 * (j + 1)]).then_inc(sy.h, 16)
        em.add("gpsimd", dout)
        sy.n += 32
        n2 = sy.n

        def dget2(s, n2=n2):
            s.wait_ge(sy.h, n2)
            s.dma_start(out=dbg[:, 0:1024], in_=wdT_d[0:128, :]).then_inc(sx.h, 16)
        em.add("sync", dget2)
        sx.n += 16
        n3 = sx.n

        def dcast2(v, n3=n3):
            v.wait_ge(sx.h, n3)
            v.tensor_copy(dbgf[:, 0:1024], dbg[:, 0:1024]).then_inc(sv.h, 1)
        em.add("vector", dcast2)
        sv.n += 1
        n4 = sv.n
        fin_sy = sy.n + 16

        def dout2(g, n4=n4, fin_sy=fin_sy):
            g.wait_ge(sv.h, n4)
            g.dma_start(out=y3d[4, :, :], in_=dbgf[:, 0:1024]).then_inc(sy.h, 16)
            g.wait_ge(sy.h, fin_sy)
        em.add("gpsimd", dout2)
        sy.n += 16
        em.flush()
        es.close()
        mybir.codegen_inst_isa_subclasses(nc)
        return nc
    for seq in range(NSEQ):
        emit_pre(seq)
        xqf_ready.clear()
        emit_ffn(seq, 0, 0, False)
        emit_ffn(seq, 1, 0, DEBUG_STAGE == 1)
        if DEBUG_STAGE >= 2:
            emit_attn(seq)
            if DEBUG_STAGE >= 3:
                emit_ffn(seq, 0, 2, False)
                emit_ffn(seq, 1, 2, True)
            if DEBUG_STAGE == 2:
                y3d = y_out.rearrange("(a p) e -> a p e", p=128)
                wsv = sv.n

                def dx2(g, seq=seq, wsv=wsv):
                    g.wait_ge(sv.h, wsv)
                    for t in range(NT):
                        g.dma_start(out=y3d[seq * NT + t, :, :],
                                    in_=X1[:, E * t:E * (t + 1)]).then_inc(sy.h, 16)
                em.add("gpsimd", dx2)
                sy.n += NT * 16

                def wgf(g):
                    g.wait_ge(sy.h, sy.n)
                em.add("gpsimd", wgf)
                em.flush()

    es.close()
    mybir.codegen_inst_isa_subclasses(nc)
    return nc


_PROG = None


def kernel(**inputs):
    global _PROG
    x = np.ascontiguousarray(np.asarray(inputs["L0_attn_hidden_states"], dtype=np.float32))
    ident = np.eye(128, dtype=BF)
    mstart = np.arange(128, dtype=np.float32).reshape(128, 1)
    mend = mstart + 65.0

    if _PROG is None:
        _PROG = build_program()
    nc = _PROG

    base = {
        "wg": np.ascontiguousarray(np.asarray(inputs["L0_ffn_w_gate"], dtype=np.float32)),
        "wu": np.ascontiguousarray(np.asarray(inputs["L0_ffn_w_up"], dtype=np.float32)),
        "wd": np.ascontiguousarray(np.asarray(inputs["L0_ffn_w_down"], dtype=np.float32)),
        "qw": np.ascontiguousarray(np.asarray(inputs["L0_attn_q_w"], dtype=np.float32)),
        "kw": np.ascontiguousarray(np.asarray(inputs["L0_attn_k_w"], dtype=np.float32)),
        "vw": np.ascontiguousarray(np.asarray(inputs["L0_attn_v_w"], dtype=np.float32)),
        "ow": np.ascontiguousarray(np.asarray(inputs["L0_attn_o_w"], dtype=np.float32)),
        "qb": np.ascontiguousarray(np.asarray(inputs["L0_attn_q_b"], dtype=np.float32)),
        "kb": np.ascontiguousarray(np.asarray(inputs["L0_attn_k_b"], dtype=np.float32)),
        "vb": np.ascontiguousarray(np.asarray(inputs["L0_attn_v_b"], dtype=np.float32)),
        "ob": np.ascontiguousarray(np.asarray(inputs["L0_attn_o_b"], dtype=np.float32)),
        "ident": ident, "mstart": mstart, "mend": mend,
    }
    in_maps = []
    for c in range(NCORES):
        m = dict(base)
        m["x"] = np.ascontiguousarray(x[c * NSEQ:(c + 1) * NSEQ].reshape(NSEQ * SEQ, E))
        in_maps.append(m)
    res = run_bass_kernel_spmd(nc, in_maps, core_ids=list(range(NCORES)))
    out = np.concatenate([r["y"].reshape(NSEQ, SEQ, E) for r in res.results], axis=0)
    return out


# version 04 (final)
# speedup vs baseline: 1.2014x; 1.0006x over previous
import contextlib
import numpy as np
import ml_dtypes

import concourse.bass as bass
import concourse.mybir as mybir
from concourse.bass_utils import run_bass_kernel_spmd
from concourse import dve_ops as dvo
from concourse.dve_op import DveOp
from concourse.dve_spec import Spec, Src0, C0, C1, Zero, maxx, minn, lower
from concourse.dve_table_gen import dve_ver_for

F32 = mybir.dt.float32
BF16 = mybir.dt.bfloat16
ALU = mybir.AluOpType
AF = mybir.ActivationFunctionType
AX = mybir.AxisListType
BF = ml_dtypes.bfloat16

MAGIC = float(np.float32(1.5 * 2 ** 23))
DEKC = float(np.float32(2 ** 22 + 1))

NCORES = 8
B = 16
SEQ = 2048
E = 1024
EC = 8
I = 2816
NS = 2
IH = I // NS
ICH = IH // 128
NT = SEQ // 128
NSEQ = B // NCORES
NHEAD = 16
HD = 64
EPS = 1e-5

DEBUG_STAGE = 3


def _refA(in0, in1, s0, s1, imm2):
    f = np.float64 if in0.dtype == np.float64 else np.float32
    r1 = ((in0.astype(f) + np.array(s1, f)) - np.array(s1, f)).astype(f)
    return np.clip(r1, -s0, s0).astype(f)


def _refB(in0, in1, s0, s1, imm2):
    an32 = in0.astype(np.float32)
    w = (an32 * np.float32(s1)).astype(np.float32)
    t = (w - an32).astype(np.float32)
    D = (w - t).astype(in0.dtype if in0.dtype == np.float64 else np.float32)
    return (D - np.clip(D, -s0, s0)).astype(D.dtype)


def register_ops():
    ver = dve_ver_for("TRN2")
    out = []
    _r1 = (Src0 + C1) - C1
    specA = Spec(body=maxx(minn(_r1, C0), Zero - C0))
    _w = Src0 * C1
    _D = _w - (_w - Src0)
    specB = Spec(body=_D - maxx(minn(_D, C0), Zero - C0))
    for name, spec, ref in [("ANT_FP4_A", specA, _refA), ("ANT_FP4_B", specB, _refB)]:
        ex = next((o for o in dvo.OPS if o.name == name), None)
        if ex is not None:
            out.append(ex)
            continue
        compiled = lower(spec, ver=ver, perf_en=False, name=name)
        op = DveOp(name, spec, subdim=False, uops_sha={ver: compiled.sha(ver)},
                   reference=ref)
        row = dvo._CUSTOM_DVE_ROW_BASE + len(dvo.OPS)
        assert row < 0x20
        dvo.OPS.append(op)
        dvo._SUB_OPCODE_FOR_NAME[name] = row
        dvo.CUSTOM_DVE_SPECS[name] = spec
        dvo.CUSTOM_DVE_REFS[name] = ref
        out.append(op)
    return out


OPA, OPB = register_ops()


class Sem:
    def __init__(self, h):
        self.h = h
        self.n = 0

    def inc(self, inst, k=1):
        inst.then_inc(self.h, k)
        self.n += k
        return self.n


class Emitter:
    ENGINES = ("sync", "tensor", "vector", "scalar", "gpsimd")

    def __init__(self, nc):
        self.nc = nc
        self.new_phase()

    def new_phase(self):
        self.ops = {e: [] for e in self.ENGINES}

    def add(self, eng, fn):
        self.ops[eng].append(fn)

    def flush(self):
        ops = self.ops
        with self.nc.Block() as block:
            @block.sync
            def _(eng, ops=ops):
                for f in ops["sync"]:
                    f(eng)

            @block.tensor
            def _(eng, ops=ops):
                for f in ops["tensor"]:
                    f(eng)

            @block.vector
            def _(eng, ops=ops):
                for f in ops["vector"]:
                    f(eng)

            @block.scalar
            def _(eng, ops=ops):
                for f in ops["scalar"]:
                    f(eng)

            @block.gpsimd
            def _(eng, ops=ops):
                for f in ops["gpsimd"]:
                    f(eng)
        self.new_phase()


class Carver:
    def __init__(self, pools):
        self.pools = pools
        self.pos = [0] * len(pools)

    def f32(self, cols):
        for i, (t, n) in enumerate(self.pools):
            if self.pos[i] + cols <= n:
                off = self.pos[i]
                self.pos[i] += cols
                return t[:, off:off + cols]
        raise RuntimeError(f"carver OOM: need {cols}, pos={self.pos}")

    def bf16(self, cols):
        assert cols % 2 == 0
        return self.f32(cols // 2).bitcast(BF16)


def bcast16(ap, nblocks):
    return bass.AP(ap.tensor, ap.offset,
                   [list(ap.ap[0]), [list(ap.ap[1])[0], nblocks], [0, 16]])


class Guards:

    def __init__(self):
        self.g = {}

    def set(self, key, sem, n):
        self.g[key] = (sem, n)

    def snap(self, *keys):
        best = {}
        for k in keys:
            if k in self.g:
                s, n = self.g[k]
                if best.get(id(s), (None, -1))[1] < n:
                    best[id(s)] = (s, n)
        return tuple(best.values())

    @staticmethod
    def apply(eng, snap):
        for s, n in snap:
            if n > 0:
                eng.wait_ge(s.h, n)


def build_program():
    import os as _os
    nc = bass.Bass("TRN2", target_bir_lowering=False,
                   detect_race_conditions=not _os.environ.get("NORACE"))

    x_in = nc.dram_tensor("x", [NSEQ * SEQ, E], F32, kind="ExternalInput").ap()
    y_out = nc.dram_tensor("y", [NSEQ * SEQ, E], F32, kind="ExternalOutput").ap()
    wg_in = nc.dram_tensor("wg", [I, E], F32, kind="ExternalInput").ap()
    wu_in = nc.dram_tensor("wu", [I, E], F32, kind="ExternalInput").ap()
    wd_in = nc.dram_tensor("wd", [E, I], F32, kind="ExternalInput").ap()
    qw_in = nc.dram_tensor("qw", [E, E], F32, kind="ExternalInput").ap()
    kw_in = nc.dram_tensor("kw", [E, E], F32, kind="ExternalInput").ap()
    vw_in = nc.dram_tensor("vw", [E, E], F32, kind="ExternalInput").ap()
    ow_in = nc.dram_tensor("ow", [E, E], F32, kind="ExternalInput").ap()
    qb_in = nc.dram_tensor("qb", [E], F32, kind="ExternalInput").ap()
    kb_in = nc.dram_tensor("kb", [E], F32, kind="ExternalInput").ap()
    vb_in = nc.dram_tensor("vb", [E], F32, kind="ExternalInput").ap()
    ob_in = nc.dram_tensor("ob", [E], F32, kind="ExternalInput").ap()
    ident_in = nc.dram_tensor("ident", [128, 128], BF16, kind="ExternalInput").ap()
    mst_in = nc.dram_tensor("mstart", [128, 1], F32, kind="ExternalInput").ap()
    men_in = nc.dram_tensor("mend", [128, 1], F32, kind="ExternalInput").ap()

    wgT_d = nc.dram_tensor("wgT", [EC * 128, I], BF16, kind="Internal").ap()
    wuT_d = nc.dram_tensor("wuT", [EC * 128, I], BF16, kind="Internal").ap()
    wdT_d = nc.dram_tensor("wdT", [I, E], BF16, kind="Internal").ap()
    qwT_d = nc.dram_tensor("qwT", [EC * 128, E], BF16, kind="Internal").ap()
    kwT_d = nc.dram_tensor("kwT", [EC * 128, E], BF16, kind="Internal").ap()
    vwT_d = nc.dram_tensor("vwT", [EC * 128, E], BF16, kind="Internal").ap()
    owT_d = nc.dram_tensor("owT", [EC * 128, E], BF16, kind="Internal").ap()
    wT_scratch = {id(wg_in): wgT_d, id(wu_in): wuT_d, id(wd_in): wdT_d,
                  id(qw_in): qwT_d, id(kw_in): kwT_d, id(vw_in): vwT_d,
                  id(ow_in): owT_d}

    es = contextlib.ExitStack()

    def sbt(name, shape, dtype):
        return es.enter_context(nc.sbuf_tensor(name, shape, dtype))

    ident = sbt("identt", [128, 128], BF16)
    mst = sbt("mstt", [128, 1], F32)
    men = sbt("ment", [128, 1], F32)
    qbc = sbt("qbc", [128, EC], F32)
    kbc = sbt("kbc", [128, EC], F32)
    vbrep = sbt("vbrep", [128, E], F32)
    obrep = sbt("obrep", [128, E], F32)
    zeroV = sbt("zeroV", [128, E], BF16)
    stats = sbt("stats", [128, 96], F32)
    negmb = sbt("negmb", [128, 48], F32)
    rstdb = sbt("rstdb", [128, 48], F32)
    tinyw = sbt("tinyw", [128, 16], F32)

    X1 = sbt("X1", [128, NT * E], F32)
    XQF = sbt("XQF", [128, NT * E // 2], F32)
    WSB = sbt("WSB", [128, 12288], F32)
    WKA = sbt("WKA", [128, 12288], F32)
    junk = WKA[:, 0:512].bitcast(BF16)
    brow = None

    PS = [es.enter_context(nc.psum_tensor(f"ps{i}", [128, 512], F32))
          for i in range(8)]

    sx = Sem(es.enter_context(nc.semaphore("sx")))
    sw = Sem(es.enter_context(nc.semaphore("sw")))
    sv = Sem(es.enter_context(nc.semaphore("sv")))
    sa = Sem(es.enter_context(nc.semaphore("sa")))
    stt = Sem(es.enter_context(nc.semaphore("stt")))
    sg = Sem(es.enter_context(nc.semaphore("sg")))
    sy = Sem(es.enter_context(nc.semaphore("sy")))

    em = Emitter(nc)
    G = Guards()

    def emit_quant(vec_ops, src, n, scl, q2l, hi, q_out, inc=None):
        nb = n // 16
        bm, rec, bs = scl[:, 0:nb], scl[:, nb:2 * nb], scl[:, 2 * nb:3 * nb]

        def f(v, src=src, nb=nb, bm=bm, rec=rec, bs=bs, q2l=q2l, hi=hi,
              q_out=q_out, inc=inc):
            s3 = src.rearrange("p (s x) -> p s x", x=16)
            v.tensor_reduce(bm, s3, AX.X, ALU.max, apply_absolute_value=True)
            v.tensor_scalar_max(bm, bm, 6e-12)
            v.reciprocal(rec, bm)
            v.drain()
            v.tensor_scalar_mul(rec, rec, 12.0)
            v.tensor_scalar_mul(bs, bm, float(np.float32(1.0 / 12.0)))
            v.tensor_tensor(s3, s3, bcast16(rec, nb), ALU.mult)
            v._custom_dve(OPA, out=q2l, in0=src, s0=4.0, s1=MAGIC)
            v._custom_dve(OPB, out=hi, in0=src, s0=4.0, s1=DEKC)
            v.tensor_add(q2l, q2l, hi)
            last = v.tensor_tensor(q_out.rearrange("p (s x) -> p s x", x=16),
                                   q2l.rearrange("p (s x) -> p s x", x=16),
                                   bcast16(bs, nb), ALU.mult)
            if inc is not None:
                last.then_inc(inc.h, 1)
        vec_ops.append(f)

    def emit_stats_finish(slot, wait_sem, wait_n):
        sc = 32 * slot
        nsl = 16 * slot

        def fin(v):
            v.wait_ge(wait_sem.h, wait_n)
            v.tensor_scalar_mul(negmb[:, nsl:nsl + 16], stats[:, sc:sc + 16], -1.0 / E)
            v.drain()
            v.tensor_mul(tinyw[:, 0:16], negmb[:, nsl:nsl + 16], negmb[:, nsl:nsl + 16])
            v.tensor_scalar(rstdb[:, nsl:nsl + 16], stats[:, sc + 16:sc + 32],
                            1.0 / E, EPS, ALU.mult, ALU.add)
            v.drain()
            v.tensor_sub(rstdb[:, nsl:nsl + 16], rstdb[:, nsl:nsl + 16],
                         tinyw[:, 0:16]).then_inc(sv.h, 1)
        em.add("vector", fin)
        sv.n += 1
        var_done = sv.n

        def sq(a):
            a.wait_ge(sv.h, var_done)
            a.activation(tinyw[:, 0:16], rstdb[:, nsl:nsl + 16], AF.Ln)
            a.drain()
            a.activation(rstdb[:, nsl:nsl + 16], tinyw[:, 0:16], AF.Exp,
                         scale=-0.5).then_inc(sa.h, 1)
        em.add("scalar", sq)
        sa.n += 1
        return sa.n

    def emit_preproc():
        C = Carver([(WKA, 12288), (WSB, 12288)])
        browt = C.f32(E)
        brow = browt

        import os
        _dbg_skip = set(os.environ.get("DBGSKIP", "").split(","))
        _loadlist = []
        if "ident" not in _dbg_skip:
            _loadlist.append(lambda s: s.dma_start(out=ident[:, :], in_=ident_in))
        if "mst" not in _dbg_skip:
            _loadlist.append(lambda s: s.dma_start(out=mst[:, :], in_=mst_in))
            _loadlist.append(lambda s: s.dma_start(out=men[:, :], in_=men_in))
        if "qbc" not in _dbg_skip:
            _loadlist.append(lambda s: s.dma_start(out=qbc[:, :], in_=qb_in.rearrange("(c p) -> p c", p=128), allow_slow_non_contiguous=True))
            _loadlist.append(lambda s: s.dma_start(out=kbc[:, :], in_=kb_in.rearrange("(c p) -> p c", p=128), allow_slow_non_contiguous=True))
        if "rep" not in _dbg_skip:
            _loadlist.append(lambda s: s.dma_start(out=vbrep[:, :], in_=bass.AP(vb_in.tensor, 0, [[0, 128], [1, E]])))
            _loadlist.append(lambda s: s.dma_start(out=obrep[:, :], in_=bass.AP(ob_in.tensor, 0, [[0, 128], [1, E]])))

        def loads(s):
            for f in _loadlist:
                f(s).then_inc(sx.h, 16)
        em.add("sync", loads)
        sx.n += len(_loadlist) * 16

        def zv(v):
            v.memset(zeroV[:, :], 0.0).then_inc(sv.h, 1)
        em.add("vector", zv)
        sv.n += 1

        wtmp = [C.f32(I), C.f32(I)]
        wq = [C.bf16(I), C.bf16(I)]
        wscl = C.f32(3 * (I // 16) + 16)
        wq2l = C.bf16(I)
        whi = C.bf16(I)
        stg = [C.bf16(512), C.bf16(512)]

        jobs = [(wg_in, I, E, True), (wu_in, I, E, True), (wd_in, E, I, True),
                (qw_in, E, E, False), (kw_in, E, E, False),
                (vw_in, E, E, False), (ow_in, E, E, False)]
        chunks = []
        for (din, rows, cols, doq) in jobs:
            for r in range(rows // 128):
                chunks.append((din, cols, doq, r))
        import os
        _lim = int(os.environ.get("DBG_NCHUNKS", "999"))
        chunks = chunks[:_lim]

        for idx, (din, cols, doq, r) in enumerate(chunks):
            rot = idx % 2
            wt = wtmp[rot][:, 0:cols]
            wqb = wq[rot][:, 0:cols]
            dout = wT_scratch[id(din)]

            gw = G.snap(("pp_wt", rot))

            def ld(s, din=din, cols=cols, r=r, wt=wt, gw=gw):
                G.apply(s, gw)
                s.dma_start(out=wt, in_=din[128 * r:128 * (r + 1), 0:cols]).then_inc(sx.h, 16)
            em.add("sync", ld)
            sx.n += 16
            x_after = sx.n

            vec_ops = []

            gw2 = G.snap(("pp_wq", rot))

            def wv(v, x_after=x_after, gw2=gw2):
                v.wait_ge(sx.h, x_after)
                G.apply(v, gw2)
            vec_ops.append(wv)
            if doq:
                emit_quant(vec_ops, wt, cols, wscl, wq2l[:, 0:cols], whi[:, 0:cols],
                           wqb, inc=sv)
            else:
                vec_ops.append(lambda v, wt=wt, wqb=wqb:
                               v.tensor_copy(wqb, wt).then_inc(sv.h, 1))
            for f in vec_ops:
                em.add("vector", f)
            sv.n += 1
            q_done = sv.n
            G.set(("pp_wt", rot), sv, q_done)

            ncc = cols // 128
            for cc in range(ncc):
                slot = cc % 4
                bank = PS[4 + slot]
                bslot = bank[:, :].bitcast(BF16)[:, 0:128]

                gw3 = G.snap(("pp_ps", slot))

                def tr(t, wqb=wqb, cc=cc, bslot=bslot, q_done=q_done, gw3=gw3):
                    t.wait_ge(sv.h, q_done)
                    G.apply(t, gw3)
                    t.transpose(bslot, wqb[:, 128 * cc:128 * (cc + 1)],
                                ident[:, :]).then_inc(stt.h, 1)
                em.add("tensor", tr)
                stt.n += 1
                trd = stt.n

                srot = (cc // 4) % 2
                sslot = stg[srot][:, 128 * (cc % 4):128 * (cc % 4) + 128]

                gw4 = G.snap(("pp_stg", srot, cc % 4))

                def ev(a, bslot=bslot, sslot=sslot, trd=trd, gw4=gw4):
                    a.wait_ge(stt.h, trd)
                    G.apply(a, gw4)
                    a.copy(sslot, bslot).then_inc(sa.h, 1)
                em.add("scalar", ev)
                sa.n += 1
                G.set(("pp_ps", slot), sa, sa.n)
                ev_done = sa.n

                def out(g, dout=dout, cc=cc, r=r, sslot=sslot, ev_done=ev_done):
                    g.wait_ge(sa.h, ev_done)
                    g.dma_start(out=dout[128 * cc:128 * (cc + 1), 128 * r:128 * (r + 1)],
                                in_=sslot).then_inc(sg.h, 16)
                em.add("gpsimd", out)
                sg.n += 16
                G.set(("pp_stg", srot, cc % 4), sg, sg.n)
            G.set(("pp_wq", rot), stt, stt.n)

        def fin(g):
            g.wait_ge(sg.h, sg.n)
        em.add("gpsimd", fin)
        em.flush()

    def emit_pre(seq):
        x3d = x_in.rearrange("(a p) e -> a p e", p=128)

        def ld(s, seq=seq):
            for t in range(NT):
                s.dma_start(out=X1[:, E * t:E * (t + 1)],
                            in_=x3d[seq * NT + t, :, :]).then_inc(sx.h, 16)
        em.add("sync", ld)
        sx.n += NT * 16
        x_done = sx.n

        def acts(a, x_done=x_done):
            for t in range(NT):
                a.wait_ge(sx.h, x_done - (NT - 1 - t) * 16)
                a.activation(junk[:, :], X1[:, E * t:E * (t + 1)], AF.Identity,
                             accum_out=stats[:, t:t + 1])
                a.activation(junk[:, :], X1[:, E * t:E * (t + 1)], AF.Square,
                             accum_out=stats[:, 16 + t:17 + t])
            a.drain()
            a.activation(junk[:, 0:1], stats[:, 0:1], AF.Identity).then_inc(sa.h, 1)
        em.add("scalar", acts)
        sa.n += 1
        emit_stats_finish(0, sa, sa.n)
        em.flush()

    xqf_ready = {}

    def emit_ffn(seq, sl, ln_slot, write_y, flush=True):
        C = Carver([(WKA, 12288)])
        lnt = C.f32(E)
        xscl = C.f32(3 * (E // 16))
        q2l = C.bf16(E)
        xqtm = C.bf16(E)
        gsil = C.f32(IH)
        hfull = C.f32(IH)
        hscl = C.f32(3 * (IH // 16))
        q2lh = C.bf16(IH)
        hih = C.bf16(IH)
        hqf = [C.bf16(ICH * 128) for _ in range(4)]
        wdb = [C.bf16(E), C.bf16(E)]

        WG = WSB[:, 0:EC * IH // 2].bitcast(BF16)
        WU = WSB[:, EC * IH // 2:EC * IH].bitcast(BF16)
        nm = negmb[:, 16 * ln_slot:16 * ln_slot + 16]
        rs = rstdb[:, 16 * ln_slot:16 * ln_slot + 16]

        nchunks = [(0, 512), (512, 512), (1024, 384)]

        w_base = sw.n
        gwq_prev = G.snap("wsb_rd")

        def wld(s, sl=sl, gwq_prev=gwq_prev):
            G.apply(s, gwq_prev)
            for coff, clen in nchunks:
                for W, wT in ((WG, wgT_d), (WU, wuT_d)):
                    for k in range(EC):
                        s.dma_start(out=W[:, IH * k + coff:IH * k + coff + clen],
                                    in_=wT[128 * k:128 * (k + 1),
                                           IH * sl + coff:IH * sl + coff + clen]
                                    ).then_inc(sw.h, 16)
        em.add("sync", wld)
        sw.n += len(nchunks) * 2 * EC * 16
        wg_need = [w_base + 16 * (16 * ci + 8) for ci in range(len(nchunks))]
        wu_need = [w_base + 16 * (16 * ci + 16) for ci in range(len(nchunks))]
        hq_ready = {}
        x1_done = {}

        for g in range(4):
            tiles = [4 * g + j for j in range(4)]
            for tt_i, t in enumerate(tiles):
                xt = X1[:, E * t:E * (t + 1)]
                xqf_t = XQF[:, (E // 2) * t:(E // 2) * (t + 1)].bitcast(BF16)

                if sl == 0:
                    vec_ops = []

                    gwl = G.snap("lnt")

                    def lnq(v, xt=xt, t=t, gwl=gwl):
                        G.apply(v, gwl)
                        v.tensor_scalar(lnt, xt, nm[:, t:t + 1], rs[:, t:t + 1],
                                        ALU.add, ALU.mult)
                    vec_ops.append(lnq)
                    emit_quant(vec_ops, lnt, E, xscl, q2l, xqtm, xqtm, inc=sv)
                    for f in vec_ops:
                        em.add("vector", f)
                    sv.n += 1
                    xq_done = sv.n
                    G.set("lnt", sv, xq_done)

                    for cc in range(EC):
                        slot = cc % 4
                        bank = PS[4 + slot]
                        bslot = bank[:, :].bitcast(BF16)[:, 0:128]

                        gwt = G.snap(("f_ps", slot), "banks_all")

                        def tr(te, bslot=bslot, cc=cc, xq_done=xq_done, gwt=gwt):
                            te.wait_ge(sv.h, xq_done)
                            G.apply(te, gwt)
                            te.transpose(bslot, xqtm[:, 128 * cc:128 * (cc + 1)],
                                         ident[:, :]).then_inc(stt.h, 1)
                        em.add("tensor", tr)
                        stt.n += 1
                        trd = stt.n

                        def ev(a, bslot=bslot, dst=xqf_t[:, 128 * cc:128 * (cc + 1)], trd=trd):
                            a.wait_ge(stt.h, trd)
                            a.copy(dst, bslot).then_inc(sa.h, 1)
                        em.add("scalar", ev)
                        sa.n += 1
                        G.set(("f_ps", slot), sa, sa.n)
                    xqf_ready[t] = (sa, sa.n)

                xr_sem, xr_n = xqf_ready[t]

                for ci, (coff, clen) in enumerate(nchunks):
                    gb = PS[ci % 2]
                    ub = PS[2 + ci % 2]

                    gwg = G.snap(("gu_g", ci % 2), ("gu_u", ci % 2), "banks_all")

                    def gu(te, coff=coff, clen=clen, gb=gb, ub=ub, xqf_t=xqf_t,
                           xr_sem=xr_sem, xr_n=xr_n, gwg=gwg,
                           wgn=wg_need[ci], wun=wu_need[ci]):
                        te.wait_ge(sw.h, wgn)
                        te.wait_ge(xr_sem.h, xr_n)
                        G.apply(te, gwg)
                        for k in range(EC):
                            te.matmul(gb[:, 0:clen], xqf_t[:, 128 * k:128 * (k + 1)],
                                      WG[:, IH * k + coff:IH * k + coff + clen],
                                      start=(k == 0), stop=(k == EC - 1))
                        te.wait_ge(sw.h, wun)
                        for k in range(EC):
                            last = te.matmul(ub[:, 0:clen], xqf_t[:, 128 * k:128 * (k + 1)],
                                             WU[:, IH * k + coff:IH * k + coff + clen],
                                             start=(k == 0), stop=(k == EC - 1))
                        last.then_inc(stt.h, 1)
                    em.add("tensor", gu)
                    stt.n += 1
                    gu_done = stt.n
                    G.set("wsb_rd", stt, stt.n)

                    def sil(a, gb=gb, coff=coff, clen=clen, gu_done=gu_done):
                        a.wait_ge(stt.h, gu_done)
                        a.activation(gsil[:, coff:coff + clen], gb[:, 0:clen],
                                     AF.Silu).then_inc(sa.h, 1)
                    em.add("scalar", sil)
                    sa.n += 1
                    G.set(("gu_g", ci % 2), sa, sa.n)
                    sil_done = sa.n

                    def hm(v, ub=ub, coff=coff, clen=clen, sil_done=sil_done):
                        v.wait_ge(sa.h, sil_done)
                        v.tensor_mul(hfull[:, coff:coff + clen],
                                     gsil[:, coff:coff + clen],
                                     ub[:, 0:clen]).then_inc(sv.h, 1)
                    em.add("vector", hm)
                    sv.n += 1
                    G.set(("gu_u", ci % 2), sv, sv.n)

                vec_ops = []
                gwh = G.snap(("hq_b", 0), ("hq_b", 1))
                vec_ops.append(lambda v, gwh=gwh: G.apply(v, gwh))
                emit_quant(vec_ops, hfull, IH, hscl, q2lh, hih, hih, inc=sv)
                for f in vec_ops:
                    em.add("vector", f)
                sv.n += 1
                hq_done = sv.n
                G.set(("hq_b", 0), sv, hq_done)
                G.set(("hq_b", 1), sv, hq_done)

                for cc in range(ICH):
                    slot = cc % 4
                    bank = PS[4 + slot]
                    bslot = bank[:, :].bitcast(BF16)[:, 0:128]

                    gwt2 = G.snap(("f_ps", slot), "banks_all")

                    def tr(te, bslot=bslot, cc=cc, hq_done=hq_done, gwt2=gwt2):
                        te.wait_ge(sv.h, hq_done)
                        G.apply(te, gwt2)
                        te.transpose(bslot, hih[:, 128 * cc:128 * (cc + 1)],
                                     ident[:, :]).then_inc(stt.h, 1)
                    em.add("tensor", tr)
                    stt.n += 1
                    trd = stt.n

                    def ev(a, bslot=bslot, dst=hqf[tt_i][:, 128 * cc:128 * (cc + 1)], trd=trd):
                        a.wait_ge(stt.h, trd)
                        a.copy(dst, bslot).then_inc(sa.h, 1)
                    em.add("scalar", ev)
                    sa.n += 1
                    G.set(("f_ps", slot), sa, sa.n)
                hq_ready[tt_i] = (sa, sa.n)

            hq_sem, hq_n = hq_ready[3]
            for c in range(ICH):
                rot = c % 2

                gww = G.snap(("wd_b", rot))

                def wdl(s, c=c, rot=rot, sl=sl, gww=gww):
                    G.apply(s, gww)
                    s.dma_start(out=wdb[rot],
                                in_=wdT_d[128 * (ICH * sl + c):128 * (ICH * sl + c + 1), :]
                                ).then_inc(sw.h, 16)
                em.add("sync", wdl)
                sw.n += 16
                wd_ld = sw.n

                gwd = G.snap("banks_dn", ("f_ps", 0), ("f_ps", 1), ("f_ps", 2),
                             ("f_ps", 3), ("f_ps", 4), ("f_ps", 5), ("f_ps", 6),
                             ("f_ps", 7), ("gu_g", 0), ("gu_g", 1),
                             ("gu_u", 0), ("gu_u", 1)) if c == 0 else ()

                def dn(te, c=c, rot=rot, hq_n=hq_n, wd_ld=wd_ld, gwd=gwd):
                    te.wait_ge(hq_sem.h, hq_n)
                    te.wait_ge(sw.h, wd_ld)
                    G.apply(te, gwd)
                    for tt_i in range(4):
                        for half in range(2):
                            last = te.matmul(PS[2 * tt_i + half][:, :],
                                             hqf[tt_i][:, 128 * c:128 * (c + 1)],
                                             wdb[rot][:, 512 * half:512 * (half + 1)],
                                             start=(c == 0), stop=(c == ICH - 1))
                    last.then_inc(stt.h, 1)
                em.add("tensor", dn)
                stt.n += 1
                G.set(("wd_b", rot), stt, stt.n)
            dn_done = stt.n

            for tt_i, t in enumerate(tiles):
                def evd(v, tt_i=tt_i, t=t, dn_done=dn_done):
                    v.wait_ge(stt.h, dn_done)
                    for half in range(2):
                        xs = X1[:, E * t + 512 * half:E * t + 512 * (half + 1)]
                        last = v.scalar_tensor_tensor(xs, PS[2 * tt_i + half][:, :],
                                                      0.5, xs, ALU.mult, ALU.add)
                    last.then_inc(sv.h, 1)
                em.add("vector", evd)
                sv.n += 1
                x1_done[t] = sv.n
            G.set("banks_dn", sv, sv.n)
            G.set("banks_all", sv, sv.n)

            if write_y:
                y3d = y_out.rearrange("(a p) e -> a p e", p=128)
                for t in tiles:
                    def sty(gps, t=t, seq=seq, n=x1_done[t]):
                        gps.wait_ge(sv.h, n)
                        gps.dma_start(out=y3d[seq * NT + t, :, :],
                                      in_=X1[:, E * t:E * (t + 1)]).then_inc(sy.h, 16)
                    em.add("gpsimd", sty)
                    sy.n += 16

        if sl == 1 and not write_y:
            def acts(a, x1_done=dict(x1_done)):
                for t in range(NT):
                    a.wait_ge(sv.h, x1_done[t])
                    a.activation(junk[:, :], X1[:, E * t:E * (t + 1)], AF.Identity,
                                 accum_out=stats[:, 32 + t:33 + t])
                    a.activation(junk[:, :], X1[:, E * t:E * (t + 1)], AF.Square,
                                 accum_out=stats[:, 48 + t:49 + t])
                a.drain()
                a.activation(junk[:, 0:1], stats[:, 32:33], AF.Identity).then_inc(sa.h, 1)
            em.add("scalar", acts)
            sa.n += 1
            emit_stats_finish(1, sa, sa.n)

        if flush:
            def wv(v):
                v.wait_ge(sv.h, sv.n)
            em.add("vector", wv)
            if write_y:
                def wgf(g):
                    g.wait_ge(sy.h, sy.n)
                em.add("gpsimd", wgf)
            em.flush()


    def emit_attn(seq):
        C = Carver([(WKA, 12288), (XQF, NT * E // 2)])
        ln2t = C.bf16(E)
        ln2f = C.bf16(EC * 128)
        Qr = [C.bf16(EC * 128), C.bf16(EC * 128)]
        KRS, VRS = 3, 4
        Kr = [C.bf16(EC * 192) for _ in range(KRS)]
        Vr = [C.bf16(E) for _ in range(VRS)]
        smask = C.f32(2 * 192)
        pbuf = C.bf16(8 * 384)
        ptr = C.bf16(2 * 384)
        attb = [C.bf16(EC * 128), C.bf16(EC * 128)]
        lsum = C.f32(16)
        rsum = C.f32(16)
        cjunk = C.bf16(E)
        WSq = WSB[:, 0:4096].bitcast(BF16)
        WSk = WSB[:, 4096:8192].bitcast(BF16)
        WSv = WSB[:, 8192:12288].bitcast(BF16)
        WSo = C.bf16(EC * E)

        nm2 = negmb[:, 16:32]
        rs2 = rstdb[:, 16:32]

        w_base = sw.n

        def wld(s):
            for wsrc, wdst in ((qwT_d, WSq), (kwT_d, WSk), (vwT_d, WSv), (owT_d, WSo)):
                for k in range(EC):
                    s.dma_start(out=wdst[:, E * k:E * (k + 1)],
                                in_=wsrc[128 * k:128 * (k + 1), :]).then_inc(sw.h, 16)
        em.add("sync", wld)
        sw.n += 4 * EC * 16
        wq_done = w_base + EC * 16
        wk_done = w_base + 2 * EC * 16
        wv_done = w_base + 3 * EC * 16
        w_done = sw.n

        def vslot(u):
            if u < 0 or u >= NT:
                return zeroV
            return Vr[u % VRS]

        def pbz(v):
            v.memset(bass.AP(pbuf.tensor, pbuf.offset,
                             [list(pbuf.ap[0]), [384, 8], [1, 96]]), 0.0)
            v.memset(bass.AP(pbuf.tensor, pbuf.offset + 288,
                             [list(pbuf.ap[0]), [384, 8], [1, 96]]), 0.0)
            v.drain().then_inc(sv.h, 1)
        em.add("vector", pbz)
        sv.n += 1

        def emit_qkv(t):
            gws = G.snap("c_ln2t")

            def lnq(v, t=t, gws=gws):
                G.apply(v, gws)
                v.tensor_scalar(ln2t, X1[:, E * t:E * (t + 1)], nm2[:, t:t + 1],
                                rs2[:, t:t + 1], ALU.add, ALU.mult).then_inc(sv.h, 1)
            em.add("vector", lnq)
            sv.n += 1
            ln_done = sv.n

            for cc in range(EC):
                bi = 2 + cc % 3 if cc % 3 < 2 else 7
                bank = PS[bi]
                bslot = bank[:, :].bitcast(BF16)[:, 0:128]
                trslot = ("c_tr", bi)
                gwt = G.snap(trslot)

                def tr(te, bslot=bslot, cc=cc, ln_done=ln_done, gwt=gwt):
                    te.wait_ge(sv.h, ln_done)
                    G.apply(te, gwt)
                    te.transpose(bslot, ln2t[:, 128 * cc:128 * (cc + 1)],
                                 ident[:, :]).then_inc(stt.h, 1)
                em.add("tensor", tr)
                stt.n += 1
                trd = stt.n

                def ev(a, bslot=bslot, dst=ln2f[:, 128 * cc:128 * (cc + 1)], trd=trd):
                    a.wait_ge(stt.h, trd)
                    a.copy(dst, bslot).then_inc(sa.h, 1)
                em.add("scalar", ev)
                sa.n += 1
                G.set(trslot, sa, sa.n)
            lnf_done = (sa, sa.n)
            G.set("c_ln2t", stt, stt.n)

            for proj, WS, bias, scale in (("q", WSq, qbc, 0.125), ("k", WSk, kbc, None)):
                for oc in range(EC):
                    bank = 5 + (oc % 2)
                    gwp = G.snap(("c_qkv", bank))

                    def pmm(te, WS=WS, oc=oc, bank=bank, lnf=lnf_done, gwp=gwp,
                            wdn=(wq_done if proj == "q" else wk_done)):
                        te.wait_ge(sw.h, wdn)
                        te.wait_ge(lnf[0].h, lnf[1])
                        G.apply(te, gwp)
                        for k in range(EC):
                            last = te.matmul(PS[bank][:, 0:128],
                                             WS[:, E * k + 128 * oc:E * k + 128 * (oc + 1)],
                                             ln2f[:, 128 * k:128 * (k + 1)],
                                             start=(k == 0), stop=(k == EC - 1))
                        last.then_inc(stt.h, 1)
                    em.add("tensor", pmm)
                    stt.n += 1
                    pdone = stt.n

                    if proj == "q":
                        dst = Qr[t % 2][:, 128 * oc:128 * (oc + 1)]
                        gwr = G.snap(("c_qr", t % 2))

                        def pev(v, dst=dst, oc=oc, bank=bank, pdone=pdone, gwr=gwr):
                            v.wait_ge(stt.h, pdone)
                            G.apply(v, gwr)
                            v.tensor_scalar(dst, PS[bank][:, 0:128], qbc[:, oc:oc + 1],
                                            0.125, ALU.add, ALU.mult).then_inc(sv.h, 1)
                    else:
                        dst = Kr[t % KRS][:, 192 * oc + 32:192 * oc + 160]
                        gwr = G.snap(("c_kr", t % KRS))

                        def pev(v, dst=dst, oc=oc, bank=bank, pdone=pdone, gwr=gwr):
                            v.wait_ge(stt.h, pdone)
                            G.apply(v, gwr)
                            v.tensor_scalar(dst, PS[bank][:, 0:128], kbc[:, oc:oc + 1],
                                            None, ALU.add).then_inc(sv.h, 1)
                    em.add("vector", pev)
                    sv.n += 1
                    G.set(("c_qkv", bank), sv, sv.n)
            for half in range(2):
                bank = 5 + (half % 2)
                gwp = G.snap(("c_qkv", bank))

                def vmm(te, half=half, bank=bank, lnf=lnf_done, gwp=gwp, wdn=wv_done):
                    te.wait_ge(sw.h, wdn)
                    te.wait_ge(lnf[0].h, lnf[1])
                    G.apply(te, gwp)
                    for k in range(EC):
                        last = te.matmul(PS[bank][:, :],
                                         ln2f[:, 128 * k:128 * (k + 1)],
                                         WSv[:, E * k + 512 * half:E * k + 512 * (half + 1)],
                                         start=(k == 0), stop=(k == EC - 1))
                    last.then_inc(stt.h, 1)
                em.add("tensor", vmm)
                stt.n += 1
                vdone = stt.n
                dst = Vr[t % VRS][:, 512 * half:512 * (half + 1)]
                gwv = G.snap(("c_vr", t % VRS))

                def vev(v, dst=dst, bank=bank, half=half, vdone=vdone, gwv=gwv):
                    v.wait_ge(stt.h, vdone)
                    G.apply(v, gwv)
                    v.tensor_tensor(dst, PS[bank][:, :],
                                    vbrep[:, 512 * half:512 * (half + 1)],
                                    ALU.add).then_inc(sv.h, 1)
                em.add("vector", vev)
                sv.n += 1
                G.set(("c_qkv", bank), sv, sv.n)
            kv_done = sv.n

            def kedges(v, t=t, kv_done=kv_done):
                v.wait_ge(sv.h, kv_done)
                st_ = Kr[t % KRS]
                if t == 0:
                    v.memset(bass.AP(st_.tensor, st_.offset,
                                     [list(st_.ap[0]), [192, EC], [1, 32]]), 0.0)
                else:
                    sp = Kr[(t - 1) % KRS]
                    v.tensor_copy(
                        bass.AP(st_.tensor, st_.offset,
                                [list(st_.ap[0]), [192, EC], [1, 32]]),
                        bass.AP(sp.tensor, sp.offset + 128,
                                [list(sp.ap[0]), [192, EC], [1, 32]]))
                    v.tensor_copy(
                        bass.AP(sp.tensor, sp.offset + 160,
                                [list(sp.ap[0]), [192, EC], [1, 32]]),
                        bass.AP(st_.tensor, st_.offset + 32,
                                [list(st_.ap[0]), [192, EC], [1, 32]]))
                v.drain().then_inc(sv.h, 1)
            em.add("vector", kedges)
            sv.n += 1
            return sv.n

        def emit_kpad_end(t):
            def kz(v, t=t, w=sv.n):
                v.wait_ge(sv.h, w)
                st_ = Kr[t % KRS]
                v.memset(bass.AP(st_.tensor, st_.offset + 160,
                                 [list(st_.ap[0]), [192, EC], [1, 32]]), 0.0)
                v.drain().then_inc(sv.h, 1)
            em.add("vector", kz)
            sv.n += 1
            return sv.n

        def emit_att(t, k_ready):
            at = attb[t % 2]
            for b in range(2):
                for hp in range(4 * b, 4 * b + 4):
                    gws = G.snap(("c_s", 0), ("c_s", 1))

                    def smm(te, t=t, hp=hp, k_ready=k_ready, gws=gws):
                        te.wait_ge(sv.h, k_ready)
                        G.apply(te, gws)
                        for h01 in range(2):
                            p0 = 64 * h01
                            te.matmul(PS[h01][:, 0:192],
                                      Qr[t % 2][p0:p0 + 64, 128 * hp:128 * (hp + 1)],
                                      Kr[t % KRS][p0:p0 + 64, 192 * hp:192 * (hp + 1)],
                                      start=True, stop=True).then_inc(stt.h, 1)
                    em.add("tensor", smm)
                    stt.n += 2
                    sdone = stt.n

                    j = hp % 4
                    gwm = G.snap("c_pb" + str(j), "c_sm")

                    def msk(v, hp=hp, sdone=sdone, gwm=gwm):
                        v.wait_ge(stt.h, sdone)
                        G.apply(v, gwm)
                        v.tensor_mask_reduce(smask[:, 0:192], PS[0][:, 0:192],
                                             mst[:, :], men[:, :], 1.0, -1e30,
                                             ALU.max)
                        v.tensor_mask_reduce(smask[:, 192:384], PS[1][:, 0:192],
                                             mst[:, :], men[:, :], 1.0, -1e30,
                                             ALU.max).then_inc(sv.h, 1)
                    em.add("vector", msk)
                    sv.n += 1
                    mdone = sv.n
                    G.set(("c_s", 0), sv, sv.n)
                    G.set(("c_s", 1), sv, sv.n)

                    def ex(a, hp=hp, j=j, mdone=mdone):
                        a.wait_ge(sv.h, mdone)
                        a.activation(pbuf[:, 768 * j + 96:768 * j + 288],
                                     smask[:, 0:192], AF.Exp,
                                     accum_out=lsum[:, 2 * hp:2 * hp + 1])
                        a.activation(pbuf[:, 768 * j + 384 + 96:768 * j + 384 + 288],
                                     smask[:, 192:384], AF.Exp,
                                     accum_out=lsum[:, 2 * hp + 1:2 * hp + 2]
                                     ).then_inc(sa.h, 1)
                    em.add("scalar", ex)
                    sa.n += 1
                    G.set("c_sm", sa, sa.n)
                exp_done = sa.n

                def rc(v, b=b, exp_done=exp_done):
                    v.wait_ge(sa.h, exp_done)
                    v.drain()
                    v.reciprocal(rsum[:, 8 * b:8 * b + 8], lsum[:, 8 * b:8 * b + 8])
                    v.drain().then_inc(sv.h, 1)
                em.add("vector", rc)
                sv.n += 1
                rc_done = sv.n

                for hp in range(4 * b, 4 * b + 4):
                    j = hp % 4

                    def pm(v, hp=hp, j=j, rc_done=rc_done):
                        v.wait_ge(sv.h, rc_done)
                        v.tensor_scalar(pbuf[:, 768 * j + 96:768 * j + 288],
                                        pbuf[:, 768 * j + 96:768 * j + 288],
                                        rsum[:, 2 * hp:2 * hp + 1], None, ALU.mult)
                        v.tensor_scalar(pbuf[:, 768 * j + 384 + 96:768 * j + 384 + 288],
                                        pbuf[:, 768 * j + 384 + 96:768 * j + 384 + 288],
                                        rsum[:, 2 * hp + 1:2 * hp + 2], None,
                                        ALU.mult).then_inc(sv.h, 1)
                    em.add("vector", pm)
                    sv.n += 1
                    pm_done = sv.n

                    for i in range(6):
                        bi = 2 if i % 3 == 0 else (3 if i % 3 == 1 else 7)
                        bank = PS[bi]
                        bkey = ("c_tr", bi)
                        bslot = bank[:, :].bitcast(BF16)[:, 0:128]
                        gwt = G.snap(bkey)

                        def trp(te, j=j, i=i, bslot=bslot, pm_done=pm_done, gwt=gwt):
                            te.wait_ge(sv.h, pm_done)
                            G.apply(te, gwt)
                            te.transpose(bslot, pbuf[:, 768 * j + 128 * i:768 * j + 128 * (i + 1)],
                                         ident[:, :]).then_inc(stt.h, 1)
                        em.add("tensor", trp)
                        stt.n += 1
                        trd = stt.n

                        def evp(a, i=i, bslot=bslot, trd=trd):
                            a.wait_ge(stt.h, trd)
                            a.copy(ptr[:, 128 * i:128 * (i + 1)], bslot).then_inc(sa.h, 1)
                        em.add("scalar", evp)
                        sa.n += 1
                        G.set(bkey, sa, sa.n)
                    ptr_done = (sa, sa.n)

                    gwa = G.snap("c_av")

                    def avm(te, t=t, hp=hp, ptr_done=ptr_done, gwa=gwa):
                        te.wait_ge(ptr_done[0].h, ptr_done[1])
                        G.apply(te, gwa)
                        for h01 in range(2):
                            hh = 2 * hp + h01
                            for i in range(3):
                                vs_ = vslot(t - 1 + i)
                                last = te.matmul(PS[4][64 * h01:64 * h01 + 64, 0:128],
                                                 vs_[:, 64 * hh:64 * hh + 64],
                                                 ptr[:, 384 * h01 + 128 * i:384 * h01 + 128 * (i + 1)],
                                                 start=(i == 0), stop=(i == 2))
                        last.then_inc(stt.h, 1)
                    em.add("tensor", avm)
                    stt.n += 1
                    av_done = stt.n
                    G.set(("c_pb" + str(j)), stt, stt.n)

                    def ave(a, hp=hp, t=t, av_done=av_done):
                        a.wait_ge(stt.h, av_done)
                        a.copy(attb[t % 2][:, 128 * hp:128 * (hp + 1)],
                               PS[4][:, 0:128]).then_inc(sa.h, 1)
                    em.add("scalar", ave)
                    sa.n += 1
                    G.set("c_av", sa, sa.n)
            att_done = sa.n
            G.set(("c_qr", t % 2), stt, stt.n)
            G.set(("c_kr", t % KRS), stt, stt.n)
            for u in (t - 1, t, t + 1):
                if 0 <= u < NT:
                    G.set(("c_vr", u % VRS), stt, stt.n)

            for half in range(2):
                bank = 5 + half
                gwo = G.snap(("c_qkv", bank))

                def omm(te, t=t, half=half, bank=bank, att_done=att_done, gwo=gwo,
                        wdn=w_done):
                    te.wait_ge(sw.h, wdn)
                    te.wait_ge(sa.h, att_done)
                    G.apply(te, gwo)
                    for k in range(EC):
                        last = te.matmul(PS[bank][:, :],
                                         attb[t % 2][:, 128 * k:128 * (k + 1)],
                                         WSo[:, E * k + 512 * half:E * k + 512 * (half + 1)],
                                         start=(k == 0), stop=(k == EC - 1))
                    last.then_inc(stt.h, 1)
                em.add("tensor", omm)
                stt.n += 1
                odone = stt.n

                def oev(v, t=t, half=half, bank=bank, odone=odone):
                    v.wait_ge(stt.h, odone)
                    xs = X1[:, E * t + 512 * half:E * t + 512 * (half + 1)]
                    v.scalar_tensor_tensor(xs, PS[bank][:, :], 1.0, xs,
                                           ALU.mult, ALU.add)
                    v.scalar_tensor_tensor(xs, obrep[:, 512 * half:512 * (half + 1)],
                                           1.0, xs, ALU.mult, ALU.add).then_inc(sv.h, 1)
                em.add("vector", oev)
                sv.n += 1
                G.set(("c_qkv", bank), sv, sv.n)
            x2_done = sv.n

            def st3(a, t=t, x2_done=x2_done):
                a.wait_ge(sv.h, x2_done)
                a.activation(cjunk, X1[:, E * t:E * (t + 1)], AF.Identity,
                             accum_out=stats[:, 64 + t:65 + t])
                a.activation(cjunk, X1[:, E * t:E * (t + 1)], AF.Square,
                             accum_out=stats[:, 80 + t:81 + t]).then_inc(sa.h, 1)
            em.add("scalar", st3)
            sa.n += 1

        import os as _os3
        _natt = int(_os3.environ.get("DBG_ATT_TILES", str(NT)))
        k_ready = {}
        for t in range(_natt):
            k_ready[t - 1] = emit_qkv(t)
            if t >= 1:
                emit_att(t - 1, k_ready[t - 1])
        if _natt < NT:
            return
        k_ready[NT - 1] = emit_kpad_end(NT - 1)
        emit_att(NT - 1, k_ready[NT - 1])

        def sfin(a):
            a.drain()
            a.activation(cjunk[:, 0:1], stats[:, 64:65], AF.Identity).then_inc(sa.h, 1)
        em.add("scalar", sfin)
        sa.n += 1
        emit_stats_finish(2, sa, sa.n)

        def wv(v):
            v.wait_ge(sv.h, sv.n)
        em.add("vector", wv)
        em.flush()

    if DEBUG_STAGE == -2:
        x3d = x_in.rearrange("(a p) e -> a p e", p=128)
        y3d = y_out.rearrange("(a p) e -> a p e", p=128)

        def mv(s):
            s.dma_start(out=X1[:, 0:E], in_=x3d[0, :, :]).then_inc(sx.h, 16)
        em.add("sync", mv)
        sx.n += 16
        n0 = sx.n

        def cp(v):
            v.wait_ge(sx.h, n0)
            v.tensor_scalar_mul(X1[:, E:2 * E], X1[:, 0:E], 2.0).then_inc(sv.h, 1)
        em.add("vector", cp)
        sv.n += 1
        em.flush()
        n1 = sv.n

        def out(g):
            g.wait_ge(sv.h, n1)
            g.dma_start(out=y3d[0, :, :], in_=X1[:, E:2 * E]).then_inc(sy.h, 16)
            g.wait_ge(sy.h, 16)
        em.add("gpsimd", out)
        sy.n += 16
        em.flush()
        es.close()
        mybir.codegen_inst_isa_subclasses(nc)
        return nc
    import os as _os2
    _tap = _os2.environ.get("DBG_TAP", "")
    emit_preproc()
    if _tap == "pre":
        emit_pre(0)
        y3d = y_out.rearrange("(a p) e -> a p e", p=128)
        n0 = sv.n + sa.n

        def dmp(g, wv=sv.n, wa=sa.n):
            g.wait_ge(sv.h, wv)
            g.wait_ge(sa.h, wa)
            g.dma_start(out=y3d[0, :, 0:96], in_=stats[:, :]).then_inc(sy.h, 16)
            g.dma_start(out=y3d[0, :, 96:144], in_=negmb[:, :]).then_inc(sy.h, 16)
            g.dma_start(out=y3d[0, :, 144:192], in_=rstdb[:, :]).then_inc(sy.h, 16)
            g.dma_start(out=y3d[1, :, :], in_=X1[:, 0:E]).then_inc(sy.h, 16)
            g.wait_ge(sy.h, 64)
        em.add("gpsimd", dmp)
        sy.n += 64
        em.flush()
        es.close()
        mybir.codegen_inst_isa_subclasses(nc)
        return nc
    if DEBUG_STAGE == -1:
        em.flush()
        es.close()
        mybir.codegen_inst_isa_subclasses(nc)
        return nc
    if DEBUG_STAGE == 0:
        dbg = WKA[:, 0:1408].bitcast(BF16)
        dbgf = WSB[:, 0:2816]
        y3d = y_out.rearrange("(a p) e -> a p e", p=128)
        base_sy = sy.n

        def dget(s, w=sg.n):
            s.wait_ge(sg.h, w)
            s.dma_start(out=dbg[:, 0:2816], in_=wgT_d[0:128, :]).then_inc(sx.h, 16)
        em.add("sync", dget)
        sx.n += 16
        n0 = sx.n

        def dcast(v, n0=n0):
            v.wait_ge(sx.h, n0)
            v.tensor_copy(dbgf[:, 0:2816], dbg[:, 0:2816]).then_inc(sv.h, 1)
        em.add("vector", dcast)
        sv.n += 1
        n1 = sv.n

        def dout(g, n1=n1):
            g.wait_ge(sv.h, n1)
            for j in range(2):
                g.dma_start(out=y3d[j, :, :], in_=dbgf[:, 1024 * j:1024 * (j + 1)]).then_inc(sy.h, 16)
        em.add("gpsimd", dout)
        sy.n += 32
        n2 = sy.n

        def dget2(s, n2=n2):
            s.wait_ge(sy.h, n2)
            s.dma_start(out=dbg[:, 0:1024], in_=wdT_d[0:128, :]).then_inc(sx.h, 16)
        em.add("sync", dget2)
        sx.n += 16
        n3 = sx.n

        def dcast2(v, n3=n3):
            v.wait_ge(sx.h, n3)
            v.tensor_copy(dbgf[:, 0:1024], dbg[:, 0:1024]).then_inc(sv.h, 1)
        em.add("vector", dcast2)
        sv.n += 1
        n4 = sv.n
        fin_sy = sy.n + 16

        def dout2(g, n4=n4, fin_sy=fin_sy):
            g.wait_ge(sv.h, n4)
            g.dma_start(out=y3d[4, :, :], in_=dbgf[:, 0:1024]).then_inc(sy.h, 16)
            g.wait_ge(sy.h, fin_sy)
        em.add("gpsimd", dout2)
        sy.n += 16
        em.flush()
        es.close()
        mybir.codegen_inst_isa_subclasses(nc)
        return nc
    for seq in range(NSEQ):
        emit_pre(seq)
        xqf_ready.clear()
        emit_ffn(seq, 0, 0, False, flush=False)
        emit_ffn(seq, 1, 0, DEBUG_STAGE == 1)
        if DEBUG_STAGE >= 2:
            emit_attn(seq)
            if DEBUG_STAGE >= 3:
                emit_ffn(seq, 0, 2, False, flush=False)
                emit_ffn(seq, 1, 2, True)
            if DEBUG_STAGE == 2:
                y3d = y_out.rearrange("(a p) e -> a p e", p=128)
                wsv = sv.n

                def dx2(g, seq=seq, wsv=wsv):
                    g.wait_ge(sv.h, wsv)
                    for t in range(NT):
                        g.dma_start(out=y3d[seq * NT + t, :, :],
                                    in_=X1[:, E * t:E * (t + 1)]).then_inc(sy.h, 16)
                em.add("gpsimd", dx2)
                sy.n += NT * 16

                def wgf(g):
                    g.wait_ge(sy.h, sy.n)
                em.add("gpsimd", wgf)
                em.flush()

    es.close()
    mybir.codegen_inst_isa_subclasses(nc)
    return nc


_PROG = None


def kernel(**inputs):
    global _PROG
    x = np.ascontiguousarray(np.asarray(inputs["L0_attn_hidden_states"], dtype=np.float32))
    ident = np.eye(128, dtype=BF)
    mstart = np.arange(128, dtype=np.float32).reshape(128, 1)
    mend = mstart + 65.0

    if _PROG is None:
        _PROG = build_program()
    nc = _PROG

    base = {
        "wg": np.ascontiguousarray(np.asarray(inputs["L0_ffn_w_gate"], dtype=np.float32)),
        "wu": np.ascontiguousarray(np.asarray(inputs["L0_ffn_w_up"], dtype=np.float32)),
        "wd": np.ascontiguousarray(np.asarray(inputs["L0_ffn_w_down"], dtype=np.float32)),
        "qw": np.ascontiguousarray(np.asarray(inputs["L0_attn_q_w"], dtype=np.float32)),
        "kw": np.ascontiguousarray(np.asarray(inputs["L0_attn_k_w"], dtype=np.float32)),
        "vw": np.ascontiguousarray(np.asarray(inputs["L0_attn_v_w"], dtype=np.float32)),
        "ow": np.ascontiguousarray(np.asarray(inputs["L0_attn_o_w"], dtype=np.float32)),
        "qb": np.ascontiguousarray(np.asarray(inputs["L0_attn_q_b"], dtype=np.float32)),
        "kb": np.ascontiguousarray(np.asarray(inputs["L0_attn_k_b"], dtype=np.float32)),
        "vb": np.ascontiguousarray(np.asarray(inputs["L0_attn_v_b"], dtype=np.float32)),
        "ob": np.ascontiguousarray(np.asarray(inputs["L0_attn_o_b"], dtype=np.float32)),
        "ident": ident, "mstart": mstart, "mend": mend,
    }
    in_maps = []
    for c in range(NCORES):
        m = dict(base)
        m["x"] = np.ascontiguousarray(x[c * NSEQ:(c + 1) * NSEQ].reshape(NSEQ * SEQ, E))
        in_maps.append(m)
    res = run_bass_kernel_spmd(nc, in_maps, core_ids=list(range(NCORES)))
    out = np.concatenate([r["y"].reshape(NSEQ, SEQ, E) for r in res.results], axis=0)
    return out


# version 05
# speedup vs baseline: 1.2058x; 1.0036x over previous
import contextlib
import numpy as np
import ml_dtypes

import concourse.bass as bass
import concourse.mybir as mybir
from concourse.bass_utils import run_bass_kernel_spmd
from concourse import dve_ops as dvo
from concourse.dve_op import DveOp
from concourse.dve_spec import Spec, Src0, C0, C1, Zero, maxx, minn, lower
from concourse.dve_table_gen import dve_ver_for

F32 = mybir.dt.float32
BF16 = mybir.dt.bfloat16
ALU = mybir.AluOpType
AF = mybir.ActivationFunctionType
AX = mybir.AxisListType
BF = ml_dtypes.bfloat16

MAGIC = float(np.float32(1.5 * 2 ** 23))
DEKC = float(np.float32(2 ** 22 + 1))

NCORES = 8
B = 16
SEQ = 2048
E = 1024
EC = 8
I = 2816
NS = 2
IH = I // NS
ICH = IH // 128
NT = SEQ // 128
NSEQ = B // NCORES
NHEAD = 16
HD = 64
EPS = 1e-5

DEBUG_STAGE = 3


def _refA(in0, in1, s0, s1, imm2):
    f = np.float64 if in0.dtype == np.float64 else np.float32
    r1 = ((in0.astype(f) + np.array(s1, f)) - np.array(s1, f)).astype(f)
    return np.clip(r1, -s0, s0).astype(f)


def _refB(in0, in1, s0, s1, imm2):
    an32 = in0.astype(np.float32)
    w = (an32 * np.float32(s1)).astype(np.float32)
    t = (w - an32).astype(np.float32)
    D = (w - t).astype(in0.dtype if in0.dtype == np.float64 else np.float32)
    return (D - np.clip(D, -s0, s0)).astype(D.dtype)


def register_ops():
    ver = dve_ver_for("TRN2")
    out = []
    _r1 = (Src0 + C1) - C1
    specA = Spec(body=maxx(minn(_r1, C0), Zero - C0))
    _w = Src0 * C1
    _D = _w - (_w - Src0)
    specB = Spec(body=_D - maxx(minn(_D, C0), Zero - C0))
    for name, spec, ref in [("ANT_FP4_A", specA, _refA), ("ANT_FP4_B", specB, _refB)]:
        ex = next((o for o in dvo.OPS if o.name == name), None)
        if ex is not None:
            out.append(ex)
            continue
        compiled = lower(spec, ver=ver, perf_en=False, name=name)
        op = DveOp(name, spec, subdim=False, uops_sha={ver: compiled.sha(ver)},
                   reference=ref)
        row = dvo._CUSTOM_DVE_ROW_BASE + len(dvo.OPS)
        assert row < 0x20
        dvo.OPS.append(op)
        dvo._SUB_OPCODE_FOR_NAME[name] = row
        dvo.CUSTOM_DVE_SPECS[name] = spec
        dvo.CUSTOM_DVE_REFS[name] = ref
        out.append(op)
    return out


OPA, OPB = register_ops()


class Sem:
    def __init__(self, h):
        self.h = h
        self.n = 0

    def inc(self, inst, k=1):
        inst.then_inc(self.h, k)
        self.n += k
        return self.n


class Emitter:
    ENGINES = ("sync", "tensor", "vector", "scalar", "gpsimd")

    def __init__(self, nc):
        self.nc = nc
        self.new_phase()

    def new_phase(self):
        self.ops = {e: [] for e in self.ENGINES}

    def add(self, eng, fn):
        self.ops[eng].append(fn)

    def flush(self):
        ops = self.ops
        with self.nc.Block() as block:
            @block.sync
            def _(eng, ops=ops):
                for f in ops["sync"]:
                    f(eng)

            @block.tensor
            def _(eng, ops=ops):
                for f in ops["tensor"]:
                    f(eng)

            @block.vector
            def _(eng, ops=ops):
                for f in ops["vector"]:
                    f(eng)

            @block.scalar
            def _(eng, ops=ops):
                for f in ops["scalar"]:
                    f(eng)

            @block.gpsimd
            def _(eng, ops=ops):
                for f in ops["gpsimd"]:
                    f(eng)
        self.new_phase()


class Carver:
    def __init__(self, pools):
        self.pools = pools
        self.pos = [0] * len(pools)

    def f32(self, cols):
        for i, (t, n) in enumerate(self.pools):
            if self.pos[i] + cols <= n:
                off = self.pos[i]
                self.pos[i] += cols
                return t[:, off:off + cols]
        raise RuntimeError(f"carver OOM: need {cols}, pos={self.pos}")

    def bf16(self, cols):
        assert cols % 2 == 0
        return self.f32(cols // 2).bitcast(BF16)


def bcast16(ap, nblocks):
    return bass.AP(ap.tensor, ap.offset,
                   [list(ap.ap[0]), [list(ap.ap[1])[0], nblocks], [0, 16]])


class Guards:

    def __init__(self):
        self.g = {}

    def set(self, key, sem, n):
        self.g[key] = (sem, n)

    def snap(self, *keys):
        best = {}
        for k in keys:
            if k in self.g:
                s, n = self.g[k]
                if best.get(id(s), (None, -1))[1] < n:
                    best[id(s)] = (s, n)
        return tuple(best.values())

    @staticmethod
    def apply(eng, snap):
        for s, n in snap:
            if n > 0:
                eng.wait_ge(s.h, n)


def build_program():
    import os as _os
    nc = bass.Bass("TRN2", target_bir_lowering=False,
                   detect_race_conditions=not _os.environ.get("NORACE"))

    x_in = nc.dram_tensor("x", [NSEQ * SEQ, E], F32, kind="ExternalInput").ap()
    y_out = nc.dram_tensor("y", [NSEQ * SEQ, E], F32, kind="ExternalOutput").ap()
    wg_in = nc.dram_tensor("wg", [I, E], F32, kind="ExternalInput").ap()
    wu_in = nc.dram_tensor("wu", [I, E], F32, kind="ExternalInput").ap()
    wd_in = nc.dram_tensor("wd", [E, I], F32, kind="ExternalInput").ap()
    qw_in = nc.dram_tensor("qw", [E, E], F32, kind="ExternalInput").ap()
    kw_in = nc.dram_tensor("kw", [E, E], F32, kind="ExternalInput").ap()
    vw_in = nc.dram_tensor("vw", [E, E], F32, kind="ExternalInput").ap()
    ow_in = nc.dram_tensor("ow", [E, E], F32, kind="ExternalInput").ap()
    qb_in = nc.dram_tensor("qb", [E], F32, kind="ExternalInput").ap()
    kb_in = nc.dram_tensor("kb", [E], F32, kind="ExternalInput").ap()
    vb_in = nc.dram_tensor("vb", [E], F32, kind="ExternalInput").ap()
    ob_in = nc.dram_tensor("ob", [E], F32, kind="ExternalInput").ap()
    ident_in = nc.dram_tensor("ident", [128, 128], BF16, kind="ExternalInput").ap()
    mst_in = nc.dram_tensor("mstart", [128, 1], F32, kind="ExternalInput").ap()
    men_in = nc.dram_tensor("mend", [128, 1], F32, kind="ExternalInput").ap()

    wgT_d = nc.dram_tensor("wgT", [EC * 128, I], BF16, kind="Internal").ap()
    wuT_d = nc.dram_tensor("wuT", [EC * 128, I], BF16, kind="Internal").ap()
    wdT_d = nc.dram_tensor("wdT", [I, E], BF16, kind="Internal").ap()
    qwT_d = nc.dram_tensor("qwT", [EC * 128, E], BF16, kind="Internal").ap()
    kwT_d = nc.dram_tensor("kwT", [EC * 128, E], BF16, kind="Internal").ap()
    vwT_d = nc.dram_tensor("vwT", [EC * 128, E], BF16, kind="Internal").ap()
    owT_d = nc.dram_tensor("owT", [EC * 128, E], BF16, kind="Internal").ap()
    wT_scratch = {id(wg_in): wgT_d, id(wu_in): wuT_d, id(wd_in): wdT_d,
                  id(qw_in): qwT_d, id(kw_in): kwT_d, id(vw_in): vwT_d,
                  id(ow_in): owT_d}

    es = contextlib.ExitStack()

    def sbt(name, shape, dtype):
        return es.enter_context(nc.sbuf_tensor(name, shape, dtype))

    ident = sbt("identt", [128, 128], BF16)
    mst = sbt("mstt", [128, 1], F32)
    men = sbt("ment", [128, 1], F32)
    qbc = sbt("qbc", [128, EC], F32)
    kbc = sbt("kbc", [128, EC], F32)
    vbrep = sbt("vbrep", [128, E], F32)
    obrep = sbt("obrep", [128, E], F32)
    zeroV = sbt("zeroV", [128, E], BF16)
    stats = sbt("stats", [128, 96], F32)
    negmb = sbt("negmb", [128, 48], F32)
    rstdb = sbt("rstdb", [128, 48], F32)
    tinyw = sbt("tinyw", [128, 16], F32)

    X1 = sbt("X1", [128, NT * E], F32)
    XQF = sbt("XQF", [128, NT * E // 2], F32)
    WSB = sbt("WSB", [128, 12288], F32)
    WKA = sbt("WKA", [128, 12288], F32)
    junk = WKA[:, 0:512].bitcast(BF16)
    brow = None

    PS = [es.enter_context(nc.psum_tensor(f"ps{i}", [128, 512], F32))
          for i in range(8)]

    sx = Sem(es.enter_context(nc.semaphore("sx")))
    sw = Sem(es.enter_context(nc.semaphore("sw")))
    sv = Sem(es.enter_context(nc.semaphore("sv")))
    sa = Sem(es.enter_context(nc.semaphore("sa")))
    stt = Sem(es.enter_context(nc.semaphore("stt")))
    sg = Sem(es.enter_context(nc.semaphore("sg")))
    sy = Sem(es.enter_context(nc.semaphore("sy")))

    em = Emitter(nc)
    G = Guards()

    def emit_quant(vec_ops, src, n, scl, q2l, hi, q_out, inc=None):
        nb = n // 16
        bm, rec, bs = scl[:, 0:nb], scl[:, nb:2 * nb], scl[:, 2 * nb:3 * nb]

        def f(v, src=src, nb=nb, bm=bm, rec=rec, bs=bs, q2l=q2l, hi=hi,
              q_out=q_out, inc=inc):
            s3 = src.rearrange("p (s x) -> p s x", x=16)
            v.tensor_reduce(bm, s3, AX.X, ALU.max, apply_absolute_value=True)
            v.tensor_scalar_max(bm, bm, 6e-12)
            v.reciprocal(rec, bm)
            v.drain()
            v.tensor_scalar_mul(rec, rec, 12.0)
            v.tensor_scalar_mul(bs, bm, float(np.float32(1.0 / 12.0)))
            v.tensor_tensor(s3, s3, bcast16(rec, nb), ALU.mult)
            v._custom_dve(OPA, out=q2l, in0=src, s0=4.0, s1=MAGIC)
            v._custom_dve(OPB, out=hi, in0=src, s0=4.0, s1=DEKC)
            v.tensor_add(q2l, q2l, hi)
            last = v.tensor_tensor(q_out.rearrange("p (s x) -> p s x", x=16),
                                   q2l.rearrange("p (s x) -> p s x", x=16),
                                   bcast16(bs, nb), ALU.mult)
            if inc is not None:
                last.then_inc(inc.h, 1)
        vec_ops.append(f)

    def emit_stats_finish(slot, wait_sem, wait_n):
        sc = 32 * slot
        nsl = 16 * slot

        def fin(v):
            v.wait_ge(wait_sem.h, wait_n)
            v.tensor_scalar_mul(negmb[:, nsl:nsl + 16], stats[:, sc:sc + 16], -1.0 / E)
            v.drain()
            v.tensor_mul(tinyw[:, 0:16], negmb[:, nsl:nsl + 16], negmb[:, nsl:nsl + 16])
            v.tensor_scalar(rstdb[:, nsl:nsl + 16], stats[:, sc + 16:sc + 32],
                            1.0 / E, EPS, ALU.mult, ALU.add)
            v.drain()
            v.tensor_sub(rstdb[:, nsl:nsl + 16], rstdb[:, nsl:nsl + 16],
                         tinyw[:, 0:16]).then_inc(sv.h, 1)
        em.add("vector", fin)
        sv.n += 1
        var_done = sv.n

        def sq(a):
            a.wait_ge(sv.h, var_done)
            a.activation(tinyw[:, 0:16], rstdb[:, nsl:nsl + 16], AF.Ln)
            a.drain()
            a.activation(rstdb[:, nsl:nsl + 16], tinyw[:, 0:16], AF.Exp,
                         scale=-0.5).then_inc(sa.h, 1)
        em.add("scalar", sq)
        sa.n += 1
        return sa.n

    def emit_preproc():
        C = Carver([(WKA, 12288), (WSB, 12288)])
        browt = C.f32(E)
        brow = browt

        import os
        _dbg_skip = set(os.environ.get("DBGSKIP", "").split(","))
        _loadlist = []
        if "ident" not in _dbg_skip:
            _loadlist.append(lambda s: s.dma_start(out=ident[:, :], in_=ident_in))
        if "mst" not in _dbg_skip:
            _loadlist.append(lambda s: s.dma_start(out=mst[:, :], in_=mst_in))
            _loadlist.append(lambda s: s.dma_start(out=men[:, :], in_=men_in))
        if "qbc" not in _dbg_skip:
            _loadlist.append(lambda s: s.dma_start(out=qbc[:, :], in_=qb_in.rearrange("(c p) -> p c", p=128), allow_slow_non_contiguous=True))
            _loadlist.append(lambda s: s.dma_start(out=kbc[:, :], in_=kb_in.rearrange("(c p) -> p c", p=128), allow_slow_non_contiguous=True))
        if "rep" not in _dbg_skip:
            _loadlist.append(lambda s: s.dma_start(out=vbrep[:, :], in_=bass.AP(vb_in.tensor, 0, [[0, 128], [1, E]])))
            _loadlist.append(lambda s: s.dma_start(out=obrep[:, :], in_=bass.AP(ob_in.tensor, 0, [[0, 128], [1, E]])))

        def loads(s):
            for f in _loadlist:
                f(s).then_inc(sx.h, 16)
        em.add("sync", loads)
        sx.n += len(_loadlist) * 16

        def zv(v):
            v.memset(zeroV[:, :], 0.0).then_inc(sv.h, 1)
        em.add("vector", zv)
        sv.n += 1

        wtmp = [C.f32(I), C.f32(I)]
        wq = [C.bf16(I), C.bf16(I)]
        wscl = C.f32(3 * (I // 16) + 16)
        wq2l = C.bf16(I)
        whi = C.bf16(I)
        stg = [C.bf16(512), C.bf16(512)]

        jobs = [(wg_in, I, E, True), (wu_in, I, E, True), (wd_in, E, I, True),
                (qw_in, E, E, False), (kw_in, E, E, False),
                (vw_in, E, E, False), (ow_in, E, E, False)]
        chunks = []
        for (din, rows, cols, doq) in jobs:
            for r in range(rows // 128):
                chunks.append((din, cols, doq, r))
        import os
        _lim = int(os.environ.get("DBG_NCHUNKS", "999"))
        chunks = chunks[:_lim]

        for idx, (din, cols, doq, r) in enumerate(chunks):
            rot = idx % 2
            wt = wtmp[rot][:, 0:cols]
            wqb = wq[rot][:, 0:cols]
            dout = wT_scratch[id(din)]

            gw = G.snap(("pp_wt", rot))

            def ld(s, din=din, cols=cols, r=r, wt=wt, gw=gw):
                G.apply(s, gw)
                s.dma_start(out=wt, in_=din[128 * r:128 * (r + 1), 0:cols]).then_inc(sx.h, 16)
            em.add("sync", ld)
            sx.n += 16
            x_after = sx.n

            vec_ops = []

            gw2 = G.snap(("pp_wq", rot))

            def wv(v, x_after=x_after, gw2=gw2):
                v.wait_ge(sx.h, x_after)
                G.apply(v, gw2)
            vec_ops.append(wv)
            if doq:
                emit_quant(vec_ops, wt, cols, wscl, wq2l[:, 0:cols], whi[:, 0:cols],
                           wqb, inc=sv)
            else:
                vec_ops.append(lambda v, wt=wt, wqb=wqb:
                               v.tensor_copy(wqb, wt).then_inc(sv.h, 1))
            for f in vec_ops:
                em.add("vector", f)
            sv.n += 1
            q_done = sv.n
            G.set(("pp_wt", rot), sv, q_done)

            ncc = cols // 128
            for cc in range(ncc):
                slot = cc % 4
                bank = PS[4 + slot]
                bslot = bank[:, :].bitcast(BF16)[:, 0:128]

                gw3 = G.snap(("pp_ps", slot))

                def tr(t, wqb=wqb, cc=cc, bslot=bslot, q_done=q_done, gw3=gw3):
                    t.wait_ge(sv.h, q_done)
                    G.apply(t, gw3)
                    t.transpose(bslot, wqb[:, 128 * cc:128 * (cc + 1)],
                                ident[:, :]).then_inc(stt.h, 1)
                em.add("tensor", tr)
                stt.n += 1
                trd = stt.n

                srot = (cc // 4) % 2
                sslot = stg[srot][:, 128 * (cc % 4):128 * (cc % 4) + 128]

                gw4 = G.snap(("pp_stg", srot, cc % 4))

                def ev(a, bslot=bslot, sslot=sslot, trd=trd, gw4=gw4):
                    a.wait_ge(stt.h, trd)
                    G.apply(a, gw4)
                    a.copy(sslot, bslot).then_inc(sa.h, 1)
                em.add("scalar", ev)
                sa.n += 1
                G.set(("pp_ps", slot), sa, sa.n)
                ev_done = sa.n

                def out(g, dout=dout, cc=cc, r=r, sslot=sslot, ev_done=ev_done):
                    g.wait_ge(sa.h, ev_done)
                    g.dma_start(out=dout[128 * cc:128 * (cc + 1), 128 * r:128 * (r + 1)],
                                in_=sslot).then_inc(sg.h, 16)
                em.add("gpsimd", out)
                sg.n += 16
                G.set(("pp_stg", srot, cc % 4), sg, sg.n)
            G.set(("pp_wq", rot), stt, stt.n)

        def fin(g):
            g.wait_ge(sg.h, sg.n)
        em.add("gpsimd", fin)
        em.flush()

    def emit_pre(seq):
        x3d = x_in.rearrange("(a p) e -> a p e", p=128)

        def ld(s, seq=seq):
            for t in range(NT):
                s.dma_start(out=X1[:, E * t:E * (t + 1)],
                            in_=x3d[seq * NT + t, :, :]).then_inc(sx.h, 16)
        em.add("sync", ld)
        sx.n += NT * 16
        x_done = sx.n

        def acts(a, x_done=x_done):
            for t in range(NT):
                a.wait_ge(sx.h, x_done - (NT - 1 - t) * 16)
                a.activation(junk[:, :], X1[:, E * t:E * (t + 1)], AF.Identity,
                             accum_out=stats[:, t:t + 1])
                a.activation(junk[:, :], X1[:, E * t:E * (t + 1)], AF.Square,
                             accum_out=stats[:, 16 + t:17 + t])
            a.drain()
            a.activation(junk[:, 0:1], stats[:, 0:1], AF.Identity).then_inc(sa.h, 1)
        em.add("scalar", acts)
        sa.n += 1
        emit_stats_finish(0, sa, sa.n)
        em.flush()

    xqf_ready = {}

    def emit_ffn(seq, sl, ln_slot, write_y, flush=True):
        C = Carver([(WKA, 12288)])
        lnt = C.f32(E)
        xscl = C.f32(3 * (E // 16))
        q2l = C.bf16(E)
        xqtm = C.bf16(E)
        gsil = C.f32(IH)
        hfull = C.f32(IH)
        hscl = C.f32(3 * (IH // 16))
        q2lh = C.bf16(IH)
        hih = C.bf16(IH)
        hqf = [C.bf16(ICH * 128) for _ in range(4)]
        wdb = [C.bf16(E), C.bf16(E)]

        WG = WSB[:, 0:EC * IH // 2].bitcast(BF16)
        WU = WSB[:, EC * IH // 2:EC * IH].bitcast(BF16)
        nm = negmb[:, 16 * ln_slot:16 * ln_slot + 16]
        rs = rstdb[:, 16 * ln_slot:16 * ln_slot + 16]

        nchunks = [(0, 512), (512, 512), (1024, 384)]

        w_base = sw.n
        gwq_prev = G.snap("wsb_rd")

        def wld(s, sl=sl, gwq_prev=gwq_prev):
            G.apply(s, gwq_prev)
            for coff, clen in nchunks:
                for W, wT in ((WG, wgT_d), (WU, wuT_d)):
                    for k in range(EC):
                        s.dma_start(out=W[:, IH * k + coff:IH * k + coff + clen],
                                    in_=wT[128 * k:128 * (k + 1),
                                           IH * sl + coff:IH * sl + coff + clen]
                                    ).then_inc(sw.h, 16)
        em.add("sync", wld)
        sw.n += len(nchunks) * 2 * EC * 16
        wg_need = [w_base + 16 * (16 * ci + 8) for ci in range(len(nchunks))]
        wu_need = [w_base + 16 * (16 * ci + 16) for ci in range(len(nchunks))]
        hq_ready = {}
        x1_done = {}

        for g in range(4):
            tiles = [4 * g + j for j in range(4)]
            for tt_i, t in enumerate(tiles):
                xt = X1[:, E * t:E * (t + 1)]
                xqf_t = XQF[:, (E // 2) * t:(E // 2) * (t + 1)].bitcast(BF16)

                if sl == 0:
                    vec_ops = []

                    gwl = G.snap("lnt")

                    def lnq(v, xt=xt, t=t, gwl=gwl):
                        G.apply(v, gwl)
                        v.tensor_scalar(lnt, xt, nm[:, t:t + 1], rs[:, t:t + 1],
                                        ALU.add, ALU.mult)
                    vec_ops.append(lnq)
                    emit_quant(vec_ops, lnt, E, xscl, q2l, xqtm, xqtm, inc=sv)
                    for f in vec_ops:
                        em.add("vector", f)
                    sv.n += 1
                    xq_done = sv.n
                    G.set("lnt", sv, xq_done)

                    for cc in range(EC):
                        slot = cc % 4
                        bank = PS[4 + slot]
                        bslot = bank[:, :].bitcast(BF16)[:, 0:128]

                        gwt = G.snap(("f_ps", slot), "banks_all")

                        def tr(te, bslot=bslot, cc=cc, xq_done=xq_done, gwt=gwt):
                            te.wait_ge(sv.h, xq_done)
                            G.apply(te, gwt)
                            te.transpose(bslot, xqtm[:, 128 * cc:128 * (cc + 1)],
                                         ident[:, :]).then_inc(stt.h, 1)
                        em.add("tensor", tr)
                        stt.n += 1
                        trd = stt.n

                        def ev(a, bslot=bslot, dst=xqf_t[:, 128 * cc:128 * (cc + 1)], trd=trd):
                            a.wait_ge(stt.h, trd)
                            a.copy(dst, bslot).then_inc(sa.h, 1)
                        em.add("scalar", ev)
                        sa.n += 1
                        G.set(("f_ps", slot), sa, sa.n)
                    xqf_ready[t] = (sa, sa.n)

                xr_sem, xr_n = xqf_ready[t]

                for ci, (coff, clen) in enumerate(nchunks):
                    gb = PS[ci % 2]
                    ub = PS[2 + ci % 2]

                    gwg = G.snap(("gu_g", ci % 2), ("gu_u", ci % 2), "banks_all")

                    def gu(te, coff=coff, clen=clen, gb=gb, ub=ub, xqf_t=xqf_t,
                           xr_sem=xr_sem, xr_n=xr_n, gwg=gwg,
                           wgn=wg_need[ci], wun=wu_need[ci]):
                        te.wait_ge(sw.h, wgn)
                        te.wait_ge(xr_sem.h, xr_n)
                        G.apply(te, gwg)
                        for k in range(EC):
                            te.matmul(gb[:, 0:clen], xqf_t[:, 128 * k:128 * (k + 1)],
                                      WG[:, IH * k + coff:IH * k + coff + clen],
                                      start=(k == 0), stop=(k == EC - 1))
                        te.wait_ge(sw.h, wun)
                        for k in range(EC):
                            last = te.matmul(ub[:, 0:clen], xqf_t[:, 128 * k:128 * (k + 1)],
                                             WU[:, IH * k + coff:IH * k + coff + clen],
                                             start=(k == 0), stop=(k == EC - 1))
                        last.then_inc(stt.h, 1)
                    em.add("tensor", gu)
                    stt.n += 1
                    gu_done = stt.n
                    G.set("wsb_rd", stt, stt.n)

                    def sil(a, gb=gb, coff=coff, clen=clen, gu_done=gu_done):
                        a.wait_ge(stt.h, gu_done)
                        a.activation(gsil[:, coff:coff + clen], gb[:, 0:clen],
                                     AF.Silu).then_inc(sa.h, 1)
                    em.add("scalar", sil)
                    sa.n += 1
                    G.set(("gu_g", ci % 2), sa, sa.n)
                    sil_done = sa.n

                    def hm(v, ub=ub, coff=coff, clen=clen, sil_done=sil_done):
                        v.wait_ge(sa.h, sil_done)
                        v.tensor_mul(hfull[:, coff:coff + clen],
                                     gsil[:, coff:coff + clen],
                                     ub[:, 0:clen]).then_inc(sv.h, 1)
                    em.add("vector", hm)
                    sv.n += 1
                    G.set(("gu_u", ci % 2), sv, sv.n)

                vec_ops = []
                gwh = G.snap(("hq_b", 0), ("hq_b", 1))
                vec_ops.append(lambda v, gwh=gwh: G.apply(v, gwh))
                emit_quant(vec_ops, hfull, IH, hscl, q2lh, hih, hih, inc=sv)
                for f in vec_ops:
                    em.add("vector", f)
                sv.n += 1
                hq_done = sv.n
                G.set(("hq_b", 0), sv, hq_done)
                G.set(("hq_b", 1), sv, hq_done)

                for cc in range(ICH):
                    slot = cc % 4
                    bank = PS[4 + slot]
                    bslot = bank[:, :].bitcast(BF16)[:, 0:128]

                    gwt2 = G.snap(("f_ps", slot), "banks_all")

                    def tr(te, bslot=bslot, cc=cc, hq_done=hq_done, gwt2=gwt2):
                        te.wait_ge(sv.h, hq_done)
                        G.apply(te, gwt2)
                        te.transpose(bslot, hih[:, 128 * cc:128 * (cc + 1)],
                                     ident[:, :]).then_inc(stt.h, 1)
                    em.add("tensor", tr)
                    stt.n += 1
                    trd = stt.n

                    def ev(a, bslot=bslot, dst=hqf[tt_i][:, 128 * cc:128 * (cc + 1)], trd=trd):
                        a.wait_ge(stt.h, trd)
                        a.copy(dst, bslot).then_inc(sa.h, 1)
                    em.add("scalar", ev)
                    sa.n += 1
                    G.set(("f_ps", slot), sa, sa.n)
                hq_ready[tt_i] = (sa, sa.n)

            hq_sem, hq_n = hq_ready[3]
            for c in range(ICH):
                rot = c % 2

                gww = G.snap(("wd_b", rot))

                def wdl(s, c=c, rot=rot, sl=sl, gww=gww):
                    G.apply(s, gww)
                    s.dma_start(out=wdb[rot],
                                in_=wdT_d[128 * (ICH * sl + c):128 * (ICH * sl + c + 1), :]
                                ).then_inc(sw.h, 16)
                em.add("sync", wdl)
                sw.n += 16
                wd_ld = sw.n

                gwd = G.snap("banks_dn", ("f_ps", 0), ("f_ps", 1), ("f_ps", 2),
                             ("f_ps", 3), ("f_ps", 4), ("f_ps", 5), ("f_ps", 6),
                             ("f_ps", 7), ("gu_g", 0), ("gu_g", 1),
                             ("gu_u", 0), ("gu_u", 1)) if c == 0 else ()

                def dn(te, c=c, rot=rot, hq_n=hq_n, wd_ld=wd_ld, gwd=gwd):
                    te.wait_ge(hq_sem.h, hq_n)
                    te.wait_ge(sw.h, wd_ld)
                    G.apply(te, gwd)
                    for tt_i in range(4):
                        for half in range(2):
                            last = te.matmul(PS[2 * tt_i + half][:, :],
                                             hqf[tt_i][:, 128 * c:128 * (c + 1)],
                                             wdb[rot][:, 512 * half:512 * (half + 1)],
                                             start=(c == 0), stop=(c == ICH - 1))
                    last.then_inc(stt.h, 1)
                em.add("tensor", dn)
                stt.n += 1
                G.set(("wd_b", rot), stt, stt.n)
            dn_done = stt.n

            for tt_i, t in enumerate(tiles):
                def evd(v, tt_i=tt_i, t=t, dn_done=dn_done):
                    v.wait_ge(stt.h, dn_done)
                    for half in range(2):
                        xs = X1[:, E * t + 512 * half:E * t + 512 * (half + 1)]
                        last = v.scalar_tensor_tensor(xs, PS[2 * tt_i + half][:, :],
                                                      0.5, xs, ALU.mult, ALU.add)
                    last.then_inc(sv.h, 1)
                em.add("vector", evd)
                sv.n += 1
                x1_done[t] = sv.n
            G.set("banks_dn", sv, sv.n)
            G.set("banks_all", sv, sv.n)

            if write_y:
                y3d = y_out.rearrange("(a p) e -> a p e", p=128)
                for t in tiles:
                    def sty(gps, t=t, seq=seq, n=x1_done[t]):
                        gps.wait_ge(sv.h, n)
                        gps.dma_start(out=y3d[seq * NT + t, :, :],
                                      in_=X1[:, E * t:E * (t + 1)]).then_inc(sy.h, 16)
                    em.add("gpsimd", sty)
                    sy.n += 16

        if sl == 1 and not write_y:
            def acts(a, x1_done=dict(x1_done)):
                for t in range(NT):
                    a.wait_ge(sv.h, x1_done[t])
                    a.activation(junk[:, :], X1[:, E * t:E * (t + 1)], AF.Identity,
                                 accum_out=stats[:, 32 + t:33 + t])
                    a.activation(junk[:, :], X1[:, E * t:E * (t + 1)], AF.Square,
                                 accum_out=stats[:, 48 + t:49 + t])
                a.drain()
                a.activation(junk[:, 0:1], stats[:, 32:33], AF.Identity).then_inc(sa.h, 1)
            em.add("scalar", acts)
            sa.n += 1
            rstd_ready = emit_stats_finish(1, sa, sa.n)
            G.set("c_ln2t", sa, rstd_ready)

        if flush:
            def wv(v):
                v.wait_ge(sv.h, sv.n)
            em.add("vector", wv)
            if write_y:
                def wgf(g):
                    g.wait_ge(sy.h, sy.n)
                em.add("gpsimd", wgf)
            em.flush()


    def emit_attn(seq):
        C = Carver([(WKA, 12288), (XQF, NT * E // 2)])
        ln2t = C.bf16(E)
        ln2f = C.bf16(EC * 128)
        Qr = [C.bf16(EC * 128), C.bf16(EC * 128)]
        KRS, VRS = 3, 4
        Kr = [C.bf16(EC * 192) for _ in range(KRS)]
        Vr = [C.bf16(E) for _ in range(VRS)]
        smask = C.f32(2 * 192)
        pbuf = C.bf16(8 * 384)
        ptr = C.bf16(2 * 384)
        attb = [C.bf16(EC * 128), C.bf16(EC * 128)]
        lsum = C.f32(16)
        rsum = C.f32(16)
        cjunk = C.bf16(E)
        WSq = WSB[:, 0:4096].bitcast(BF16)
        WSk = WSB[:, 4096:8192].bitcast(BF16)
        WSv = WSB[:, 8192:12288].bitcast(BF16)
        WSo = C.bf16(EC * E)

        nm2 = negmb[:, 16:32]
        rs2 = rstdb[:, 16:32]

        w_base = sw.n
        gwq_prev = G.snap("wsb_rd")

        def wld(s, gwq_prev=gwq_prev):
            G.apply(s, gwq_prev)
            for wsrc, wdst in ((qwT_d, WSq), (kwT_d, WSk), (vwT_d, WSv), (owT_d, WSo)):
                for k in range(EC):
                    s.dma_start(out=wdst[:, E * k:E * (k + 1)],
                                in_=wsrc[128 * k:128 * (k + 1), :]).then_inc(sw.h, 16)
        em.add("sync", wld)
        sw.n += 4 * EC * 16
        wq_done = w_base + EC * 16
        wk_done = w_base + 2 * EC * 16
        wv_done = w_base + 3 * EC * 16
        w_done = sw.n

        def vslot(u):
            if u < 0 or u >= NT:
                return zeroV
            return Vr[u % VRS]

        def pbz(v):
            v.memset(bass.AP(pbuf.tensor, pbuf.offset,
                             [list(pbuf.ap[0]), [384, 8], [1, 96]]), 0.0)
            v.memset(bass.AP(pbuf.tensor, pbuf.offset + 288,
                             [list(pbuf.ap[0]), [384, 8], [1, 96]]), 0.0)
            v.drain().then_inc(sv.h, 1)
        em.add("vector", pbz)
        sv.n += 1

        def emit_qkv(t):
            gws = G.snap("c_ln2t")

            def lnq(v, t=t, gws=gws):
                G.apply(v, gws)
                v.tensor_scalar(ln2t, X1[:, E * t:E * (t + 1)], nm2[:, t:t + 1],
                                rs2[:, t:t + 1], ALU.add, ALU.mult).then_inc(sv.h, 1)
            em.add("vector", lnq)
            sv.n += 1
            ln_done = sv.n

            for cc in range(EC):
                bi = 2 + cc % 3 if cc % 3 < 2 else 7
                bank = PS[bi]
                bslot = bank[:, :].bitcast(BF16)[:, 0:128]
                trslot = ("c_tr", bi)
                gwt = G.snap(trslot)

                def tr(te, bslot=bslot, cc=cc, ln_done=ln_done, gwt=gwt):
                    te.wait_ge(sv.h, ln_done)
                    G.apply(te, gwt)
                    te.transpose(bslot, ln2t[:, 128 * cc:128 * (cc + 1)],
                                 ident[:, :]).then_inc(stt.h, 1)
                em.add("tensor", tr)
                stt.n += 1
                trd = stt.n

                def ev(a, bslot=bslot, dst=ln2f[:, 128 * cc:128 * (cc + 1)], trd=trd):
                    a.wait_ge(stt.h, trd)
                    a.copy(dst, bslot).then_inc(sa.h, 1)
                em.add("scalar", ev)
                sa.n += 1
                G.set(trslot, sa, sa.n)
            lnf_done = (sa, sa.n)
            G.set("c_ln2t", stt, stt.n)

            for proj, WS, bias, scale in (("q", WSq, qbc, 0.125), ("k", WSk, kbc, None)):
                for oc in range(EC):
                    bank = 5 + (oc % 2)
                    gwp = G.snap(("c_qkv", bank))

                    def pmm(te, WS=WS, oc=oc, bank=bank, lnf=lnf_done, gwp=gwp,
                            wdn=(wq_done if proj == "q" else wk_done)):
                        te.wait_ge(sw.h, wdn)
                        te.wait_ge(lnf[0].h, lnf[1])
                        G.apply(te, gwp)
                        for k in range(EC):
                            last = te.matmul(PS[bank][:, 0:128],
                                             WS[:, E * k + 128 * oc:E * k + 128 * (oc + 1)],
                                             ln2f[:, 128 * k:128 * (k + 1)],
                                             start=(k == 0), stop=(k == EC - 1))
                        last.then_inc(stt.h, 1)
                    em.add("tensor", pmm)
                    stt.n += 1
                    pdone = stt.n

                    if proj == "q":
                        dst = Qr[t % 2][:, 128 * oc:128 * (oc + 1)]
                        gwr = G.snap(("c_qr", t % 2))

                        def pev(v, dst=dst, oc=oc, bank=bank, pdone=pdone, gwr=gwr):
                            v.wait_ge(stt.h, pdone)
                            G.apply(v, gwr)
                            v.tensor_scalar(dst, PS[bank][:, 0:128], qbc[:, oc:oc + 1],
                                            0.125, ALU.add, ALU.mult).then_inc(sv.h, 1)
                    else:
                        dst = Kr[t % KRS][:, 192 * oc + 32:192 * oc + 160]
                        gwr = G.snap(("c_kr", t % KRS))

                        def pev(v, dst=dst, oc=oc, bank=bank, pdone=pdone, gwr=gwr):
                            v.wait_ge(stt.h, pdone)
                            G.apply(v, gwr)
                            v.tensor_scalar(dst, PS[bank][:, 0:128], kbc[:, oc:oc + 1],
                                            None, ALU.add).then_inc(sv.h, 1)
                    em.add("vector", pev)
                    sv.n += 1
                    G.set(("c_qkv", bank), sv, sv.n)
            for half in range(2):
                bank = 5 + (half % 2)
                gwp = G.snap(("c_qkv", bank))

                def vmm(te, half=half, bank=bank, lnf=lnf_done, gwp=gwp, wdn=wv_done):
                    te.wait_ge(sw.h, wdn)
                    te.wait_ge(lnf[0].h, lnf[1])
                    G.apply(te, gwp)
                    for k in range(EC):
                        last = te.matmul(PS[bank][:, :],
                                         ln2f[:, 128 * k:128 * (k + 1)],
                                         WSv[:, E * k + 512 * half:E * k + 512 * (half + 1)],
                                         start=(k == 0), stop=(k == EC - 1))
                    last.then_inc(stt.h, 1)
                em.add("tensor", vmm)
                stt.n += 1
                vdone = stt.n
                dst = Vr[t % VRS][:, 512 * half:512 * (half + 1)]
                gwv = G.snap(("c_vr", t % VRS))

                def vev(v, dst=dst, bank=bank, half=half, vdone=vdone, gwv=gwv):
                    v.wait_ge(stt.h, vdone)
                    G.apply(v, gwv)
                    v.tensor_tensor(dst, PS[bank][:, :],
                                    vbrep[:, 512 * half:512 * (half + 1)],
                                    ALU.add).then_inc(sv.h, 1)
                em.add("vector", vev)
                sv.n += 1
                G.set(("c_qkv", bank), sv, sv.n)
            kv_done = sv.n

            def kedges(v, t=t, kv_done=kv_done):
                v.wait_ge(sv.h, kv_done)
                st_ = Kr[t % KRS]
                if t == 0:
                    v.memset(bass.AP(st_.tensor, st_.offset,
                                     [list(st_.ap[0]), [192, EC], [1, 32]]), 0.0)
                else:
                    sp = Kr[(t - 1) % KRS]
                    v.tensor_copy(
                        bass.AP(st_.tensor, st_.offset,
                                [list(st_.ap[0]), [192, EC], [1, 32]]),
                        bass.AP(sp.tensor, sp.offset + 128,
                                [list(sp.ap[0]), [192, EC], [1, 32]]))
                    v.tensor_copy(
                        bass.AP(sp.tensor, sp.offset + 160,
                                [list(sp.ap[0]), [192, EC], [1, 32]]),
                        bass.AP(st_.tensor, st_.offset + 32,
                                [list(st_.ap[0]), [192, EC], [1, 32]]))
                v.drain().then_inc(sv.h, 1)
            em.add("vector", kedges)
            sv.n += 1
            return sv.n

        def emit_kpad_end(t):
            def kz(v, t=t, w=sv.n):
                v.wait_ge(sv.h, w)
                st_ = Kr[t % KRS]
                v.memset(bass.AP(st_.tensor, st_.offset + 160,
                                 [list(st_.ap[0]), [192, EC], [1, 32]]), 0.0)
                v.drain().then_inc(sv.h, 1)
            em.add("vector", kz)
            sv.n += 1
            return sv.n

        def emit_att(t, k_ready):
            at = attb[t % 2]
            for b in range(2):
                for hp in range(4 * b, 4 * b + 4):
                    gws = G.snap(("c_s", 0), ("c_s", 1))

                    def smm(te, t=t, hp=hp, k_ready=k_ready, gws=gws):
                        te.wait_ge(sv.h, k_ready)
                        G.apply(te, gws)
                        for h01 in range(2):
                            p0 = 64 * h01
                            te.matmul(PS[h01][:, 0:192],
                                      Qr[t % 2][p0:p0 + 64, 128 * hp:128 * (hp + 1)],
                                      Kr[t % KRS][p0:p0 + 64, 192 * hp:192 * (hp + 1)],
                                      start=True, stop=True).then_inc(stt.h, 1)
                    em.add("tensor", smm)
                    stt.n += 2
                    sdone = stt.n

                    j = hp % 4
                    gwm = G.snap("c_pb" + str(j), "c_sm")

                    def msk(v, hp=hp, sdone=sdone, gwm=gwm):
                        v.wait_ge(stt.h, sdone)
                        G.apply(v, gwm)
                        v.tensor_mask_reduce(smask[:, 0:192], PS[0][:, 0:192],
                                             mst[:, :], men[:, :], 1.0, -1e30,
                                             ALU.max)
                        v.tensor_mask_reduce(smask[:, 192:384], PS[1][:, 0:192],
                                             mst[:, :], men[:, :], 1.0, -1e30,
                                             ALU.max).then_inc(sv.h, 1)
                    em.add("vector", msk)
                    sv.n += 1
                    mdone = sv.n
                    G.set(("c_s", 0), sv, sv.n)
                    G.set(("c_s", 1), sv, sv.n)

                    def ex(a, hp=hp, j=j, mdone=mdone):
                        a.wait_ge(sv.h, mdone)
                        a.activation(pbuf[:, 768 * j + 96:768 * j + 288],
                                     smask[:, 0:192], AF.Exp,
                                     accum_out=lsum[:, 2 * hp:2 * hp + 1])
                        a.activation(pbuf[:, 768 * j + 384 + 96:768 * j + 384 + 288],
                                     smask[:, 192:384], AF.Exp,
                                     accum_out=lsum[:, 2 * hp + 1:2 * hp + 2]
                                     ).then_inc(sa.h, 1)
                    em.add("scalar", ex)
                    sa.n += 1
                    G.set("c_sm", sa, sa.n)
                exp_done = sa.n

                def rc(v, b=b, exp_done=exp_done):
                    v.wait_ge(sa.h, exp_done)
                    v.drain()
                    v.reciprocal(rsum[:, 8 * b:8 * b + 8], lsum[:, 8 * b:8 * b + 8])
                    v.drain().then_inc(sv.h, 1)
                em.add("vector", rc)
                sv.n += 1
                rc_done = sv.n

                for hp in range(4 * b, 4 * b + 4):
                    j = hp % 4

                    def pm(v, hp=hp, j=j, rc_done=rc_done):
                        v.wait_ge(sv.h, rc_done)
                        v.tensor_scalar(pbuf[:, 768 * j + 96:768 * j + 288],
                                        pbuf[:, 768 * j + 96:768 * j + 288],
                                        rsum[:, 2 * hp:2 * hp + 1], None, ALU.mult)
                        v.tensor_scalar(pbuf[:, 768 * j + 384 + 96:768 * j + 384 + 288],
                                        pbuf[:, 768 * j + 384 + 96:768 * j + 384 + 288],
                                        rsum[:, 2 * hp + 1:2 * hp + 2], None,
                                        ALU.mult).then_inc(sv.h, 1)
                    em.add("vector", pm)
                    sv.n += 1
                    pm_done = sv.n

                    for i in range(6):
                        bi = 2 if i % 3 == 0 else (3 if i % 3 == 1 else 7)
                        bank = PS[bi]
                        bkey = ("c_tr", bi)
                        bslot = bank[:, :].bitcast(BF16)[:, 0:128]
                        gwt = G.snap(bkey)

                        def trp(te, j=j, i=i, bslot=bslot, pm_done=pm_done, gwt=gwt):
                            te.wait_ge(sv.h, pm_done)
                            G.apply(te, gwt)
                            te.transpose(bslot, pbuf[:, 768 * j + 128 * i:768 * j + 128 * (i + 1)],
                                         ident[:, :]).then_inc(stt.h, 1)
                        em.add("tensor", trp)
                        stt.n += 1
                        trd = stt.n

                        def evp(a, i=i, bslot=bslot, trd=trd):
                            a.wait_ge(stt.h, trd)
                            a.copy(ptr[:, 128 * i:128 * (i + 1)], bslot).then_inc(sa.h, 1)
                        em.add("scalar", evp)
                        sa.n += 1
                        G.set(bkey, sa, sa.n)
                    ptr_done = (sa, sa.n)

                    gwa = G.snap("c_av")

                    def avm(te, t=t, hp=hp, ptr_done=ptr_done, gwa=gwa):
                        te.wait_ge(ptr_done[0].h, ptr_done[1])
                        G.apply(te, gwa)
                        for h01 in range(2):
                            hh = 2 * hp + h01
                            for i in range(3):
                                vs_ = vslot(t - 1 + i)
                                last = te.matmul(PS[4][64 * h01:64 * h01 + 64, 0:128],
                                                 vs_[:, 64 * hh:64 * hh + 64],
                                                 ptr[:, 384 * h01 + 128 * i:384 * h01 + 128 * (i + 1)],
                                                 start=(i == 0), stop=(i == 2))
                        last.then_inc(stt.h, 1)
                    em.add("tensor", avm)
                    stt.n += 1
                    av_done = stt.n
                    G.set(("c_pb" + str(j)), stt, stt.n)

                    def ave(a, hp=hp, t=t, av_done=av_done):
                        a.wait_ge(stt.h, av_done)
                        a.copy(attb[t % 2][:, 128 * hp:128 * (hp + 1)],
                               PS[4][:, 0:128]).then_inc(sa.h, 1)
                    em.add("scalar", ave)
                    sa.n += 1
                    G.set("c_av", sa, sa.n)
            att_done = sa.n
            G.set(("c_qr", t % 2), stt, stt.n)
            G.set(("c_kr", t % KRS), stt, stt.n)
            for u in (t - 1, t, t + 1):
                if 0 <= u < NT:
                    G.set(("c_vr", u % VRS), stt, stt.n)

            for half in range(2):
                bank = 5 + half
                gwo = G.snap(("c_qkv", bank))

                def omm(te, t=t, half=half, bank=bank, att_done=att_done, gwo=gwo,
                        wdn=w_done):
                    te.wait_ge(sw.h, wdn)
                    te.wait_ge(sa.h, att_done)
                    G.apply(te, gwo)
                    for k in range(EC):
                        last = te.matmul(PS[bank][:, :],
                                         attb[t % 2][:, 128 * k:128 * (k + 1)],
                                         WSo[:, E * k + 512 * half:E * k + 512 * (half + 1)],
                                         start=(k == 0), stop=(k == EC - 1))
                    last.then_inc(stt.h, 1)
                em.add("tensor", omm)
                stt.n += 1
                odone = stt.n

                def oev(v, t=t, half=half, bank=bank, odone=odone):
                    v.wait_ge(stt.h, odone)
                    xs = X1[:, E * t + 512 * half:E * t + 512 * (half + 1)]
                    v.scalar_tensor_tensor(xs, PS[bank][:, :], 1.0, xs,
                                           ALU.mult, ALU.add)
                    v.scalar_tensor_tensor(xs, obrep[:, 512 * half:512 * (half + 1)],
                                           1.0, xs, ALU.mult, ALU.add).then_inc(sv.h, 1)
                em.add("vector", oev)
                sv.n += 1
                G.set(("c_qkv", bank), sv, sv.n)
            x2_done = sv.n

            def st3(a, t=t, x2_done=x2_done):
                a.wait_ge(sv.h, x2_done)
                a.activation(cjunk, X1[:, E * t:E * (t + 1)], AF.Identity,
                             accum_out=stats[:, 64 + t:65 + t])
                a.activation(cjunk, X1[:, E * t:E * (t + 1)], AF.Square,
                             accum_out=stats[:, 80 + t:81 + t]).then_inc(sa.h, 1)
            em.add("scalar", st3)
            sa.n += 1

        import os as _os3
        _natt = int(_os3.environ.get("DBG_ATT_TILES", str(NT)))
        k_ready = {}
        for t in range(_natt):
            k_ready[t - 1] = emit_qkv(t)
            if t >= 1:
                emit_att(t - 1, k_ready[t - 1])
        if _natt < NT:
            return
        k_ready[NT - 1] = emit_kpad_end(NT - 1)
        emit_att(NT - 1, k_ready[NT - 1])

        def sfin(a):
            a.drain()
            a.activation(cjunk[:, 0:1], stats[:, 64:65], AF.Identity).then_inc(sa.h, 1)
        em.add("scalar", sfin)
        sa.n += 1
        emit_stats_finish(2, sa, sa.n)

        def wv(v):
            v.wait_ge(sv.h, sv.n)
        em.add("vector", wv)
        em.flush()

    if DEBUG_STAGE == -2:
        x3d = x_in.rearrange("(a p) e -> a p e", p=128)
        y3d = y_out.rearrange("(a p) e -> a p e", p=128)

        def mv(s):
            s.dma_start(out=X1[:, 0:E], in_=x3d[0, :, :]).then_inc(sx.h, 16)
        em.add("sync", mv)
        sx.n += 16
        n0 = sx.n

        def cp(v):
            v.wait_ge(sx.h, n0)
            v.tensor_scalar_mul(X1[:, E:2 * E], X1[:, 0:E], 2.0).then_inc(sv.h, 1)
        em.add("vector", cp)
        sv.n += 1
        em.flush()
        n1 = sv.n

        def out(g):
            g.wait_ge(sv.h, n1)
            g.dma_start(out=y3d[0, :, :], in_=X1[:, E:2 * E]).then_inc(sy.h, 16)
            g.wait_ge(sy.h, 16)
        em.add("gpsimd", out)
        sy.n += 16
        em.flush()
        es.close()
        mybir.codegen_inst_isa_subclasses(nc)
        return nc
    import os as _os2
    _tap = _os2.environ.get("DBG_TAP", "")
    emit_preproc()
    if _tap == "pre":
        emit_pre(0)
        y3d = y_out.rearrange("(a p) e -> a p e", p=128)
        n0 = sv.n + sa.n

        def dmp(g, wv=sv.n, wa=sa.n):
            g.wait_ge(sv.h, wv)
            g.wait_ge(sa.h, wa)
            g.dma_start(out=y3d[0, :, 0:96], in_=stats[:, :]).then_inc(sy.h, 16)
            g.dma_start(out=y3d[0, :, 96:144], in_=negmb[:, :]).then_inc(sy.h, 16)
            g.dma_start(out=y3d[0, :, 144:192], in_=rstdb[:, :]).then_inc(sy.h, 16)
            g.dma_start(out=y3d[1, :, :], in_=X1[:, 0:E]).then_inc(sy.h, 16)
            g.wait_ge(sy.h, 64)
        em.add("gpsimd", dmp)
        sy.n += 64
        em.flush()
        es.close()
        mybir.codegen_inst_isa_subclasses(nc)
        return nc
    if DEBUG_STAGE == -1:
        em.flush()
        es.close()
        mybir.codegen_inst_isa_subclasses(nc)
        return nc
    if DEBUG_STAGE == 0:
        dbg = WKA[:, 0:1408].bitcast(BF16)
        dbgf = WSB[:, 0:2816]
        y3d = y_out.rearrange("(a p) e -> a p e", p=128)
        base_sy = sy.n

        def dget(s, w=sg.n):
            s.wait_ge(sg.h, w)
            s.dma_start(out=dbg[:, 0:2816], in_=wgT_d[0:128, :]).then_inc(sx.h, 16)
        em.add("sync", dget)
        sx.n += 16
        n0 = sx.n

        def dcast(v, n0=n0):
            v.wait_ge(sx.h, n0)
            v.tensor_copy(dbgf[:, 0:2816], dbg[:, 0:2816]).then_inc(sv.h, 1)
        em.add("vector", dcast)
        sv.n += 1
        n1 = sv.n

        def dout(g, n1=n1):
            g.wait_ge(sv.h, n1)
            for j in range(2):
                g.dma_start(out=y3d[j, :, :], in_=dbgf[:, 1024 * j:1024 * (j + 1)]).then_inc(sy.h, 16)
        em.add("gpsimd", dout)
        sy.n += 32
        n2 = sy.n

        def dget2(s, n2=n2):
            s.wait_ge(sy.h, n2)
            s.dma_start(out=dbg[:, 0:1024], in_=wdT_d[0:128, :]).then_inc(sx.h, 16)
        em.add("sync", dget2)
        sx.n += 16
        n3 = sx.n

        def dcast2(v, n3=n3):
            v.wait_ge(sx.h, n3)
            v.tensor_copy(dbgf[:, 0:1024], dbg[:, 0:1024]).then_inc(sv.h, 1)
        em.add("vector", dcast2)
        sv.n += 1
        n4 = sv.n
        fin_sy = sy.n + 16

        def dout2(g, n4=n4, fin_sy=fin_sy):
            g.wait_ge(sv.h, n4)
            g.dma_start(out=y3d[4, :, :], in_=dbgf[:, 0:1024]).then_inc(sy.h, 16)
            g.wait_ge(sy.h, fin_sy)
        em.add("gpsimd", dout2)
        sy.n += 16
        em.flush()
        es.close()
        mybir.codegen_inst_isa_subclasses(nc)
        return nc
    for seq in range(NSEQ):
        emit_pre(seq)
        xqf_ready.clear()
        emit_ffn(seq, 0, 0, False, flush=False)
        emit_ffn(seq, 1, 0, DEBUG_STAGE == 1, flush=(DEBUG_STAGE < 2))
        if DEBUG_STAGE >= 2:
            emit_attn(seq)
            if DEBUG_STAGE >= 3:
                emit_ffn(seq, 0, 2, False, flush=False)
                emit_ffn(seq, 1, 2, True)
            if DEBUG_STAGE == 2:
                y3d = y_out.rearrange("(a p) e -> a p e", p=128)
                wsv = sv.n

                def dx2(g, seq=seq, wsv=wsv):
                    g.wait_ge(sv.h, wsv)
                    for t in range(NT):
                        g.dma_start(out=y3d[seq * NT + t, :, :],
                                    in_=X1[:, E * t:E * (t + 1)]).then_inc(sy.h, 16)
                em.add("gpsimd", dx2)
                sy.n += NT * 16

                def wgf(g):
                    g.wait_ge(sy.h, sy.n)
                em.add("gpsimd", wgf)
                em.flush()

    es.close()
    mybir.codegen_inst_isa_subclasses(nc)
    return nc


_PROG = None


def kernel(**inputs):
    global _PROG
    x = np.ascontiguousarray(np.asarray(inputs["L0_attn_hidden_states"], dtype=np.float32))
    ident = np.eye(128, dtype=BF)
    mstart = np.arange(128, dtype=np.float32).reshape(128, 1)
    mend = mstart + 65.0

    if _PROG is None:
        _PROG = build_program()
    nc = _PROG

    base = {
        "wg": np.ascontiguousarray(np.asarray(inputs["L0_ffn_w_gate"], dtype=np.float32)),
        "wu": np.ascontiguousarray(np.asarray(inputs["L0_ffn_w_up"], dtype=np.float32)),
        "wd": np.ascontiguousarray(np.asarray(inputs["L0_ffn_w_down"], dtype=np.float32)),
        "qw": np.ascontiguousarray(np.asarray(inputs["L0_attn_q_w"], dtype=np.float32)),
        "kw": np.ascontiguousarray(np.asarray(inputs["L0_attn_k_w"], dtype=np.float32)),
        "vw": np.ascontiguousarray(np.asarray(inputs["L0_attn_v_w"], dtype=np.float32)),
        "ow": np.ascontiguousarray(np.asarray(inputs["L0_attn_o_w"], dtype=np.float32)),
        "qb": np.ascontiguousarray(np.asarray(inputs["L0_attn_q_b"], dtype=np.float32)),
        "kb": np.ascontiguousarray(np.asarray(inputs["L0_attn_k_b"], dtype=np.float32)),
        "vb": np.ascontiguousarray(np.asarray(inputs["L0_attn_v_b"], dtype=np.float32)),
        "ob": np.ascontiguousarray(np.asarray(inputs["L0_attn_o_b"], dtype=np.float32)),
        "ident": ident, "mstart": mstart, "mend": mend,
    }
    in_maps = []
    for c in range(NCORES):
        m = dict(base)
        m["x"] = np.ascontiguousarray(x[c * NSEQ:(c + 1) * NSEQ].reshape(NSEQ * SEQ, E))
        in_maps.append(m)
    res = run_bass_kernel_spmd(nc, in_maps, core_ids=list(range(NCORES)))
    out = np.concatenate([r["y"].reshape(NSEQ, SEQ, E) for r in res.results], axis=0)
    return out
